# version 10
# baseline (speedup 1.0000x reference)
import numpy as np
from contextlib import ExitStack

import concourse.bass as bass
import concourse.mybir as mybir
from concourse.bass_utils import run_bass_kernel_spmd

F32 = mybir.dt.float32
BF16 = mybir.dt.bfloat16
U8 = mybir.dt.uint8
AF = mybir.ActivationFunctionType
ALU = mybir.AluOpType

B_ALL, L, D = 16, 2048, 1024
NCORES = 8
BPC = B_ALL // NCORES
CT = D // 128
TCH = 512
NCHB = L // TCH
LN_EPS = 1e-5
WKV_EPS = 1e-8
INV_D = 1.0 / D

ENGS = ("tensor", "scalar", "vector", "sync", "gpsimd")


class Prog:

    def __init__(self, nc, es):
        self.nc = nc
        self.ops = {k: [] for k in ENGS}
        self.sems = {}
        self.es = es
        self.last_wait = {}

    def ctr(self, name):
        sem = self.es.enter_context(self.nc.semaphore(name))
        self.sems[name] = [sem, 0]

    def val(self, name):
        return self.sems[name][1]

    def wait(self, eng, name, v=None):
        s, cur = self.sems[name]
        if v is None:
            v = cur
        if v <= 0:
            return
        key = (eng, name)
        if self.last_wait.get(key, -1) >= v:
            return
        self.last_wait[key] = v
        self.ops[eng].append(lambda E, s=s, v=v: E.wait_ge(s, v))

    def op(self, eng, fn):
        self.ops[eng].append(fn)

    def opi(self, eng, name, fn, n=1):
        rec = self.sems[name]
        rec[1] += n
        self.ops[eng].append(lambda E, f=fn, s=rec[0], n=n: f(E).then_inc(s, n))
        return rec[1]


def sb_ap(t, offset, dims):
    return bass.AP(t.ap().tensor, offset, dims)


def dr_ap(t, offset, dims):
    return bass.AP(t.tensor, offset, dims)


DBG_SERIAL = False


def build(debug_upto=None):
    nc = bass.Bass("TRN2", target_bir_lowering=False, debug=False)

    xt = nc.dram_tensor("xt", [BPC, D, L], F32, kind="ExternalInput").ap()
    wk = nc.dram_tensor("wk", [D, D], F32, kind="ExternalInput").ap()
    wv = nc.dram_tensor("wv", [D, D], F32, kind="ExternalInput").ap()
    wr = nc.dram_tensor("wr", [D, D], F32, kind="ExternalInput").ap()
    wo = nc.dram_tensor("wo", [D, D], F32, kind="ExternalInput").ap()
    wfk = nc.dram_tensor("wfk", [D, 4 * D], F32, kind="ExternalInput").ap()
    wfv = nc.dram_tensor("wfv", [4 * D, D], F32, kind="ExternalInput").ap()
    wfr = nc.dram_tensor("wfr", [D, D], F32, kind="ExternalInput").ap()
    k0 = nc.dram_tensor("k0", [2 * D, 4 * D], U8, kind="ExternalInput").ap()
    k1 = nc.dram_tensor("k1", [4 * D, 4 * D], U8, kind="ExternalInput").ap()
    k2 = nc.dram_tensor("k2", [4 * D, D], U8, kind="ExternalInput").ap()
    VEC_NAMES = ["ln1g", "ln1b", "mixk", "mixv", "mixr", "ln2g", "ln2b", "tdr", "tfr"]
    vecs = {n: nc.dram_tensor(n, [D], F32, kind="ExternalInput").ap() for n in VEC_NAMES}
    s_in = {n: nc.dram_tensor(n, [128, 1], F32, kind="ExternalInput").ap()
            for n in ("s0", "s1", "s2")}
    ones_in = nc.dram_tensor("ones", [128, 128], F32, kind="ExternalInput").ap()
    out = nc.dram_tensor("out", [BPC, D, L], F32, kind="ExternalOutput").ap()
    x1sp = nc.dram_tensor("x1sp", [BPC, D, L], F32, kind="Internal").ap()

    def dram_chunk(ap_t, b, c):
        return dr_ap(ap_t, b * D * L + c * TCH,
                     [[L, 128], [128 * L, CT], [1, TCH]])

    SBW = [[CT * TCH, 128], [TCH, CT], [1, TCH]]

    chunks = [(b, c) for b in range(BPC) for c in range(NCHB)]

    with ExitStack() as es:
        e = es.enter_context
        P = Prog(nc, es)

        NPV = 14
        PV = e(nc.sbuf_tensor("PV", [128, NPV * CT], F32))
        ONESB = e(nc.sbuf_tensor("ONESB", [128, 128], BF16))
        SC = e(nc.sbuf_tensor("SC", [128, 4], F32))
        ZCOL = e(nc.sbuf_tensor("ZCOL", [128, CT], F32))
        STATM = [e(nc.sbuf_tensor(f"STATM{i}", [128, TCH], F32)) for i in range(2)]
        STATI = [e(nc.sbuf_tensor(f"STATI{i}", [128, TCH], F32)) for i in range(2)]
        STATT = [e(nc.sbuf_tensor("STATT", [128, TCH], F32))] * 2
        EPST = e(nc.sbuf_tensor("EPST", [128, 2], F32))

        POOL = [e(nc.psum_tensor(f"PP{i}", [128, TCH], F32)) for i in range(8)]

        for name in ("t", "s", "v", "gmi", "gxb0", "gxb1", "ldx", "stx",
                     "ld1", "ldo", "sto", "gw0", "gw1", "gw2"):
            P.ctr(name)

        PIDX = {n: i for i, n in enumerate(
            ["ln1g", "ln1b", "mixk", "mixv", "mixr", "ln2g", "ln2b",
             "tdr", "tfr", "ew", "etf", "omk", "omv", "omr"])}

        def pv_col(pname, ct):
            i = PIDX[pname] * CT + ct
            return PV[:, i:i + 1]

        def pv_blk(pname):
            i = PIDX[pname] * CT
            return PV[:, i:i + CT]

        bank_free = {}
        pool_i = [0]

        def next_bank(eng):
            b = POOL[pool_i[0] % 8]
            pool_i[0] += 1
            rec = bank_free.get(b.name)
            if rec is not None:
                P.wait(eng, rec[0], rec[1])
            return b

        def bank_guard(eng, b):
            rec = bank_free.get(b.name)
            if rec is not None:
                P.wait(eng, rec[0], rec[1])

        WA = {}
        for wname in ("k", "v", "r", "o"):
            WA[wname] = e(nc.sbuf_tensor(f"WA{wname}", [128, CT * D], BF16))
        XC = e(nc.sbuf_tensor("XC", [128, CT * TCH], F32))
        XB = e(nc.sbuf_tensor("XB", [128, CT * TCH], BF16))
        XN = e(nc.sbuf_tensor("XN", [128, CT * (TCH + 1)], F32))
        XBLK = TCH + 8
        XNb = e(nc.sbuf_tensor("XNb", [128, CT * XBLK], BF16))
        KIN = e(nc.sbuf_tensor("KIN", [128, CT * TCH], BF16))
        VIN = e(nc.sbuf_tensor("VIN", [128, CT * TCH], BF16))
        RIN = e(nc.sbuf_tensor("RIN", [128, CT * TCH], BF16))
        EK = e(nc.sbuf_tensor("EK", [128, CT * TCH], F32))
        VV = e(nc.sbuf_tensor("VV", [128, CT * TCH], F32))
        BB = e(nc.sbuf_tensor("BB", [128, CT * TCH], F32))
        RG = e(nc.sbuf_tensor("RG", [128, CT * TCH], BF16))
        CARA = e(nc.sbuf_tensor("CARA", [128, CT], F32))
        CARB = e(nc.sbuf_tensor("CARB", [128, CT], F32))
        XNB = e(nc.sbuf_tensor("XNB", [128, CT], F32))
        TMPB = [e(nc.sbuf_tensor(f"TMPB{i}", [128, TCH], BF16)) for i in range(2)]
        TMPF = [e(nc.sbuf_tensor(f"TMPF{i}", [128, TCH], F32)) for i in range(1)]
        AAv = XN[:, 0:CT * TCH]
        BBrw = BB[:, :].bitcast(BF16)[:, 0:CT * TCH]
        XNBb = XNB[:, :].bitcast(BF16)

        def aa_blk(ct):
            return XN[:, ct * TCH:(ct + 1) * TCH]

        def xnb_data(ct):
            return XNb[:, ct * XBLK + 8: ct * XBLK + 8 + TCH]

        def xnb_yp(ct):
            return XNb[:, ct * XBLK + 7: ct * XBLK + 7 + TCH]

        for n in VEC_NAMES:
            pid = PIDX[n]
            P.opi("sync", "ldx", lambda E, v=vecs[n], pid=pid: E.dma_start(
                out=sb_ap(PV, pid * CT, [[NPV * CT, 128], [1, CT]]),
                in_=dr_ap(v, 0, [[1, 128], [128, CT]]),
                allow_slow_non_contiguous=True), 16)
        for i, n in enumerate(("s0", "s1", "s2")):
            P.opi("sync", "ldx", lambda E, v=s_in[n], i=i: E.dma_start(
                out=SC[:, i:i + 1], in_=v[:, :]), 16)
        v_init_ld = P.val("ldx")
        P.opi("gpsimd", "gmi", lambda E: E.dma_start(out=ONESB[:], in_=ones_in[:]), 16)
        P.opi("gpsimd", "gmi", lambda E: E.memset(ZCOL[:], 0.0), 1)
        P.opi("gpsimd", "gmi", lambda E: E.memset(EPST[:, 0:1], LN_EPS), 1)
        P.opi("gpsimd", "gmi", lambda E: E.memset(EPST[:, 1:2], WKV_EPS), 1)
        for wname, wap in (("k", wk), ("v", wv), ("r", wr), ("o", wo)):
            P.opi("gpsimd", "gmi", lambda E, wap=wap, wt=WA[wname]: E.dma_start(
                out=sb_ap(wt, 0, [[CT * D, 128], [D, CT], [1, D]]),
                in_=dr_ap(wap, 0, [[D, 128], [128 * D, CT], [1, D]])), 16)
        v_init_g = P.val("gmi")

        P.wait("scalar", "ldx", v_init_ld)
        P.wait("scalar", "gmi", v_init_g)
        P.op("scalar", lambda E: E.activation(pv_blk("ew"), pv_blk("tdr"), AF.Exp))
        P.op("scalar", lambda E: E.activation(pv_blk("ew"), pv_blk("ew"), AF.Exp,
                                              scale=-1.0))
        P.op("scalar", lambda E: E.activation(pv_blk("etf"), pv_blk("tfr"), AF.Exp))
        P.op("scalar", lambda E: E.activation(SC[:, 0:1], SC[:, 0:1], AF.Identity,
                                              scale=4.0))
        P.op("scalar", lambda E: E.activation(SC[:, 1:2], SC[:, 1:2], AF.Identity,
                                              scale=4.0))
        P.op("scalar", lambda E: E.activation(SC[:, 2:3], SC[:, 2:3], AF.Identity,
                                              scale=2.0))
        P.op("scalar", lambda E: E.activation(pv_blk("omk"), pv_blk("mixk"),
                                              AF.Identity, scale=-1.0, bias=1.0))
        P.op("scalar", lambda E: E.activation(pv_blk("omv"), pv_blk("mixv"),
                                              AF.Identity, scale=-1.0, bias=1.0))
        s_params = P.opi("scalar", "s", lambda E: E.activation(
            pv_blk("omr"), pv_blk("mixr"), AF.Identity, scale=-1.0, bias=1.0))

        stat_free = [0, 0]
        stat_par = [0]

        def emit_stats(rhs_bf, t_rhs_waits, s_sq_src, s_sq_waits,
                       out_bf=False, xsq=None):
            sp = stat_par[0] % 2
            stat_par[0] += 1
            M, I, T = STATM[sp], STATI[sp], STATT[sp]
            if out_bf:
                mb = STATM[sp][:, :].bitcast(BF16)
                Mout, Iout = mb[:, 0:TCH], mb[:, TCH:2 * TCH]
            else:
                Mout, Iout = M[:], I[:]
            XSQ = xsq if xsq is not None else VIN
            for cn, cv in s_sq_waits:
                P.wait("scalar", cn, cv)
            src = rhs_bf if s_sq_src is None else s_sq_src
            sqv = P.opi("scalar", "s", lambda E, src=src, XSQ=XSQ: E.activation(
                XSQ[:], src[:], AF.Square))
            S1 = next_bank("tensor")
            for cn, cv in t_rhs_waits:
                P.wait("tensor", cn, cv)
            t_s1 = None
            for ct in range(CT):
                last = ct == CT - 1
                fn = lambda E, ct=ct, last=last, S1=S1: E.matmul(
                    S1[:], ONESB[:], rhs_bf[:, ct * TCH:(ct + 1) * TCH],
                    start=(ct == 0), stop=last)
                if last:
                    t_s1 = P.opi("tensor", "t", fn)
                else:
                    P.op("tensor", fn)
            S2 = next_bank("tensor")
            P.wait("tensor", "s", sqv)
            t_s2 = None
            for ct in range(CT):
                last = ct == CT - 1
                fn = lambda E, ct=ct, last=last, S2=S2, XSQ=XSQ: E.matmul(
                    S2[:], ONESB[:], XSQ[:, ct * TCH:(ct + 1) * TCH],
                    start=(ct == 0), stop=last)
                if last:
                    t_s2 = P.opi("tensor", "t", fn)
                else:
                    P.op("tensor", fn)
            P.wait("scalar", "t", t_s1)
            P.wait("scalar", "v", stat_free[sp])
            P.opi("scalar", "s", lambda E, M=Mout, S1=S1: E.activation(
                M, S1[:], AF.Identity, scale=INV_D))
            s_t2 = P.opi("scalar", "s", lambda E, T=T, S1=S1: E.activation(
                T[:], S1[:], AF.Square, scale=INV_D))
            bank_free[S1.name] = ("s", s_t2)
            P.wait("vector", "t", t_s2)
            P.wait("vector", "s", s_t2)
            v_veps = P.opi("vector", "v", lambda E, I=I, S2=S2, T=T:
                           E.scalar_tensor_tensor(
                               I[:], S2[:], INV_D, T[:], ALU.mult, ALU.subtract))
            bank_free[S2.name] = ("v", v_veps)
            P.wait("scalar", "v", v_veps)
            P.opi("scalar", "s", lambda E, I=I: E.activation(
                I[:], I[:], AF.Ln, bias=EPST[:, 0:1]))
            s_is = P.opi("scalar", "s", lambda E, I=I, Io=Iout: E.activation(
                Io, I[:], AF.Exp, scale=-0.5))
            return s_is, sp, Mout, Iout

        def xn_blk(ct, lo, hi):
            base = ct * (TCH + 1)
            return XN[:, base + lo: base + hi]

        def mm_group(rhs_fn, lhsT_fn, nk):
            bank = next_bank("tensor")
            tg = None
            for ki in range(nk):
                last = ki == nk - 1
                fn = lambda E, ki=ki, bank=bank, last=last: E.matmul(
                    bank[:], lhsT_fn(ki), rhs_fn(ki), start=(ki == 0), stop=last)
                if last:
                    tg = P.opi("tensor", "t", fn)
                else:
                    P.op("tensor", fn)
            return bank, tg

        spill_done = {}
        prev_kvr_t = [0]
        bscan_done = {-1: 0}
        v_rw_d = {-1: 0}
        wo_t = {-1: 0, -2: 0}
        s_ekd = {}
        front_mix = {}
        front_free = {}
        vxc = {}
        P.wait("tensor", "gmi", v_init_g)

        front_stats = {}

        def emit_front_stats(i):
            front_stats[i] = emit_stats(
                XB, [("gxb0", xbload[i])], None,
                [("t", prev_kvr_t[0]), ("gxb0", xbload[i])],
                out_bf=True, xsq=VIN)

        def emit_front_rest(i):
            b, c = chunks[i]
            s_is, sp, SMb, SIb = front_stats[i]
            pad_dst = bass.AP(XNb.ap().tensor, 7,
                              [[CT * XBLK, 128], [XBLK, CT], [1, 1]])
            if c == 0:
                P.wait("vector", "gmi", v_init_g)
                P.op("vector", lambda E, d=pad_dst: E.tensor_copy(
                    d, sb_ap(ZCOL, 0, [[CT, 128], [1, CT], [1, 1]])))
            else:
                P.op("vector", lambda E, d=pad_dst: E.tensor_copy(
                    d, XNBb[:, 0:CT]))
            P.wait("vector", "gxb0", xbload[i])
            P.wait("vector", "s", s_is)
            for ct in range(CT):
                P.op("vector", lambda E, ct=ct, SMb=SMb: E.tensor_tensor(
                    xnb_data(ct), XB[:, ct * TCH:(ct + 1) * TCH],
                    SMb, ALU.subtract))
            v_lnmul = None
            for ct in range(CT):
                fn = lambda E, ct=ct, SIb=SIb: E.tensor_tensor(
                    xnb_data(ct), xnb_data(ct), SIb, ALU.mult)
                v_lnmul = P.opi("vector", "v", fn) if ct == CT - 1 else (
                    P.op("vector", fn) or v_lnmul)
            stat_free[sp] = v_lnmul
            P.wait("scalar", "v", v_lnmul)
            P.wait("scalar", "s", s_params)
            s_gb = None
            for ct in range(CT):
                fn = lambda E, ct=ct: E.activation(
                    xnb_data(ct), xnb_data(ct), AF.Identity,
                    scale=pv_col("ln1g", ct), bias=pv_col("ln1b", ct))
                s_gb = P.opi("scalar", "s", fn) if ct == CT - 1 else (
                    P.op("scalar", fn) or s_gb)
            P.wait("vector", "s", s_gb)
            P.wait("vector", "t", prev_kvr_t[0])
            v_mix = None
            for mixn, omn, dst in (("mixk", "omk", KIN), ("mixv", "omv", VIN),
                                   ("mixr", "omr", RIN)):
                for ct in range(CT):
                    P.op("vector", lambda E, ct=ct, m=mixn, d=dst:
                         E.tensor_scalar(d[:, ct * TCH:(ct + 1) * TCH],
                                         xnb_data(ct), pv_col(m, ct), None,
                                         ALU.mult))
                for ct in range(CT):
                    fn = lambda E, ct=ct, o=omn, d=dst: E.scalar_tensor_tensor(
                        d[:, ct * TCH:(ct + 1) * TCH], xnb_yp(ct),
                        pv_col(o, ct), d[:, ct * TCH:(ct + 1) * TCH],
                        ALU.mult, ALU.add)
                    if mixn == "mixr" and ct == CT - 1:
                        v_mix = P.opi("vector", "v", fn)
                    else:
                        P.op("vector", fn)
            P.op("vector", lambda E: E.tensor_copy(
                XNBb[:, 0:CT],
                bass.AP(XNb.ap().tensor, 8 + TCH - 1,
                        [[CT * XBLK, 128], [XBLK, CT], [1, 1]])))
            front_mix[i] = v_mix
            front_free[i] = v_lnmul

        def emit_kvr(i):
            P.wait("tensor", "v", front_mix[i])
            if DBG_SERIAL:
                P.wait("tensor", "v", v_rw_d[i - 1])
                P.wait("tensor", "s", P.val("s"))
            for wname, rhs_t in (("k", KIN), ("v", VIN), ("r", RIN)):
                for mo in range(CT):
                    bank, tg = mm_group(
                        lambda ki, rhs_t=rhs_t: rhs_t[:, ki * TCH:(ki + 1) * TCH],
                        lambda ki, w=WA[wname], mo=mo:
                            w[:, ki * D + mo * 128: ki * D + (mo + 1) * 128],
                        CT)
                    if wname == "k":
                        P.wait("scalar", "t", tg)
                        P.wait("scalar", "v", bscan_done[i - 1])
                        sv = P.opi("scalar", "s", lambda E, mo=mo, bank=bank:
                                   E.activation(EK[:, mo * TCH:(mo + 1) * TCH],
                                                bank[:], AF.Exp))
                        bank_free[bank.name] = ("s", sv)
                    elif wname == "v":
                        P.wait("vector", "t", tg)
                        vv2 = P.opi("vector", "v", lambda E, mo=mo, bank=bank:
                                    E.tensor_copy(VV[:, mo * TCH:(mo + 1) * TCH],
                                                  bank[:]))
                        bank_free[bank.name] = ("v", vv2)
                    else:
                        P.wait("scalar", "t", tg)
                        P.wait("scalar", "v", v_rw_d[i - 1])
                        sv = P.opi("scalar", "s", lambda E, mo=mo, bank=bank:
                                   E.activation(RG[:, mo * TCH:(mo + 1) * TCH],
                                                bank[:], AF.Sigmoid))
                        bank_free[bank.name] = ("s", sv)
            prev_kvr_t[0] = P.val("t")
            s_ekd[i] = P.val("s")

        def emit_back(i):
            b, c = chunks[i]
            P.wait("vector", "s", s_ekd[i])
            if c == 0:
                for ct in range(CT):
                    P.op("vector", lambda E, ct=ct: E.tensor_tensor(
                        EK[:, ct * TCH:ct * TCH + 1],
                        EK[:, ct * TCH:ct * TCH + 1],
                        pv_col("etf", ct), ALU.mult))
                P.op("vector", lambda E: E.tensor_copy(CARA[:], ZCOL[:]))
                P.op("vector", lambda E: E.tensor_copy(CARB[:], ZCOL[:]))
            P.opi("vector", "v", lambda E: E.tensor_tensor(
                VV[:], EK[:], VV[:], ALU.mult))
            P.wait("vector", "t", wo_t[i - 1])
            for ct in range(CT):
                P.op("vector", lambda E, ct=ct: E.tensor_tensor_scan(
                    aa_blk(ct), pv_col("ew", ct).broadcast_to([128, TCH]),
                    VV[:, ct * TCH:(ct + 1) * TCH],
                    CARA[:, ct:ct + 1], ALU.mult, ALU.add))
            for ct in range(CT):
                P.op("vector", lambda E, ct=ct: E.tensor_tensor_scan(
                    BB[:, ct * TCH:(ct + 1) * TCH],
                    pv_col("ew", ct).broadcast_to([128, TCH]),
                    EK[:, ct * TCH:(ct + 1) * TCH],
                    CARB[:, ct:ct + 1], ALU.mult, ALU.add))
            P.op("vector", lambda E: E.tensor_copy(
                CARA[:], sb_ap(XN, TCH - 1,
                               [[CT * (TCH + 1), 128], [TCH, CT], [1, 1]])))
            v_car = P.opi("vector", "v", lambda E: E.tensor_copy(
                CARB[:], sb_ap(BB, TCH - 1, [[CT * TCH, 128], [TCH, CT], [1, 1]])))
            bscan_done[i] = v_car
            P.wait("scalar", "v", v_car)
            P.opi("scalar", "s", lambda E: E.activation(BB[:], BB[:], AF.Ln,
                                                        bias=EPST[:, 1:2]))
            s_rec = P.opi("scalar", "s", lambda E: E.activation(
                BB[:], BB[:], AF.Exp, scale=-1.0))
            P.wait("vector", "s", s_rec)
            P.opi("vector", "v", lambda E: E.tensor_tensor(AAv, AAv, BB[:],
                                                           ALU.mult))
            v_rw_d[i] = P.opi("vector", "v", lambda E: E.tensor_tensor(
                BBrw, AAv, RG[:], ALU.mult))

        def emit_wo(i):
            b, c = chunks[i]
            P.wait("tensor", "v", v_rw_d[i])
            for mo in range(CT):
                bank, tg = mm_group(
                    lambda ki: BBrw[:, ki * TCH:(ki + 1) * TCH],
                    lambda ki, mo=mo:
                        WA["o"][:, ki * D + mo * 128: ki * D + (mo + 1) * 128],
                    CT)
                P.wait("vector", "t", tg)
                P.wait("vector", "ldx", vxc[i])
                vv2 = P.opi("vector", "v", lambda E, mo=mo, bank=bank:
                            E.tensor_tensor(XC[:, mo * TCH:(mo + 1) * TCH],
                                            XC[:, mo * TCH:(mo + 1) * TCH],
                                            bank[:], ALU.add))
                bank_free[bank.name] = ("v", vv2)
            wo_t[i] = P.val("t")
            v_x1 = P.val("v")
            P.wait("sync", "v", v_x1)
            dst = out if debug_upto in ("A", "TAPS2") else x1sp
            spill_done[i] = P.opi("sync", "stx",
                                  lambda E, b=b, c=c, dst=dst: E.dma_start(
                                      out=dram_chunk(dst, b, c),
                                      in_=sb_ap(XC, 0, SBW)), 16)

        NCH = len(chunks)

        def emit_front_taps():
            P.wait("gpsimd", "v", front_mix[0])
            xnd = bass.AP(XNb.ap().tensor, 8,
                          [[CT * XBLK, 128], [XBLK, CT], [1, TCH]])
            P.opi("gpsimd", "gmi", lambda E: E.dma_start(
                out=dram_chunk(out, 0, 0), in_=xnd), 16)
            P.opi("gpsimd", "gmi", lambda E: E.dma_start(
                out=dram_chunk(out, 0, 1), in_=sb_ap(KIN, 0, SBW)), 16)
            mb = STATM[0][:, :].bitcast(BF16)
            P.opi("gpsimd", "gmi", lambda E: E.dma_start(
                out=dr_ap(out, 2 * TCH, [[L, 128], [1, TCH]]),
                in_=mb[:, 0:TCH]), 16)
            P.opi("gpsimd", "gmi", lambda E: E.dma_start(
                out=dr_ap(out, 3 * TCH, [[L, 128], [1, TCH]]),
                in_=mb[:, TCH:2 * TCH]), 16)
            P.wait("gpsimd", "gmi", P.val("gmi"))

        xbload = {}
        xbload[0] = P.opi("gpsimd", "gxb0", lambda E, b=chunks[0][0],
                          c=chunks[0][1]: E.dma_start(
                              out=sb_ap(XB, 0, SBW), in_=dram_chunk(xt, b, c)), 16)
        emit_front_stats(0)
        emit_front_rest(0)
        if debug_upto == "TAPS":
            emit_front_taps()
        nloop = NCH
        if debug_upto == "TAPS":
            nloop = 0
        elif debug_upto == "TAPS1":
            nloop = 1
        elif debug_upto == "TAPS2":
            nloop = 2
        for i in range(nloop):
            emit_kvr(i)
            allow_pre = debug_upto not in ("TAPS1", "TAPS2") or \
                (debug_upto == "TAPS2" and i == 0)
            if i + 1 < NCH and allow_pre:
                P.wait("gpsimd", "v", front_free[i])
                xbload[i + 1] = P.opi("gpsimd", "gxb0",
                                      lambda E, b=chunks[i + 1][0],
                                      c=chunks[i + 1][1]: E.dma_start(
                                          out=sb_ap(XB, 0, SBW),
                                          in_=dram_chunk(xt, b, c)), 16)
                emit_front_stats(i + 1)
            if i >= 1:
                emit_wo(i - 1)
            if i >= 1:
                P.wait("sync", "stx", spill_done[i - 1])
            vxc[i] = P.opi("sync", "ldx", lambda E, b=chunks[i][0],
                           c=chunks[i][1]: E.dma_start(
                               out=sb_ap(XC, 0, SBW),
                               in_=dram_chunk(xt, b, c)), 16)
            if i + 1 < NCH and allow_pre:
                emit_front_rest(i + 1)
            emit_back(i)
        if debug_upto == "TAPS1":
            P.wait("sync", "v", v_rw_d[0])
            P.wait("sync", "s", P.val("s"))
            P.opi("sync", "stx", lambda E: E.dma_start(
                out=dram_chunk(out, 0, 0), in_=sb_ap(EK, 0, SBW)), 16)
            P.opi("sync", "stx", lambda E: E.dma_start(
                out=dram_chunk(out, 0, 1), in_=sb_ap(VV, 0, SBW)), 16)
            P.opi("sync", "stx", lambda E: E.dma_start(
                out=dram_chunk(out, 0, 2),
                in_=bass.AP(XN.ap().tensor, 0,
                            [[CT * (TCH + 1), 128], [TCH, CT], [1, TCH]])), 16)
            P.wait("gpsimd", "v", v_rw_d[0])
            P.opi("gpsimd", "gmi", lambda E: E.dma_start(
                out=dram_chunk(out, 0, 3),
                in_=bass.AP(BB.ap().tensor, 0,
                            [[CT * TCH, 128], [TCH, CT], [1, TCH]])), 16)
            P.wait("sync", "stx", P.val("stx"))
            P.wait("gpsimd", "gmi", P.val("gmi"))
        if debug_upto == "TAPS2":
            emit_wo(1)
        if debug_upto not in ("TAPS", "TAPS1", "TAPS2"):
            emit_wo(NCH - 1)

        if debug_upto in ("A", "TAPS", "TAPS1", "TAPS2"):
            P.wait("sync", "stx", P.val("stx"))
        else:
            finals = [("t", P.val("t")), ("s", P.val("s")), ("v", P.val("v")),
                      ("stx", P.val("stx")), ("ldx", P.val("ldx")),
                      ("gxb0", P.val("gxb0")), ("gxb1", P.val("gxb1")),
                      ("gmi", P.val("gmi"))]
            for eng in ENGS:
                for cn, cv in finals:
                    P.wait(eng, cn, cv)

            X1C = XC
            XOC = BB
            X1B = XB
            ZBB = VIN
            XSQB = VV[:, :].bitcast(BF16)[:, 0:CT * TCH]
            YN = XN
            MIXB = KIN
            YB = VIN
            RRG = RIN
            X3B = RG
            EKb = EK[:, :].bitcast(BF16)
            VVb = VV[:, :].bitcast(BF16)
            YNb = XN[:, 0:CT * TCH].bitcast(BF16)
            WOb = WA["o"][:, :]

            def kk_blk(f):
                base = EKb if f < 16 else VVb
                return base[:, (f % 16) * TCH:((f % 16) + 1) * TCH]

            def g0_blk(f):
                base = YNb if f < 16 else WOb
                return base[:, (f % 16) * TCH:((f % 16) + 1) * TCH]

            g1_blk = kk_blk

            WS = [WA["k"], WA["v"], WA["r"]]

            def yn_blk(ct, lo, hi):
                base = ct * (TCH + 1)
                return YN[:, base + lo: base + hi]

            ws_state = [None, None, None]
            ws_i = [0]
            ws_sems = ["gw0", "gw1", "gw2"]

            def ws_next():
                i = ws_i[0] % 3
                ws_i[0] += 1
                return i

            def ws_load(i, fn):
                if ws_state[i] is not None:
                    P.wait("gpsimd", ws_state[i][0], ws_state[i][1])
                return P.opi("gpsimd", ws_sems[i], fn, 16)

            def ws_lhsT(wi, ki, mo, blk):
                return WS[wi][:, ki * blk + mo * 128: ki * blk + (mo + 1) * 128]

            x1_reuse = {}
            xo_reuse = {}
            out_done = {}
            x1b_reuse = {}
            prev_wfr_t = [0]
            prev_w1_t = [0]
            tmpb_free = [0, 0]
            tmp_i = [0]

            ld1_pre = {}
            gxb_pre = {}
            for k, (b, c) in enumerate(chunks):
                if k == 0:
                    P.wait("sync", "stx", spill_done[k])
                    v_x1c = P.opi("sync", "ld1", lambda E, b=b, c=c: E.dma_start(
                        out=sb_ap(X1C, 0, SBW), in_=dram_chunk(x1sp, b, c)), 16)
                    P.wait("gpsimd", "stx", spill_done[k])
                    v_x1b = P.opi("gpsimd", "gxb0", lambda E, b=b, c=c:
                                  E.dma_start(out=sb_ap(X1B, 0, SBW),
                                              in_=dram_chunk(x1sp, b, c)), 16)
                else:
                    v_x1c = ld1_pre[k]
                    v_x1b = gxb_pre[k]
                if k >= 1:
                    P.wait("sync", "sto", out_done[k - 1])
                    P.wait("sync", "v", xo_reuse[k - 1])
                v_xo = P.opi("sync", "ldo", lambda E, b=b, c=c: E.dma_start(
                    out=sb_ap(XOC, 0, SBW), in_=dram_chunk(xt, b, c)), 16)

                s_is, sp, SM, SI = emit_stats(X1B, [("gxb0", v_x1b)], None,
                                              [("t", prev_wfr_t[0]),
                                               ("gxb0", v_x1b)], xsq=XSQB)
                x1b_reuse[k] = P.val("t")
                if k + 1 < len(chunks):
                    nb, ncc = chunks[k + 1]
                    P.wait("gpsimd", "t", x1b_reuse[k])
                    gxb_pre[k + 1] = P.opi(
                        "gpsimd", "gxb0", lambda E, b=nb, c=ncc: E.dma_start(
                            out=sb_ap(X1B, 0, SBW), in_=dram_chunk(x1sp, b, c)), 16)
                P.wait("vector", "t", prev_w1_t[0])
                if c == 0:
                    P.op("vector", lambda E: E.tensor_copy(
                        sb_ap(YN, 0, [[CT * (TCH + 1), 128], [TCH + 1, CT], [1, 1]]),
                        sb_ap(ZCOL, 0, [[CT, 128], [1, CT], [1, 1]])))
                else:
                    P.op("vector", lambda E: E.tensor_copy(
                        sb_ap(YN, 0, [[CT * (TCH + 1), 128], [TCH + 1, CT], [1, 1]]),
                        sb_ap(XNB, 0, [[CT, 128], [1, CT], [1, 1]])))
                P.wait("vector", "ld1", v_x1c)
                P.wait("vector", "s", s_is)
                for ct in range(CT):
                    P.op("vector", lambda E, ct=ct, SM=SM: E.tensor_tensor(
                        yn_blk(ct, 1, TCH + 1), X1C[:, ct * TCH:(ct + 1) * TCH],
                        SM, ALU.subtract))
                v_ln = None
                for ct in range(CT):
                    fn = lambda E, ct=ct, SI=SI: E.tensor_tensor(
                        yn_blk(ct, 1, TCH + 1), yn_blk(ct, 1, TCH + 1),
                        SI, ALU.mult)
                    v_ln = P.opi("vector", "v", fn) if ct == CT - 1 else (P.op("vector", fn) or v_ln)
                stat_free[sp] = v_ln
                P.wait("scalar", "v", v_ln)
                s_yn = None
                for ct in range(CT):
                    fn = lambda E, ct=ct: E.activation(
                        yn_blk(ct, 1, TCH + 1), yn_blk(ct, 1, TCH + 1), AF.Identity,
                        scale=pv_col("ln2g", ct), bias=pv_col("ln2b", ct))
                    s_yn = P.opi("scalar", "s", fn) if ct == CT - 1 else (P.op("scalar", fn) or s_yn)

                P.wait("vector", "s", s_yn)
                for ct in range(CT):
                    P.op("vector", lambda E, ct=ct: E.tensor_tensor(
                        MIXB[:, ct * TCH:(ct + 1) * TCH], yn_blk(ct, 1, TCH + 1),
                        yn_blk(ct, 0, TCH), ALU.add))
                v_mix = None
                for ct in range(CT):
                    fn = lambda E, ct=ct: E.tensor_copy(
                        YB[:, ct * TCH:(ct + 1) * TCH], yn_blk(ct, 1, TCH + 1))
                    v_mix = P.opi("vector", "v", fn) if ct == CT - 1 else (P.op("vector", fn) or v_mix)

                P.op("vector", lambda E: E.tensor_copy(
                    sb_ap(XNB, 0, [[CT, 128], [1, CT], [1, 1]]),
                    sb_ap(YN, TCH, [[CT * (TCH + 1), 128], [TCH + 1, CT], [1, 1]])))

                P.wait("tensor", "v", v_mix)
                for g in range(4):
                    wi = ws_next()
                    lv = ws_load(wi, lambda E, g=g, wi=wi: E.dma_start(
                        out=sb_ap(WS[wi], 0, [[CT * D, 128], [1024, CT], [1, 1024]]),
                        in_=dr_ap(wfk, g * 1024,
                                  [[4 * D, 128], [128 * 4 * D, CT], [1, 1024]])))
                    P.wait("tensor", ws_sems[wi], lv)
                    for mo in range(8):
                        moa = g * 8 + mo
                        bank, tg = mm_group(
                            lambda ki: MIXB[:, ki * TCH:(ki + 1) * TCH],
                            lambda ki, wi=wi, mo=mo: ws_lhsT(wi, ki, mo, 1024),
                            CT)
                        ti = tmp_i[0] % 2
                        tmp_i[0] += 1
                        P.wait("scalar", "t", tg)
                        P.wait("scalar", "v", tmpb_free[ti])
                        sv = P.opi("scalar", "s", lambda E, bank=bank, ti=ti:
                                   E.activation(TMPB[ti][:], bank[:], AF.Relu,
                                                scale=0.5))
                        bank_free[bank.name] = ("s", sv)
                        P.wait("vector", "s", sv)
                        vv2 = P.opi("vector", "v", lambda E, moa=moa, ti=ti:
                                    E.tensor_tensor(kk_blk(moa), TMPB[ti][:],
                                                    TMPB[ti][:], ALU.mult))
                        tmpb_free[ti] = vv2
                    ws_state[wi] = ("t", P.val("t"))
                v_kk_done = P.val("v")

                wi = ws_next()
                lv = ws_load(wi, lambda E, wi=wi: E.dma_start(
                    out=sb_ap(WS[wi], 0, [[CT * D, 128], [D, CT], [1, D]]),
                    in_=dr_ap(wfr, 0, [[D, 128], [128 * D, CT], [1, D]])))
                P.wait("tensor", ws_sems[wi], lv)
                for mo in range(CT):
                    bank, tg = mm_group(
                        lambda ki: YB[:, ki * TCH:(ki + 1) * TCH],
                        lambda ki, wi=wi, mo=mo: ws_lhsT(wi, ki, mo, 1024),
                        CT)
                    P.wait("scalar", "t", tg)
                    sv = P.opi("scalar", "s", lambda E, mo=mo, bank=bank:
                               E.activation(RRG[:, mo * TCH:(mo + 1) * TCH],
                                            bank[:], AF.Sigmoid))
                    bank_free[bank.name] = ("s", sv)
                ws_state[wi] = ("t", P.val("t"))
                prev_wfr_t[0] = P.val("t")
                s_rr_done = P.val("s")

                P.wait("tensor", "v", v_kk_done)
                for g in range(4):
                    wi = ws_next()
                    lv = ws_load(wi, lambda E, g=g, wi=wi: E.dma_start(
                        out=sb_ap(WS[wi], 0, [[CT * D, 128], [256, 32], [1, 256]]),
                        in_=dr_ap(wfv, g * 256, [[D, 128], [128 * D, 32], [1, 256]])))
                    P.wait("tensor", ws_sems[wi], lv)
                    for mo in range(2):
                        moa = g * 2 + mo
                        bank, tg = mm_group(
                            lambda ki: kk_blk(ki),
                            lambda ki, wi=wi, mo=mo: ws_lhsT(wi, ki, mo, 256),
                            32)
                        P.wait("vector", "t", tg)
                        P.wait("vector", "s", s_rr_done)
                        P.op("vector", lambda E, moa=moa, bank=bank:
                             E.tensor_tensor(TMPF[0][:],
                                             RRG[:, moa * TCH:(moa + 1) * TCH],
                                             bank[:], ALU.mult))
                        vv2 = P.opi("vector", "v", lambda E, moa=moa:
                                    E.tensor_tensor(X1C[:, moa * TCH:(moa + 1) * TCH],
                                                    X1C[:, moa * TCH:(moa + 1) * TCH],
                                                    TMPF[0][:], ALU.add))
                        bank_free[bank.name] = ("v", vv2)
                    ws_state[wi] = ("t", P.val("t"))

                P.wait("vector", "ldo", v_xo)
                v_z = P.opi("vector", "v", lambda E: E.tensor_tensor(
                    X1C[:], X1C[:], XOC[:], ALU.add))
                P.wait("vector", "t", x1b_reuse[k])
                v_zb = P.opi("vector", "v", lambda E: E.tensor_copy(ZBB[:], X1C[:]))
                s_is, sp, SM, SI = emit_stats(ZBB, [("v", v_zb)], X1C,
                                              [("v", v_z)], xsq=XSQB)
                P.wait("vector", "s", s_is)
                for ct in range(CT):
                    P.op("vector", lambda E, ct=ct, SM=SM: E.tensor_tensor(
                        XOC[:, ct * TCH:(ct + 1) * TCH],
                        X1C[:, ct * TCH:(ct + 1) * TCH], SM, ALU.subtract))
                for ct in range(CT):
                    P.op("vector", lambda E, ct=ct, SI=SI: E.tensor_tensor(
                        XOC[:, ct * TCH:(ct + 1) * TCH],
                        XOC[:, ct * TCH:(ct + 1) * TCH], SI, ALU.mult))
                v_x3 = P.opi("vector", "v", lambda E: E.tensor_copy(X3B[:], XOC[:]))
                stat_free[sp] = v_x3
                x1_reuse[k] = v_x3
                if k + 1 < len(chunks):
                    nb, ncc = chunks[k + 1]
                    P.wait("sync", "v", v_x3)
                    ld1_pre[k + 1] = P.opi(
                        "sync", "ld1", lambda E, b=nb, c=ncc: E.dma_start(
                            out=sb_ap(X1C, 0, SBW), in_=dram_chunk(x1sp, b, c)), 16)

                for g in range(4):
                    wi_a = ws_next()
                    lva = ws_load(wi_a, lambda E, g=g, wi=wi_a: E.dma_start(
                        out=sb_ap(WS[wi], 0, [[CT * D, 128], [1024, CT], [1, 1024]]),
                        in_=dr_ap(k0, g * 1024,
                                  [[4 * D, 128], [128 * 4 * D, CT], [1, 1024]])))
                    wi_b = ws_next()
                    lvb = ws_load(wi_b, lambda E, g=g, wi=wi_b: E.dma_start(
                        out=sb_ap(WS[wi], 0, [[CT * D, 128], [1024, CT], [1, 1024]]),
                        in_=dr_ap(k0, D * 4 * D + g * 1024,
                                  [[4 * D, 128], [128 * 4 * D, CT], [1, 1024]])))
                    P.wait("vector", ws_sems[wi_a], lva)
                    P.wait("vector", ws_sems[wi_b], lvb)
                    v_sub = P.opi("vector", "v", lambda E, a=wi_a, bb2=wi_b:
                                  E.tensor_tensor(WS[a][:], WS[a][:],
                                                  WS[bb2][:], ALU.subtract))
                    ws_state[wi_b] = ("v", v_sub)
                    P.wait("tensor", "v", v_sub)
                    P.wait("tensor", "v", v_x3)
                    for mo in range(8):
                        moa = g * 8 + mo
                        bank, tg = mm_group(
                            lambda ki: X3B[:, ki * TCH:(ki + 1) * TCH],
                            lambda ki, wi=wi_a, mo=mo: ws_lhsT(wi, ki, mo, 1024),
                            CT)
                        P.wait("scalar", "t", tg)
                        sv = P.opi("scalar", "s", lambda E, moa=moa, bank=bank:
                                   E.activation(g0_blk(moa), bank[:], AF.Relu,
                                                scale=SC[:, 0:1]))
                        bank_free[bank.name] = ("s", sv)
                    ws_state[wi_a] = ("t", P.val("t"))
                s_g0_done = P.val("s")

                for g in range(16):
                    wi = ws_next()
                    lv = ws_load(wi, lambda E, g=g, wi=wi: E.dma_start(
                        out=sb_ap(WS[wi], 0, [[CT * D, 128], [256, 32], [1, 256]]),
                        in_=dr_ap(k1, g * 256,
                                  [[4 * D, 128], [128 * 4 * D, 32], [1, 256]])))
                    P.wait("vector", ws_sems[wi], lv)
                    v_sub = P.opi("vector", "v", lambda E, wi=wi: E.tensor_scalar(
                        WS[wi][:], WS[wi][:], -0.5, None, ALU.add))
                    P.wait("tensor", "v", v_sub)
                    P.wait("tensor", "s", s_g0_done)
                    for mo in range(2):
                        moa = g * 2 + mo
                        bank, tg = mm_group(
                            lambda ki: g0_blk(ki),
                            lambda ki, wi=wi, mo=mo: ws_lhsT(wi, ki, mo, 256),
                            32)
                        P.wait("scalar", "t", tg)
                        sv = P.opi("scalar", "s", lambda E, moa=moa, bank=bank:
                                   E.activation(g1_blk(moa), bank[:], AF.Relu,
                                                scale=SC[:, 1:2]))
                        bank_free[bank.name] = ("s", sv)
                    ws_state[wi] = ("t", P.val("t"))
                prev_w1_t[0] = P.val("t")

                for g in range(4):
                    wi = ws_next()
                    lv = ws_load(wi, lambda E, g=g, wi=wi: E.dma_start(
                        out=sb_ap(WS[wi], 0, [[CT * D, 128], [256, 32], [1, 256]]),
                        in_=dr_ap(k2, g * 256,
                                  [[D, 128], [128 * D, 32], [1, 256]])))
                    P.wait("vector", ws_sems[wi], lv)
                    v_sub = P.opi("vector", "v", lambda E, wi=wi: E.tensor_scalar(
                        WS[wi][:], WS[wi][:], -0.5, None, ALU.add))
                    P.wait("tensor", "v", v_sub)
                    for mo in range(2):
                        moa = g * 2 + mo
                        bank, tg = mm_group(
                            lambda ki: g1_blk(ki),
                            lambda ki, wi=wi, mo=mo: ws_lhsT(wi, ki, mo, 256),
                            32)
                        P.wait("vector", "t", tg)
                        vv2 = P.opi("vector", "v", lambda E, moa=moa, bank=bank:
                                    E.scalar_tensor_tensor(
                                        XOC[:, moa * TCH:(moa + 1) * TCH],
                                        bank[:], SC[:, 2:3],
                                        XOC[:, moa * TCH:(moa + 1) * TCH],
                                        ALU.mult, ALU.add))
                        bank_free[bank.name] = ("v", vv2)
                    ws_state[wi] = ("t", P.val("t"))
                v_z2 = P.val("v")

                v_zb2 = P.opi("vector", "v", lambda E: E.tensor_copy(ZBB[:], XOC[:]))
                s_is, sp, SM, SI = emit_stats(ZBB, [("v", v_zb2)], XOC,
                                              [("v", v_z2)], xsq=XSQB)
                P.wait("vector", "s", s_is)
                for ct in range(CT):
                    P.op("vector", lambda E, ct=ct, SM=SM: E.tensor_tensor(
                        XOC[:, ct * TCH:(ct + 1) * TCH],
                        XOC[:, ct * TCH:(ct + 1) * TCH], SM, ALU.subtract))
                v_out = None
                for ct in range(CT):
                    fn = lambda E, ct=ct, SI=SI: E.tensor_tensor(
                        XOC[:, ct * TCH:(ct + 1) * TCH],
                        XOC[:, ct * TCH:(ct + 1) * TCH], SI, ALU.mult)
                    v_out = P.opi("vector", "v", fn) if ct == CT - 1 else (P.op("vector", fn) or v_out)
                stat_free[sp] = v_out
                xo_reuse[k] = v_out
                P.wait("sync", "v", v_out)
                out_done[k] = P.opi("sync", "sto", lambda E, b=b, c=c: E.dma_start(
                    out=dram_chunk(out, b, c), in_=sb_ap(XOC, 0, SBW)), 16)

            P.wait("sync", "sto", P.val("sto"))

        with nc.Block() as blk:
            @blk.tensor
            def _(E):
                for f in P.ops["tensor"]:
                    f(E)

            @blk.scalar
            def _(E):
                for f in P.ops["scalar"]:
                    f(E)

            @blk.vector
            def _(E):
                for f in P.ops["vector"]:
                    f(E)

            @blk.sync
            def _(E):
                for f in P.ops["sync"]:
                    f(E)

            @blk.gpsimd
            def _(E):
                for f in P.ops["gpsimd"]:
                    f(E)

    return nc


def kernel(L0_attn_x, L0_attn_ln1_g, L0_attn_ln1_b, L0_attn_Wk, L0_attn_Wv,
           L0_attn_Wr, L0_attn_Wo, L0_attn_time_decay, L0_attn_time_first,
           L0_attn_mix_k, L0_attn_mix_v, L0_attn_mix_r, L0_attn_ln2_g,
           L0_attn_ln2_b, L0_attn_Wfk, L0_attn_Wfv, L0_attn_Wfr,
           L0_ffn_k0, L0_ffn_k1, L0_ffn_k2, L0_ffn_s0, L0_ffn_s1, L0_ffn_s2,
           _debug_upto=None):
    x = np.asarray(L0_attn_x, dtype=np.float32)
    xT = np.ascontiguousarray(np.transpose(x, (0, 2, 1)))

    def c32(a):
        return np.ascontiguousarray(np.asarray(a, dtype=np.float32))

    def cu8(a):
        a = np.asarray(a)
        return np.ascontiguousarray(a.view(np.uint8).reshape(a.shape))

    def srep(a):
        return np.ascontiguousarray(
            np.broadcast_to(np.asarray(a, dtype=np.float32).reshape(1, 1), (128, 1)))

    shared = {
        "wk": c32(L0_attn_Wk), "wv": c32(L0_attn_Wv), "wr": c32(L0_attn_Wr),
        "wo": c32(L0_attn_Wo), "wfk": c32(L0_attn_Wfk), "wfv": c32(L0_attn_Wfv),
        "wfr": c32(L0_attn_Wfr),
        "k0": cu8(L0_ffn_k0), "k1": cu8(L0_ffn_k1), "k2": cu8(L0_ffn_k2),
        "ln1g": c32(L0_attn_ln1_g), "ln1b": c32(L0_attn_ln1_b),
        "mixk": c32(L0_attn_mix_k), "mixv": c32(L0_attn_mix_v),
        "mixr": c32(L0_attn_mix_r), "ln2g": c32(L0_attn_ln2_g),
        "ln2b": c32(L0_attn_ln2_b), "tdr": c32(L0_attn_time_decay),
        "tfr": c32(L0_attn_time_first),
        "s0": srep(L0_ffn_s0), "s1": srep(L0_ffn_s1), "s2": srep(L0_ffn_s2),
        "ones": np.ones((128, 128), dtype=np.float32),
    }
    in_maps = []
    for core in range(NCORES):
        m = dict(shared)
        m["xt"] = np.ascontiguousarray(xT[core * BPC:(core + 1) * BPC])
        in_maps.append(m)

    nc = build(debug_upto=_debug_upto)
    res = run_bass_kernel_spmd(nc, in_maps, core_ids=list(range(NCORES)))
    parts = [np.transpose(r["out"], (0, 2, 1)) for r in res.results]
    return np.ascontiguousarray(np.concatenate(parts, axis=0)).astype(np.float32)


# version 11
# speedup vs baseline: 1.0385x; 1.0385x over previous
import numpy as np
from contextlib import ExitStack

import concourse.bass as bass
import concourse.mybir as mybir
from concourse.bass_utils import run_bass_kernel_spmd

F32 = mybir.dt.float32
BF16 = mybir.dt.bfloat16
U8 = mybir.dt.uint8
AF = mybir.ActivationFunctionType
ALU = mybir.AluOpType

B_ALL, L, D = 16, 2048, 1024
NCORES = 8
BPC = B_ALL // NCORES
CT = D // 128
TCH = 512
NCHB = L // TCH
LN_EPS = 1e-5
WKV_EPS = 1e-8
INV_D = 1.0 / D

ENGS = ("tensor", "scalar", "vector", "sync", "gpsimd")


class Prog:

    def __init__(self, nc, es):
        self.nc = nc
        self.ops = {k: [] for k in ENGS}
        self.sems = {}
        self.es = es
        self.last_wait = {}

    def ctr(self, name):
        sem = self.es.enter_context(self.nc.semaphore(name))
        self.sems[name] = [sem, 0]

    def val(self, name):
        return self.sems[name][1]

    def wait(self, eng, name, v=None):
        s, cur = self.sems[name]
        if v is None:
            v = cur
        if v <= 0:
            return
        key = (eng, name)
        if self.last_wait.get(key, -1) >= v:
            return
        self.last_wait[key] = v
        self.ops[eng].append(lambda E, s=s, v=v: E.wait_ge(s, v))

    def op(self, eng, fn):
        self.ops[eng].append(fn)

    def opi(self, eng, name, fn, n=1):
        rec = self.sems[name]
        rec[1] += n
        self.ops[eng].append(lambda E, f=fn, s=rec[0], n=n: f(E).then_inc(s, n))
        return rec[1]


def sb_ap(t, offset, dims):
    return bass.AP(t.ap().tensor, offset, dims)


def dr_ap(t, offset, dims):
    return bass.AP(t.tensor, offset, dims)


DBG_SERIAL = False


def build(debug_upto=None):
    nc = bass.Bass("TRN2", target_bir_lowering=False, debug=False)

    xt = nc.dram_tensor("xt", [BPC, D, L], F32, kind="ExternalInput").ap()
    wk = nc.dram_tensor("wk", [D, D], F32, kind="ExternalInput").ap()
    wv = nc.dram_tensor("wv", [D, D], F32, kind="ExternalInput").ap()
    wr = nc.dram_tensor("wr", [D, D], F32, kind="ExternalInput").ap()
    wo = nc.dram_tensor("wo", [D, D], F32, kind="ExternalInput").ap()
    wfk = nc.dram_tensor("wfk", [D, 4 * D], F32, kind="ExternalInput").ap()
    wfv = nc.dram_tensor("wfv", [4 * D, D], F32, kind="ExternalInput").ap()
    wfr = nc.dram_tensor("wfr", [D, D], F32, kind="ExternalInput").ap()
    k0 = nc.dram_tensor("k0", [2 * D, 4 * D], U8, kind="ExternalInput").ap()
    k1 = nc.dram_tensor("k1", [4 * D, 4 * D], U8, kind="ExternalInput").ap()
    k2 = nc.dram_tensor("k2", [4 * D, D], U8, kind="ExternalInput").ap()
    VEC_NAMES = ["ln1g", "ln1b", "mixk", "mixv", "mixr", "ln2g", "ln2b", "tdr", "tfr"]
    vecs = {n: nc.dram_tensor(n, [D], F32, kind="ExternalInput").ap() for n in VEC_NAMES}
    s_in = {n: nc.dram_tensor(n, [128, 1], F32, kind="ExternalInput").ap()
            for n in ("s0", "s1", "s2")}
    ones_in = nc.dram_tensor("ones", [128, 128], F32, kind="ExternalInput").ap()
    out = nc.dram_tensor("out", [BPC, D, L], F32, kind="ExternalOutput").ap()
    x1sp = nc.dram_tensor("x1sp", [BPC, D, L], F32, kind="Internal").ap()

    def dram_chunk(ap_t, b, c):
        return dr_ap(ap_t, b * D * L + c * TCH,
                     [[L, 128], [128 * L, CT], [1, TCH]])

    SBW = [[CT * TCH, 128], [TCH, CT], [1, TCH]]

    chunks = [(b, c) for b in range(BPC) for c in range(NCHB)]

    with ExitStack() as es:
        e = es.enter_context
        P = Prog(nc, es)

        NPV = 14
        PV = e(nc.sbuf_tensor("PV", [128, NPV * CT], F32))
        ONESB = e(nc.sbuf_tensor("ONESB", [128, 128], BF16))
        SC = e(nc.sbuf_tensor("SC", [128, 4], F32))
        ZCOL = e(nc.sbuf_tensor("ZCOL", [128, CT], F32))
        STATM = [e(nc.sbuf_tensor(f"STATM{i}", [128, TCH], F32)) for i in range(2)]
        STATI = [e(nc.sbuf_tensor(f"STATI{i}", [128, TCH], F32)) for i in range(2)]
        STATT = [e(nc.sbuf_tensor("STATT", [128, TCH], F32))] * 2
        EPST = e(nc.sbuf_tensor("EPST", [128, 2], F32))

        POOL = [e(nc.psum_tensor(f"PP{i}", [128, TCH], F32)) for i in range(8)]

        for name in ("t", "s", "v", "gmi", "gxb0", "gxb1", "ldx", "stx",
                     "ld1", "ldo", "sto", "gw0", "gw1", "gw2"):
            P.ctr(name)

        PIDX = {n: i for i, n in enumerate(
            ["ln1g", "ln1b", "mixk", "mixv", "mixr", "ln2g", "ln2b",
             "tdr", "tfr", "ew", "etf", "omk", "omv", "omr"])}

        def pv_col(pname, ct):
            i = PIDX[pname] * CT + ct
            return PV[:, i:i + 1]

        def pv_blk(pname):
            i = PIDX[pname] * CT
            return PV[:, i:i + CT]

        bank_free = {}
        pool_i = [0]

        def next_bank(eng):
            b = POOL[pool_i[0] % 8]
            pool_i[0] += 1
            rec = bank_free.get(b.name)
            if rec is not None:
                P.wait(eng, rec[0], rec[1])
            return b

        def bank_guard(eng, b):
            rec = bank_free.get(b.name)
            if rec is not None:
                P.wait(eng, rec[0], rec[1])

        WA = {}
        for wname in ("k", "v", "r", "o"):
            WA[wname] = e(nc.sbuf_tensor(f"WA{wname}", [128, CT * D], BF16))
        XC = e(nc.sbuf_tensor("XC", [128, CT * TCH], F32))
        XB = e(nc.sbuf_tensor("XB", [128, CT * TCH], BF16))
        XN = e(nc.sbuf_tensor("XN", [128, CT * (TCH + 1)], F32))
        XBLK = TCH + 8
        XNb = e(nc.sbuf_tensor("XNb", [128, CT * XBLK], BF16))
        KIN = e(nc.sbuf_tensor("KIN", [128, CT * TCH], BF16))
        VIN = e(nc.sbuf_tensor("VIN", [128, CT * TCH], BF16))
        RIN = e(nc.sbuf_tensor("RIN", [128, CT * TCH], BF16))
        EK = e(nc.sbuf_tensor("EK", [128, CT * TCH], F32))
        VV = e(nc.sbuf_tensor("VV", [128, CT * TCH], F32))
        BB = e(nc.sbuf_tensor("BB", [128, CT * TCH], F32))
        RG = e(nc.sbuf_tensor("RG", [128, CT * TCH], BF16))
        CARA = e(nc.sbuf_tensor("CARA", [128, CT], F32))
        CARB = e(nc.sbuf_tensor("CARB", [128, CT], F32))
        XNB = e(nc.sbuf_tensor("XNB", [128, CT], F32))
        TMPB = [e(nc.sbuf_tensor(f"TMPB{i}", [128, TCH], BF16)) for i in range(2)]
        TMPF = [e(nc.sbuf_tensor(f"TMPF{i}", [128, TCH], F32)) for i in range(1)]
        AAv = XN[:, 0:CT * TCH]
        BBrw = BB[:, :].bitcast(BF16)[:, 0:CT * TCH]
        XNBb = XNB[:, :].bitcast(BF16)

        def aa_blk(ct):
            return XN[:, ct * TCH:(ct + 1) * TCH]

        def xnb_data(ct):
            return XNb[:, ct * XBLK + 8: ct * XBLK + 8 + TCH]

        def xnb_yp(ct):
            return XNb[:, ct * XBLK + 7: ct * XBLK + 7 + TCH]

        for n in VEC_NAMES:
            pid = PIDX[n]
            P.opi("sync", "ldx", lambda E, v=vecs[n], pid=pid: E.dma_start(
                out=sb_ap(PV, pid * CT, [[NPV * CT, 128], [1, CT]]),
                in_=dr_ap(v, 0, [[1, 128], [128, CT]]),
                allow_slow_non_contiguous=True), 16)
        for i, n in enumerate(("s0", "s1", "s2")):
            P.opi("sync", "ldx", lambda E, v=s_in[n], i=i: E.dma_start(
                out=SC[:, i:i + 1], in_=v[:, :]), 16)
        v_init_ld = P.val("ldx")
        P.opi("gpsimd", "gmi", lambda E: E.dma_start(out=ONESB[:], in_=ones_in[:]), 16)
        P.opi("gpsimd", "gmi", lambda E: E.memset(ZCOL[:], 0.0), 1)
        P.opi("gpsimd", "gmi", lambda E: E.memset(EPST[:, 0:1], LN_EPS), 1)
        P.opi("gpsimd", "gmi", lambda E: E.memset(EPST[:, 1:2], WKV_EPS), 1)
        for wname, wap in (("k", wk), ("v", wv), ("r", wr), ("o", wo)):
            P.opi("gpsimd", "gmi", lambda E, wap=wap, wt=WA[wname]: E.dma_start(
                out=sb_ap(wt, 0, [[CT * D, 128], [D, CT], [1, D]]),
                in_=dr_ap(wap, 0, [[D, 128], [128 * D, CT], [1, D]])), 16)
        v_init_g = P.val("gmi")

        P.wait("scalar", "ldx", v_init_ld)
        P.wait("scalar", "gmi", v_init_g)
        P.op("scalar", lambda E: E.activation(pv_blk("ew"), pv_blk("tdr"), AF.Exp))
        P.op("scalar", lambda E: E.activation(pv_blk("ew"), pv_blk("ew"), AF.Exp,
                                              scale=-1.0))
        P.op("scalar", lambda E: E.activation(pv_blk("etf"), pv_blk("tfr"), AF.Exp))
        P.op("scalar", lambda E: E.activation(SC[:, 0:1], SC[:, 0:1], AF.Identity,
                                              scale=4.0))
        P.op("scalar", lambda E: E.activation(SC[:, 1:2], SC[:, 1:2], AF.Identity,
                                              scale=4.0))
        P.op("scalar", lambda E: E.activation(SC[:, 2:3], SC[:, 2:3], AF.Identity,
                                              scale=2.0))
        P.op("scalar", lambda E: E.activation(pv_blk("omk"), pv_blk("mixk"),
                                              AF.Identity, scale=-1.0, bias=1.0))
        P.op("scalar", lambda E: E.activation(pv_blk("omv"), pv_blk("mixv"),
                                              AF.Identity, scale=-1.0, bias=1.0))
        s_params = P.opi("scalar", "s", lambda E: E.activation(
            pv_blk("omr"), pv_blk("mixr"), AF.Identity, scale=-1.0, bias=1.0))

        stat_free = [0, 0]
        stat_par = [0]

        def emit_stats(rhs_bf, t_rhs_waits, s_sq_src, s_sq_waits,
                       out_bf=False, xsq=None):
            sp = stat_par[0] % 2
            stat_par[0] += 1
            M, I, T = STATM[sp], STATI[sp], STATT[sp]
            if out_bf:
                mb = STATM[sp][:, :].bitcast(BF16)
                Mout, Iout = mb[:, 0:TCH], mb[:, TCH:2 * TCH]
            else:
                Mout, Iout = M[:], I[:]
            XSQ = xsq if xsq is not None else VIN
            for cn, cv in s_sq_waits:
                P.wait("scalar", cn, cv)
            src = rhs_bf if s_sq_src is None else s_sq_src
            sqv = P.opi("scalar", "s", lambda E, src=src, XSQ=XSQ: E.activation(
                XSQ[:], src[:], AF.Square))
            S1 = next_bank("tensor")
            for cn, cv in t_rhs_waits:
                P.wait("tensor", cn, cv)
            t_s1 = None
            for ct in range(CT):
                last = ct == CT - 1
                fn = lambda E, ct=ct, last=last, S1=S1: E.matmul(
                    S1[:], ONESB[:], rhs_bf[:, ct * TCH:(ct + 1) * TCH],
                    start=(ct == 0), stop=last)
                if last:
                    t_s1 = P.opi("tensor", "t", fn)
                else:
                    P.op("tensor", fn)
            S2 = next_bank("tensor")
            P.wait("tensor", "s", sqv)
            t_s2 = None
            for ct in range(CT):
                last = ct == CT - 1
                fn = lambda E, ct=ct, last=last, S2=S2, XSQ=XSQ: E.matmul(
                    S2[:], ONESB[:], XSQ[:, ct * TCH:(ct + 1) * TCH],
                    start=(ct == 0), stop=last)
                if last:
                    t_s2 = P.opi("tensor", "t", fn)
                else:
                    P.op("tensor", fn)
            P.wait("scalar", "t", t_s1)
            P.wait("scalar", "v", stat_free[sp])
            P.opi("scalar", "s", lambda E, M=Mout, S1=S1: E.activation(
                M, S1[:], AF.Identity, scale=INV_D))
            s_t2 = P.opi("scalar", "s", lambda E, T=T, S1=S1: E.activation(
                T[:], S1[:], AF.Square, scale=INV_D))
            bank_free[S1.name] = ("s", s_t2)
            P.wait("vector", "t", t_s2)
            P.wait("vector", "s", s_t2)
            v_veps = P.opi("vector", "v", lambda E, I=I, S2=S2, T=T:
                           E.scalar_tensor_tensor(
                               I[:], S2[:], INV_D, T[:], ALU.mult, ALU.subtract))
            bank_free[S2.name] = ("v", v_veps)
            P.wait("scalar", "v", v_veps)
            P.opi("scalar", "s", lambda E, I=I: E.activation(
                I[:], I[:], AF.Ln, bias=EPST[:, 0:1]))
            s_is = P.opi("scalar", "s", lambda E, I=I, Io=Iout: E.activation(
                Io, I[:], AF.Exp, scale=-0.5))
            return s_is, sp, Mout, Iout

        def xn_blk(ct, lo, hi):
            base = ct * (TCH + 1)
            return XN[:, base + lo: base + hi]

        def mm_group(rhs_fn, lhsT_fn, nk):
            bank = next_bank("tensor")
            tg = None
            for ki in range(nk):
                last = ki == nk - 1
                fn = lambda E, ki=ki, bank=bank, last=last: E.matmul(
                    bank[:], lhsT_fn(ki), rhs_fn(ki), start=(ki == 0), stop=last)
                if last:
                    tg = P.opi("tensor", "t", fn)
                else:
                    P.op("tensor", fn)
            return bank, tg

        spill_done = {}
        prev_kvr_t = [0]
        vgrp_t = [0]
        bscan_done = {-1: 0}
        v_rw_d = {-1: 0}
        wo_t = {-1: 0, -2: 0}
        s_ekd = {}
        front_mix = {}
        front_free = {}
        vxc = {}
        P.wait("tensor", "gmi", v_init_g)

        front_stats = {}

        def emit_front_stats(i):
            front_stats[i] = emit_stats(
                XB, [("gxb0", xbload[i])], None,
                [("t", vgrp_t[0]), ("gxb0", xbload[i])],
                out_bf=True, xsq=VIN)

        def emit_front_rest(i):
            b, c = chunks[i]
            s_is, sp, SMb, SIb = front_stats[i]
            pad_dst = bass.AP(XNb.ap().tensor, 7,
                              [[CT * XBLK, 128], [XBLK, CT], [1, 1]])
            if c == 0:
                P.wait("vector", "gmi", v_init_g)
                P.op("vector", lambda E, d=pad_dst: E.tensor_copy(
                    d, sb_ap(ZCOL, 0, [[CT, 128], [1, CT], [1, 1]])))
            else:
                P.op("vector", lambda E, d=pad_dst: E.tensor_copy(
                    d, XNBb[:, 0:CT]))
            P.wait("vector", "gxb0", xbload[i])
            P.wait("vector", "s", s_is)
            for ct in range(CT):
                P.op("vector", lambda E, ct=ct, SMb=SMb: E.tensor_tensor(
                    xnb_data(ct), XB[:, ct * TCH:(ct + 1) * TCH],
                    SMb, ALU.subtract))
            v_lnmul = None
            for ct in range(CT):
                fn = lambda E, ct=ct, SIb=SIb: E.tensor_tensor(
                    xnb_data(ct), xnb_data(ct), SIb, ALU.mult)
                v_lnmul = P.opi("vector", "v", fn) if ct == CT - 1 else (
                    P.op("vector", fn) or v_lnmul)
            stat_free[sp] = v_lnmul
            P.wait("scalar", "v", v_lnmul)
            P.wait("scalar", "s", s_params)
            s_gb = None
            for ct in range(CT):
                fn = lambda E, ct=ct: E.activation(
                    xnb_data(ct), xnb_data(ct), AF.Identity,
                    scale=pv_col("ln1g", ct), bias=pv_col("ln1b", ct))
                s_gb = P.opi("scalar", "s", fn) if ct == CT - 1 else (
                    P.op("scalar", fn) or s_gb)
            P.wait("vector", "s", s_gb)
            P.wait("vector", "t", prev_kvr_t[0])
            v_mix = None
            for mixn, omn, dst in (("mixk", "omk", KIN), ("mixv", "omv", VIN),
                                   ("mixr", "omr", RIN)):
                for ct in range(CT):
                    P.op("vector", lambda E, ct=ct, m=mixn, d=dst:
                         E.tensor_scalar(d[:, ct * TCH:(ct + 1) * TCH],
                                         xnb_data(ct), pv_col(m, ct), None,
                                         ALU.mult))
                for ct in range(CT):
                    fn = lambda E, ct=ct, o=omn, d=dst: E.scalar_tensor_tensor(
                        d[:, ct * TCH:(ct + 1) * TCH], xnb_yp(ct),
                        pv_col(o, ct), d[:, ct * TCH:(ct + 1) * TCH],
                        ALU.mult, ALU.add)
                    if mixn == "mixr" and ct == CT - 1:
                        v_mix = P.opi("vector", "v", fn)
                    else:
                        P.op("vector", fn)
            P.op("vector", lambda E: E.tensor_copy(
                XNBb[:, 0:CT],
                bass.AP(XNb.ap().tensor, 8 + TCH - 1,
                        [[CT * XBLK, 128], [XBLK, CT], [1, 1]])))
            front_mix[i] = v_mix
            front_free[i] = v_lnmul

        def emit_kvr(i, mid_cb=None):
            P.wait("tensor", "v", front_mix[i])
            for wname, rhs_t in (("v", VIN), ("r", RIN), ("k", KIN)):
                for mo in range(CT):
                    bank, tg = mm_group(
                        lambda ki, rhs_t=rhs_t: rhs_t[:, ki * TCH:(ki + 1) * TCH],
                        lambda ki, w=WA[wname], mo=mo:
                            w[:, ki * D + mo * 128: ki * D + (mo + 1) * 128],
                        CT)
                    if wname == "k":
                        P.wait("scalar", "t", tg)
                        P.wait("scalar", "v", bscan_done[i - 1])
                        sv = P.opi("scalar", "s", lambda E, mo=mo, bank=bank:
                                   E.activation(EK[:, mo * TCH:(mo + 1) * TCH],
                                                bank[:], AF.Exp))
                        bank_free[bank.name] = ("s", sv)
                    elif wname == "v":
                        P.wait("vector", "t", tg)
                        vv2 = P.opi("vector", "v", lambda E, mo=mo, bank=bank:
                                    E.tensor_copy(VV[:, mo * TCH:(mo + 1) * TCH],
                                                  bank[:]))
                        bank_free[bank.name] = ("v", vv2)
                    else:
                        P.wait("scalar", "t", tg)
                        P.wait("scalar", "v", v_rw_d[i - 1])
                        sv = P.opi("scalar", "s", lambda E, mo=mo, bank=bank:
                                   E.activation(RG[:, mo * TCH:(mo + 1) * TCH],
                                                bank[:], AF.Sigmoid))
                        bank_free[bank.name] = ("s", sv)
                if wname == "v":
                    vgrp_t[0] = P.val("t")
                    if mid_cb is not None:
                        mid_cb()
            prev_kvr_t[0] = P.val("t")
            s_ekd[i] = P.val("s")

        def emit_back(i):
            b, c = chunks[i]
            P.wait("vector", "s", s_ekd[i])
            if c == 0:
                for ct in range(CT):
                    P.op("vector", lambda E, ct=ct: E.tensor_tensor(
                        EK[:, ct * TCH:ct * TCH + 1],
                        EK[:, ct * TCH:ct * TCH + 1],
                        pv_col("etf", ct), ALU.mult))
                P.op("vector", lambda E: E.tensor_copy(CARA[:], ZCOL[:]))
                P.op("vector", lambda E: E.tensor_copy(CARB[:], ZCOL[:]))
            P.opi("vector", "v", lambda E: E.tensor_tensor(
                VV[:], EK[:], VV[:], ALU.mult))
            P.wait("vector", "t", wo_t[i - 1])
            for ct in range(CT):
                P.op("vector", lambda E, ct=ct: E.tensor_tensor_scan(
                    aa_blk(ct), pv_col("ew", ct).broadcast_to([128, TCH]),
                    VV[:, ct * TCH:(ct + 1) * TCH],
                    CARA[:, ct:ct + 1], ALU.mult, ALU.add))
            for ct in range(CT):
                P.op("vector", lambda E, ct=ct: E.tensor_tensor_scan(
                    BB[:, ct * TCH:(ct + 1) * TCH],
                    pv_col("ew", ct).broadcast_to([128, TCH]),
                    EK[:, ct * TCH:(ct + 1) * TCH],
                    CARB[:, ct:ct + 1], ALU.mult, ALU.add))
            P.op("vector", lambda E: E.tensor_copy(
                CARA[:], sb_ap(XN, TCH - 1,
                               [[CT * (TCH + 1), 128], [TCH, CT], [1, 1]])))
            v_car = P.opi("vector", "v", lambda E: E.tensor_copy(
                CARB[:], sb_ap(BB, TCH - 1, [[CT * TCH, 128], [TCH, CT], [1, 1]])))
            bscan_done[i] = v_car
            P.wait("scalar", "v", v_car)
            P.opi("scalar", "s", lambda E: E.activation(BB[:], BB[:], AF.Ln,
                                                        bias=EPST[:, 1:2]))
            s_rec = P.opi("scalar", "s", lambda E: E.activation(
                BB[:], BB[:], AF.Exp, scale=-1.0))
            P.wait("vector", "s", s_rec)
            P.opi("vector", "v", lambda E: E.tensor_tensor(AAv, AAv, BB[:],
                                                           ALU.mult))
            v_rw_d[i] = P.opi("vector", "v", lambda E: E.tensor_tensor(
                BBrw, AAv, RG[:], ALU.mult))

        def emit_wo(i):
            b, c = chunks[i]
            P.wait("tensor", "v", v_rw_d[i])
            for mo in range(CT):
                bank, tg = mm_group(
                    lambda ki: BBrw[:, ki * TCH:(ki + 1) * TCH],
                    lambda ki, mo=mo:
                        WA["o"][:, ki * D + mo * 128: ki * D + (mo + 1) * 128],
                    CT)
                P.wait("vector", "t", tg)
                P.wait("vector", "ldx", vxc[i])
                vv2 = P.opi("vector", "v", lambda E, mo=mo, bank=bank:
                            E.tensor_tensor(XC[:, mo * TCH:(mo + 1) * TCH],
                                            XC[:, mo * TCH:(mo + 1) * TCH],
                                            bank[:], ALU.add))
                bank_free[bank.name] = ("v", vv2)
            wo_t[i] = P.val("t")
            v_x1 = P.val("v")
            P.wait("sync", "v", v_x1)
            dst = out if debug_upto in ("A", "TAPS2") else x1sp
            spill_done[i] = P.opi("sync", "stx",
                                  lambda E, b=b, c=c, dst=dst: E.dma_start(
                                      out=dram_chunk(dst, b, c),
                                      in_=sb_ap(XC, 0, SBW)), 16)

        NCH = len(chunks)

        def emit_front_taps():
            P.wait("gpsimd", "v", front_mix[0])
            xnd = bass.AP(XNb.ap().tensor, 8,
                          [[CT * XBLK, 128], [XBLK, CT], [1, TCH]])
            P.opi("gpsimd", "gmi", lambda E: E.dma_start(
                out=dram_chunk(out, 0, 0), in_=xnd), 16)
            P.opi("gpsimd", "gmi", lambda E: E.dma_start(
                out=dram_chunk(out, 0, 1), in_=sb_ap(KIN, 0, SBW)), 16)
            mb = STATM[0][:, :].bitcast(BF16)
            P.opi("gpsimd", "gmi", lambda E: E.dma_start(
                out=dr_ap(out, 2 * TCH, [[L, 128], [1, TCH]]),
                in_=mb[:, 0:TCH]), 16)
            P.opi("gpsimd", "gmi", lambda E: E.dma_start(
                out=dr_ap(out, 3 * TCH, [[L, 128], [1, TCH]]),
                in_=mb[:, TCH:2 * TCH]), 16)
            P.wait("gpsimd", "gmi", P.val("gmi"))

        xbload = {}
        xbload[0] = P.opi("gpsimd", "gxb0", lambda E, b=chunks[0][0],
                          c=chunks[0][1]: E.dma_start(
                              out=sb_ap(XB, 0, SBW), in_=dram_chunk(xt, b, c)), 16)
        emit_front_stats(0)
        emit_front_rest(0)
        if debug_upto == "TAPS":
            emit_front_taps()
        nloop = NCH
        if debug_upto == "TAPS":
            nloop = 0
        elif debug_upto == "TAPS1":
            nloop = 1
        elif debug_upto == "TAPS2":
            nloop = 2
        for i in range(nloop):
            allow_pre = debug_upto not in ("TAPS1", "TAPS2") or \
                (debug_upto == "TAPS2" and i == 0)

            def mid_cb(i=i, allow_pre=allow_pre):
                if i + 1 < NCH and allow_pre:
                    P.wait("gpsimd", "v", front_free[i])
                    xbload[i + 1] = P.opi("gpsimd", "gxb0",
                                          lambda E, b=chunks[i + 1][0],
                                          c=chunks[i + 1][1]: E.dma_start(
                                              out=sb_ap(XB, 0, SBW),
                                              in_=dram_chunk(xt, b, c)), 16)
                    emit_front_stats(i + 1)

            emit_kvr(i, mid_cb)
            if i + 1 < NCH and allow_pre:
                emit_front_rest(i + 1)
            if i >= 1:
                emit_wo(i - 1)
            if i >= 1:
                P.wait("sync", "stx", spill_done[i - 1])
            vxc[i] = P.opi("sync", "ldx", lambda E, b=chunks[i][0],
                           c=chunks[i][1]: E.dma_start(
                               out=sb_ap(XC, 0, SBW),
                               in_=dram_chunk(xt, b, c)), 16)
            emit_back(i)
        if debug_upto == "TAPS1":
            P.wait("sync", "v", v_rw_d[0])
            P.wait("sync", "s", P.val("s"))
            P.opi("sync", "stx", lambda E: E.dma_start(
                out=dram_chunk(out, 0, 0), in_=sb_ap(EK, 0, SBW)), 16)
            P.opi("sync", "stx", lambda E: E.dma_start(
                out=dram_chunk(out, 0, 1), in_=sb_ap(VV, 0, SBW)), 16)
            P.opi("sync", "stx", lambda E: E.dma_start(
                out=dram_chunk(out, 0, 2),
                in_=bass.AP(XN.ap().tensor, 0,
                            [[CT * (TCH + 1), 128], [TCH, CT], [1, TCH]])), 16)
            P.wait("gpsimd", "v", v_rw_d[0])
            P.opi("gpsimd", "gmi", lambda E: E.dma_start(
                out=dram_chunk(out, 0, 3),
                in_=bass.AP(BB.ap().tensor, 0,
                            [[CT * TCH, 128], [TCH, CT], [1, TCH]])), 16)
            P.wait("sync", "stx", P.val("stx"))
            P.wait("gpsimd", "gmi", P.val("gmi"))
        if debug_upto == "TAPS2":
            emit_wo(1)
        if debug_upto not in ("TAPS", "TAPS1", "TAPS2"):
            emit_wo(NCH - 1)

        if debug_upto in ("A", "TAPS", "TAPS1", "TAPS2"):
            P.wait("sync", "stx", P.val("stx"))
        else:
            finals = [("t", P.val("t")), ("s", P.val("s")), ("v", P.val("v")),
                      ("stx", P.val("stx")), ("ldx", P.val("ldx")),
                      ("gxb0", P.val("gxb0")), ("gxb1", P.val("gxb1")),
                      ("gmi", P.val("gmi"))]
            for eng in ENGS:
                for cn, cv in finals:
                    P.wait(eng, cn, cv)

            X1C = XC
            XOC = BB
            X1B = XB
            ZBB = VIN
            XSQB = VV[:, :].bitcast(BF16)[:, 0:CT * TCH]
            YN = XN
            MIXB = KIN
            YB = VIN
            RRG = RIN
            X3B = RG
            EKb = EK[:, :].bitcast(BF16)
            VVb = VV[:, :].bitcast(BF16)
            YNb = XN[:, 0:CT * TCH].bitcast(BF16)
            WOb = WA["o"][:, :]

            def kk_blk(f):
                base = EKb if f < 16 else VVb
                return base[:, (f % 16) * TCH:((f % 16) + 1) * TCH]

            def g0_blk(f):
                base = YNb if f < 16 else WOb
                return base[:, (f % 16) * TCH:((f % 16) + 1) * TCH]

            g1_blk = kk_blk

            WS = [WA["k"], WA["v"], WA["r"]]

            def yn_blk(ct, lo, hi):
                base = ct * (TCH + 1)
                return YN[:, base + lo: base + hi]

            ws_state = [None, None, None]
            ws_i = [0]
            ws_sems = ["gw0", "gw1", "gw2"]

            def ws_next():
                i = ws_i[0] % 3
                ws_i[0] += 1
                return i

            def ws_load(i, fn):
                if ws_state[i] is not None:
                    P.wait("gpsimd", ws_state[i][0], ws_state[i][1])
                return P.opi("gpsimd", ws_sems[i], fn, 16)

            def ws_lhsT(wi, ki, mo, blk):
                return WS[wi][:, ki * blk + mo * 128: ki * blk + (mo + 1) * 128]

            x1_reuse = {}
            xo_reuse = {}
            out_done = {}
            x1b_reuse = {}
            prev_wfr_t = [0]
            prev_w1_t = [0]
            tmpb_free = [0, 0]
            tmp_i = [0]

            ld1_pre = {}
            gxb_pre = {}
            for k, (b, c) in enumerate(chunks):
                if k == 0:
                    P.wait("sync", "stx", spill_done[k])
                    v_x1c = P.opi("sync", "ld1", lambda E, b=b, c=c: E.dma_start(
                        out=sb_ap(X1C, 0, SBW), in_=dram_chunk(x1sp, b, c)), 16)
                    P.wait("gpsimd", "stx", spill_done[k])
                    v_x1b = P.opi("gpsimd", "gxb0", lambda E, b=b, c=c:
                                  E.dma_start(out=sb_ap(X1B, 0, SBW),
                                              in_=dram_chunk(x1sp, b, c)), 16)
                else:
                    v_x1c = ld1_pre[k]
                    v_x1b = gxb_pre[k]
                if k >= 1:
                    P.wait("sync", "sto", out_done[k - 1])
                    P.wait("sync", "v", xo_reuse[k - 1])
                v_xo = P.opi("sync", "ldo", lambda E, b=b, c=c: E.dma_start(
                    out=sb_ap(XOC, 0, SBW), in_=dram_chunk(xt, b, c)), 16)

                s_is, sp, SM, SI = emit_stats(X1B, [("gxb0", v_x1b)], None,
                                              [("t", prev_wfr_t[0]),
                                               ("gxb0", v_x1b)], xsq=XSQB)
                x1b_reuse[k] = P.val("t")
                if k + 1 < len(chunks):
                    nb, ncc = chunks[k + 1]
                    P.wait("gpsimd", "t", x1b_reuse[k])
                    gxb_pre[k + 1] = P.opi(
                        "gpsimd", "gxb0", lambda E, b=nb, c=ncc: E.dma_start(
                            out=sb_ap(X1B, 0, SBW), in_=dram_chunk(x1sp, b, c)), 16)
                P.wait("vector", "t", prev_w1_t[0])
                if c == 0:
                    P.op("vector", lambda E: E.tensor_copy(
                        sb_ap(YN, 0, [[CT * (TCH + 1), 128], [TCH + 1, CT], [1, 1]]),
                        sb_ap(ZCOL, 0, [[CT, 128], [1, CT], [1, 1]])))
                else:
                    P.op("vector", lambda E: E.tensor_copy(
                        sb_ap(YN, 0, [[CT * (TCH + 1), 128], [TCH + 1, CT], [1, 1]]),
                        sb_ap(XNB, 0, [[CT, 128], [1, CT], [1, 1]])))
                P.wait("vector", "ld1", v_x1c)
                P.wait("vector", "s", s_is)
                for ct in range(CT):
                    P.op("vector", lambda E, ct=ct, SM=SM: E.tensor_tensor(
                        yn_blk(ct, 1, TCH + 1), X1C[:, ct * TCH:(ct + 1) * TCH],
                        SM, ALU.subtract))
                v_ln = None
                for ct in range(CT):
                    fn = lambda E, ct=ct, SI=SI: E.tensor_tensor(
                        yn_blk(ct, 1, TCH + 1), yn_blk(ct, 1, TCH + 1),
                        SI, ALU.mult)
                    v_ln = P.opi("vector", "v", fn) if ct == CT - 1 else (P.op("vector", fn) or v_ln)
                stat_free[sp] = v_ln
                P.wait("scalar", "v", v_ln)
                s_yn = None
                for ct in range(CT):
                    fn = lambda E, ct=ct: E.activation(
                        yn_blk(ct, 1, TCH + 1), yn_blk(ct, 1, TCH + 1), AF.Identity,
                        scale=pv_col("ln2g", ct), bias=pv_col("ln2b", ct))
                    s_yn = P.opi("scalar", "s", fn) if ct == CT - 1 else (P.op("scalar", fn) or s_yn)

                P.wait("vector", "s", s_yn)
                for ct in range(CT):
                    P.op("vector", lambda E, ct=ct: E.tensor_tensor(
                        MIXB[:, ct * TCH:(ct + 1) * TCH], yn_blk(ct, 1, TCH + 1),
                        yn_blk(ct, 0, TCH), ALU.add))
                v_mix = None
                for ct in range(CT):
                    fn = lambda E, ct=ct: E.tensor_copy(
                        YB[:, ct * TCH:(ct + 1) * TCH], yn_blk(ct, 1, TCH + 1))
                    v_mix = P.opi("vector", "v", fn) if ct == CT - 1 else (P.op("vector", fn) or v_mix)

                P.op("vector", lambda E: E.tensor_copy(
                    sb_ap(XNB, 0, [[CT, 128], [1, CT], [1, 1]]),
                    sb_ap(YN, TCH, [[CT * (TCH + 1), 128], [TCH + 1, CT], [1, 1]])))

                P.wait("tensor", "v", v_mix)
                for g in range(4):
                    wi = ws_next()
                    lv = ws_load(wi, lambda E, g=g, wi=wi: E.dma_start(
                        out=sb_ap(WS[wi], 0, [[CT * D, 128], [1024, CT], [1, 1024]]),
                        in_=dr_ap(wfk, g * 1024,
                                  [[4 * D, 128], [128 * 4 * D, CT], [1, 1024]])))
                    P.wait("tensor", ws_sems[wi], lv)
                    for mo in range(8):
                        moa = g * 8 + mo
                        bank, tg = mm_group(
                            lambda ki: MIXB[:, ki * TCH:(ki + 1) * TCH],
                            lambda ki, wi=wi, mo=mo: ws_lhsT(wi, ki, mo, 1024),
                            CT)
                        ti = tmp_i[0] % 2
                        tmp_i[0] += 1
                        P.wait("scalar", "t", tg)
                        P.wait("scalar", "v", tmpb_free[ti])
                        sv = P.opi("scalar", "s", lambda E, bank=bank, ti=ti:
                                   E.activation(TMPB[ti][:], bank[:], AF.Relu,
                                                scale=0.5))
                        bank_free[bank.name] = ("s", sv)
                        P.wait("vector", "s", sv)
                        vv2 = P.opi("vector", "v", lambda E, moa=moa, ti=ti:
                                    E.tensor_tensor(kk_blk(moa), TMPB[ti][:],
                                                    TMPB[ti][:], ALU.mult))
                        tmpb_free[ti] = vv2
                    ws_state[wi] = ("t", P.val("t"))
                v_kk_done = P.val("v")

                wi = ws_next()
                lv = ws_load(wi, lambda E, wi=wi: E.dma_start(
                    out=sb_ap(WS[wi], 0, [[CT * D, 128], [D, CT], [1, D]]),
                    in_=dr_ap(wfr, 0, [[D, 128], [128 * D, CT], [1, D]])))
                P.wait("tensor", ws_sems[wi], lv)
                for mo in range(CT):
                    bank, tg = mm_group(
                        lambda ki: YB[:, ki * TCH:(ki + 1) * TCH],
                        lambda ki, wi=wi, mo=mo: ws_lhsT(wi, ki, mo, 1024),
                        CT)
                    P.wait("scalar", "t", tg)
                    sv = P.opi("scalar", "s", lambda E, mo=mo, bank=bank:
                               E.activation(RRG[:, mo * TCH:(mo + 1) * TCH],
                                            bank[:], AF.Sigmoid))
                    bank_free[bank.name] = ("s", sv)
                ws_state[wi] = ("t", P.val("t"))
                prev_wfr_t[0] = P.val("t")
                s_rr_done = P.val("s")

                P.wait("tensor", "v", v_kk_done)
                for g in range(4):
                    wi = ws_next()
                    lv = ws_load(wi, lambda E, g=g, wi=wi: E.dma_start(
                        out=sb_ap(WS[wi], 0, [[CT * D, 128], [256, 32], [1, 256]]),
                        in_=dr_ap(wfv, g * 256, [[D, 128], [128 * D, 32], [1, 256]])))
                    P.wait("tensor", ws_sems[wi], lv)
                    for mo in range(2):
                        moa = g * 2 + mo
                        bank, tg = mm_group(
                            lambda ki: kk_blk(ki),
                            lambda ki, wi=wi, mo=mo: ws_lhsT(wi, ki, mo, 256),
                            32)
                        P.wait("vector", "t", tg)
                        P.wait("vector", "s", s_rr_done)
                        P.op("vector", lambda E, moa=moa, bank=bank:
                             E.tensor_tensor(TMPF[0][:],
                                             RRG[:, moa * TCH:(moa + 1) * TCH],
                                             bank[:], ALU.mult))
                        vv2 = P.opi("vector", "v", lambda E, moa=moa:
                                    E.tensor_tensor(X1C[:, moa * TCH:(moa + 1) * TCH],
                                                    X1C[:, moa * TCH:(moa + 1) * TCH],
                                                    TMPF[0][:], ALU.add))
                        bank_free[bank.name] = ("v", vv2)
                    ws_state[wi] = ("t", P.val("t"))

                P.wait("vector", "ldo", v_xo)
                v_z = P.opi("vector", "v", lambda E: E.tensor_tensor(
                    X1C[:], X1C[:], XOC[:], ALU.add))
                P.wait("vector", "t", x1b_reuse[k])
                v_zb = P.opi("vector", "v", lambda E: E.tensor_copy(ZBB[:], X1C[:]))
                s_is, sp, SM, SI = emit_stats(ZBB, [("v", v_zb)], X1C,
                                              [("v", v_z)], xsq=XSQB)
                P.wait("vector", "s", s_is)
                for ct in range(CT):
                    P.op("vector", lambda E, ct=ct, SM=SM: E.tensor_tensor(
                        XOC[:, ct * TCH:(ct + 1) * TCH],
                        X1C[:, ct * TCH:(ct + 1) * TCH], SM, ALU.subtract))
                for ct in range(CT):
                    P.op("vector", lambda E, ct=ct, SI=SI: E.tensor_tensor(
                        XOC[:, ct * TCH:(ct + 1) * TCH],
                        XOC[:, ct * TCH:(ct + 1) * TCH], SI, ALU.mult))
                v_x3 = P.opi("vector", "v", lambda E: E.tensor_copy(X3B[:], XOC[:]))
                stat_free[sp] = v_x3
                x1_reuse[k] = v_x3
                if k + 1 < len(chunks):
                    nb, ncc = chunks[k + 1]
                    P.wait("sync", "v", v_x3)
                    ld1_pre[k + 1] = P.opi(
                        "sync", "ld1", lambda E, b=nb, c=ncc: E.dma_start(
                            out=sb_ap(X1C, 0, SBW), in_=dram_chunk(x1sp, b, c)), 16)

                for g in range(4):
                    wi_a = ws_next()
                    lva = ws_load(wi_a, lambda E, g=g, wi=wi_a: E.dma_start(
                        out=sb_ap(WS[wi], 0, [[CT * D, 128], [1024, CT], [1, 1024]]),
                        in_=dr_ap(k0, g * 1024,
                                  [[4 * D, 128], [128 * 4 * D, CT], [1, 1024]])))
                    wi_b = ws_next()
                    lvb = ws_load(wi_b, lambda E, g=g, wi=wi_b: E.dma_start(
                        out=sb_ap(WS[wi], 0, [[CT * D, 128], [1024, CT], [1, 1024]]),
                        in_=dr_ap(k0, D * 4 * D + g * 1024,
                                  [[4 * D, 128], [128 * 4 * D, CT], [1, 1024]])))
                    P.wait("vector", ws_sems[wi_a], lva)
                    P.wait("vector", ws_sems[wi_b], lvb)
                    v_sub = P.opi("vector", "v", lambda E, a=wi_a, bb2=wi_b:
                                  E.tensor_tensor(WS[a][:], WS[a][:],
                                                  WS[bb2][:], ALU.subtract))
                    ws_state[wi_b] = ("v", v_sub)
                    P.wait("tensor", "v", v_sub)
                    P.wait("tensor", "v", v_x3)
                    for mo in range(8):
                        moa = g * 8 + mo
                        bank, tg = mm_group(
                            lambda ki: X3B[:, ki * TCH:(ki + 1) * TCH],
                            lambda ki, wi=wi_a, mo=mo: ws_lhsT(wi, ki, mo, 1024),
                            CT)
                        P.wait("scalar", "t", tg)
                        sv = P.opi("scalar", "s", lambda E, moa=moa, bank=bank:
                                   E.activation(g0_blk(moa), bank[:], AF.Relu,
                                                scale=SC[:, 0:1]))
                        bank_free[bank.name] = ("s", sv)
                    ws_state[wi_a] = ("t", P.val("t"))
                s_g0_done = P.val("s")

                for g in range(16):
                    wi = ws_next()
                    lv = ws_load(wi, lambda E, g=g, wi=wi: E.dma_start(
                        out=sb_ap(WS[wi], 0, [[CT * D, 128], [256, 32], [1, 256]]),
                        in_=dr_ap(k1, g * 256,
                                  [[4 * D, 128], [128 * 4 * D, 32], [1, 256]])))
                    P.wait("vector", ws_sems[wi], lv)
                    v_sub = P.opi("vector", "v", lambda E, wi=wi: E.tensor_scalar(
                        WS[wi][:], WS[wi][:], -0.5, None, ALU.add))
                    P.wait("tensor", "v", v_sub)
                    P.wait("tensor", "s", s_g0_done)
                    for mo in range(2):
                        moa = g * 2 + mo
                        bank, tg = mm_group(
                            lambda ki: g0_blk(ki),
                            lambda ki, wi=wi, mo=mo: ws_lhsT(wi, ki, mo, 256),
                            32)
                        P.wait("scalar", "t", tg)
                        sv = P.opi("scalar", "s", lambda E, moa=moa, bank=bank:
                                   E.activation(g1_blk(moa), bank[:], AF.Relu,
                                                scale=SC[:, 1:2]))
                        bank_free[bank.name] = ("s", sv)
                    ws_state[wi] = ("t", P.val("t"))
                prev_w1_t[0] = P.val("t")

                for g in range(4):
                    wi = ws_next()
                    lv = ws_load(wi, lambda E, g=g, wi=wi: E.dma_start(
                        out=sb_ap(WS[wi], 0, [[CT * D, 128], [256, 32], [1, 256]]),
                        in_=dr_ap(k2, g * 256,
                                  [[D, 128], [128 * D, 32], [1, 256]])))
                    P.wait("vector", ws_sems[wi], lv)
                    v_sub = P.opi("vector", "v", lambda E, wi=wi: E.tensor_scalar(
                        WS[wi][:], WS[wi][:], -0.5, None, ALU.add))
                    P.wait("tensor", "v", v_sub)
                    for mo in range(2):
                        moa = g * 2 + mo
                        bank, tg = mm_group(
                            lambda ki: g1_blk(ki),
                            lambda ki, wi=wi, mo=mo: ws_lhsT(wi, ki, mo, 256),
                            32)
                        P.wait("vector", "t", tg)
                        vv2 = P.opi("vector", "v", lambda E, moa=moa, bank=bank:
                                    E.scalar_tensor_tensor(
                                        XOC[:, moa * TCH:(moa + 1) * TCH],
                                        bank[:], SC[:, 2:3],
                                        XOC[:, moa * TCH:(moa + 1) * TCH],
                                        ALU.mult, ALU.add))
                        bank_free[bank.name] = ("v", vv2)
                    ws_state[wi] = ("t", P.val("t"))
                v_z2 = P.val("v")

                v_zb2 = P.opi("vector", "v", lambda E: E.tensor_copy(ZBB[:], XOC[:]))
                s_is, sp, SM, SI = emit_stats(ZBB, [("v", v_zb2)], XOC,
                                              [("v", v_z2)], xsq=XSQB)
                P.wait("vector", "s", s_is)
                for ct in range(CT):
                    P.op("vector", lambda E, ct=ct, SM=SM: E.tensor_tensor(
                        XOC[:, ct * TCH:(ct + 1) * TCH],
                        XOC[:, ct * TCH:(ct + 1) * TCH], SM, ALU.subtract))
                v_out = None
                for ct in range(CT):
                    fn = lambda E, ct=ct, SI=SI: E.tensor_tensor(
                        XOC[:, ct * TCH:(ct + 1) * TCH],
                        XOC[:, ct * TCH:(ct + 1) * TCH], SI, ALU.mult)
                    v_out = P.opi("vector", "v", fn) if ct == CT - 1 else (P.op("vector", fn) or v_out)
                stat_free[sp] = v_out
                xo_reuse[k] = v_out
                P.wait("sync", "v", v_out)
                out_done[k] = P.opi("sync", "sto", lambda E, b=b, c=c: E.dma_start(
                    out=dram_chunk(out, b, c), in_=sb_ap(XOC, 0, SBW)), 16)

            P.wait("sync", "sto", P.val("sto"))

        with nc.Block() as blk:
            @blk.tensor
            def _(E):
                for f in P.ops["tensor"]:
                    f(E)

            @blk.scalar
            def _(E):
                for f in P.ops["scalar"]:
                    f(E)

            @blk.vector
            def _(E):
                for f in P.ops["vector"]:
                    f(E)

            @blk.sync
            def _(E):
                for f in P.ops["sync"]:
                    f(E)

            @blk.gpsimd
            def _(E):
                for f in P.ops["gpsimd"]:
                    f(E)

    return nc


def kernel(L0_attn_x, L0_attn_ln1_g, L0_attn_ln1_b, L0_attn_Wk, L0_attn_Wv,
           L0_attn_Wr, L0_attn_Wo, L0_attn_time_decay, L0_attn_time_first,
           L0_attn_mix_k, L0_attn_mix_v, L0_attn_mix_r, L0_attn_ln2_g,
           L0_attn_ln2_b, L0_attn_Wfk, L0_attn_Wfv, L0_attn_Wfr,
           L0_ffn_k0, L0_ffn_k1, L0_ffn_k2, L0_ffn_s0, L0_ffn_s1, L0_ffn_s2,
           _debug_upto=None):
    x = np.asarray(L0_attn_x, dtype=np.float32)
    xT = np.ascontiguousarray(np.transpose(x, (0, 2, 1)))

    def c32(a):
        return np.ascontiguousarray(np.asarray(a, dtype=np.float32))

    def cu8(a):
        a = np.asarray(a)
        return np.ascontiguousarray(a.view(np.uint8).reshape(a.shape))

    def srep(a):
        return np.ascontiguousarray(
            np.broadcast_to(np.asarray(a, dtype=np.float32).reshape(1, 1), (128, 1)))

    shared = {
        "wk": c32(L0_attn_Wk), "wv": c32(L0_attn_Wv), "wr": c32(L0_attn_Wr),
        "wo": c32(L0_attn_Wo), "wfk": c32(L0_attn_Wfk), "wfv": c32(L0_attn_Wfv),
        "wfr": c32(L0_attn_Wfr),
        "k0": cu8(L0_ffn_k0), "k1": cu8(L0_ffn_k1), "k2": cu8(L0_ffn_k2),
        "ln1g": c32(L0_attn_ln1_g), "ln1b": c32(L0_attn_ln1_b),
        "mixk": c32(L0_attn_mix_k), "mixv": c32(L0_attn_mix_v),
        "mixr": c32(L0_attn_mix_r), "ln2g": c32(L0_attn_ln2_g),
        "ln2b": c32(L0_attn_ln2_b), "tdr": c32(L0_attn_time_decay),
        "tfr": c32(L0_attn_time_first),
        "s0": srep(L0_ffn_s0), "s1": srep(L0_ffn_s1), "s2": srep(L0_ffn_s2),
        "ones": np.ones((128, 128), dtype=np.float32),
    }
    in_maps = []
    for core in range(NCORES):
        m = dict(shared)
        m["xt"] = np.ascontiguousarray(xT[core * BPC:(core + 1) * BPC])
        in_maps.append(m)

    nc = build(debug_upto=_debug_upto)
    res = run_bass_kernel_spmd(nc, in_maps, core_ids=list(range(NCORES)))
    parts = [np.transpose(r["out"], (0, 2, 1)) for r in res.results]
    return np.ascontiguousarray(np.concatenate(parts, axis=0)).astype(np.float32)


# version 23
# speedup vs baseline: 1.0521x; 1.0131x over previous
import numpy as np
from contextlib import ExitStack

import concourse.bass as bass
import concourse.mybir as mybir
from concourse.bass_utils import run_bass_kernel_spmd

F32 = mybir.dt.float32
BF16 = mybir.dt.bfloat16
U8 = mybir.dt.uint8
AF = mybir.ActivationFunctionType
ALU = mybir.AluOpType

B_ALL, L, D = 16, 2048, 1024
NCORES = 8
BPC = B_ALL // NCORES
CT = D // 128
TCH = 512
NCHB = L // TCH
LN_EPS = 1e-5
WKV_EPS = 1e-8
INV_D = 1.0 / D

ENGS = ("tensor", "scalar", "vector", "sync", "gpsimd")


class Prog:

    def __init__(self, nc, es):
        self.nc = nc
        self.ops = {k: [] for k in ENGS}
        self.sems = {}
        self.es = es
        self.last_wait = {}

    def ctr(self, name):
        sem = self.es.enter_context(self.nc.semaphore(name))
        self.sems[name] = [sem, 0]

    def val(self, name):
        return self.sems[name][1]

    def wait(self, eng, name, v=None):
        s, cur = self.sems[name]
        if v is None:
            v = cur
        if v <= 0:
            return
        key = (eng, name)
        if self.last_wait.get(key, -1) >= v:
            return
        self.last_wait[key] = v
        self.ops[eng].append(lambda E, s=s, v=v: E.wait_ge(s, v))

    def op(self, eng, fn):
        self.ops[eng].append(fn)

    def opi(self, eng, name, fn, n=1):
        rec = self.sems[name]
        rec[1] += n
        self.ops[eng].append(lambda E, f=fn, s=rec[0], n=n: f(E).then_inc(s, n))
        return rec[1]

    def dma(self, eng, name, fn):
        self.wait(eng, name)
        return self.opi(eng, name, fn, 16)


def sb_ap(t, offset, dims):
    return bass.AP(t.ap().tensor, offset, dims)


def dr_ap(t, offset, dims):
    return bass.AP(t.tensor, offset, dims)


DBG_SERIAL = False


def build(debug_upto=None):
    nc = bass.Bass("TRN2", target_bir_lowering=False, debug=False,
                   detect_race_conditions=False)

    xt = nc.dram_tensor("xt", [BPC, D, L], F32, kind="ExternalInput").ap()
    wk = nc.dram_tensor("wk", [D, D], F32, kind="ExternalInput").ap()
    wv = nc.dram_tensor("wv", [D, D], F32, kind="ExternalInput").ap()
    wr = nc.dram_tensor("wr", [D, D], F32, kind="ExternalInput").ap()
    wo = nc.dram_tensor("wo", [D, D], F32, kind="ExternalInput").ap()
    wfk = nc.dram_tensor("wfk", [D, 4 * D], F32, kind="ExternalInput").ap()
    wfv = nc.dram_tensor("wfv", [4 * D, D], F32, kind="ExternalInput").ap()
    wfr = nc.dram_tensor("wfr", [D, D], F32, kind="ExternalInput").ap()
    k0 = nc.dram_tensor("k0", [2 * D, 4 * D], U8, kind="ExternalInput").ap()
    k1 = nc.dram_tensor("k1", [4 * D, 4 * D], U8, kind="ExternalInput").ap()
    k2 = nc.dram_tensor("k2", [4 * D, D], U8, kind="ExternalInput").ap()
    VEC_NAMES = ["ln1g", "ln1b", "mixk", "mixv", "mixr", "ln2g", "ln2b", "tdr", "tfr"]
    vecs = {n: nc.dram_tensor(n, [D], F32, kind="ExternalInput").ap() for n in VEC_NAMES}
    s_in = {n: nc.dram_tensor(n, [128, 1], F32, kind="ExternalInput").ap()
            for n in ("s0", "s1", "s2")}
    ones_in = nc.dram_tensor("ones", [128, 128], F32, kind="ExternalInput").ap()
    out = nc.dram_tensor("out", [BPC, D, L], F32, kind="ExternalOutput").ap()
    x1sp = nc.dram_tensor("x1sp", [BPC, D, L], F32, kind="Internal").ap()

    def dram_chunk(ap_t, b, c):
        return dr_ap(ap_t, b * D * L + c * TCH,
                     [[L, 128], [128 * L, CT], [1, TCH]])

    SBW = [[CT * TCH, 128], [TCH, CT], [1, TCH]]

    chunks = [(b, c) for b in range(BPC) for c in range(NCHB)]

    with ExitStack() as es:
        e = es.enter_context
        P = Prog(nc, es)

        NPV = 14
        PV = e(nc.sbuf_tensor("PV", [128, NPV * CT], F32))
        ONESB = e(nc.sbuf_tensor("ONESB", [128, 128], BF16))
        SC = e(nc.sbuf_tensor("SC", [128, 4], F32))
        ZCOL = e(nc.sbuf_tensor("ZCOL", [128, CT], F32))
        STATM = [e(nc.sbuf_tensor(f"STATM{i}", [128, TCH], F32)) for i in range(2)]
        STATI = [e(nc.sbuf_tensor(f"STATI{i}", [128, TCH], F32)) for i in range(2)]
        STATT = [e(nc.sbuf_tensor("STATT", [128, TCH], F32))] * 2
        EPST = e(nc.sbuf_tensor("EPST", [128, 2], F32))

        POOL = [e(nc.psum_tensor(f"PP{i}", [128, TCH], F32)) for i in range(8)]

        for name in ("t", "s", "v", "gmi", "gms", "gxb0", "gxb1", "ldx", "ldp",
                     "stx", "ld1", "ldo", "sto", "gw0", "gw1", "gw2"):
            P.ctr(name)

        PIDX = {n: i for i, n in enumerate(
            ["ln1g", "ln1b", "mixk", "mixv", "mixr", "ln2g", "ln2b",
             "tdr", "tfr", "ew", "etf", "omk", "omv", "omr"])}

        def pv_col(pname, ct):
            i = PIDX[pname] * CT + ct
            return PV[:, i:i + 1]

        def pv_blk(pname):
            i = PIDX[pname] * CT
            return PV[:, i:i + CT]

        bank_free = {}
        pool_i = [0]

        def next_bank(eng):
            b = POOL[pool_i[0] % 8]
            pool_i[0] += 1
            rec = bank_free.get(b.name)
            if rec is not None:
                P.wait(eng, rec[0], rec[1])
            return b

        def bank_guard(eng, b):
            rec = bank_free.get(b.name)
            if rec is not None:
                P.wait(eng, rec[0], rec[1])

        WA = {}
        for wname in ("k", "v", "r", "o"):
            WA[wname] = e(nc.sbuf_tensor(f"WA{wname}", [128, CT * D], BF16))
        XC = e(nc.sbuf_tensor("XC", [128, CT * TCH], F32))
        XB = e(nc.sbuf_tensor("XB", [128, CT * TCH], BF16))
        XN = e(nc.sbuf_tensor("XN", [128, CT * (TCH + 1)], F32))
        XBLK = TCH + 8
        XNb = e(nc.sbuf_tensor("XNb", [128, CT * XBLK], BF16))
        KIN = e(nc.sbuf_tensor("KIN", [128, CT * TCH], BF16))
        VIN = e(nc.sbuf_tensor("VIN", [128, CT * TCH], BF16))
        RIN = e(nc.sbuf_tensor("RIN", [128, CT * TCH], BF16))
        EK = e(nc.sbuf_tensor("EK", [128, CT * TCH], F32))
        VV = e(nc.sbuf_tensor("VV", [128, CT * TCH], F32))
        BB = e(nc.sbuf_tensor("BB", [128, CT * TCH], F32))
        RG = e(nc.sbuf_tensor("RG", [128, CT * TCH], BF16))
        CARA = e(nc.sbuf_tensor("CARA", [128, CT], F32))
        CARB = e(nc.sbuf_tensor("CARB", [128, CT], F32))
        XNB = e(nc.sbuf_tensor("XNB", [128, CT], F32))
        TMPB = [e(nc.sbuf_tensor(f"TMPB{i}", [128, TCH], BF16)) for i in range(2)]
        TMPF = [e(nc.sbuf_tensor(f"TMPF{i}", [128, TCH], F32)) for i in range(1)]
        AAv = XN[:, 0:CT * TCH]
        BBrw = BB[:, :].bitcast(BF16)[:, 0:CT * TCH]
        XNBb = XNB[:, :].bitcast(BF16)

        def aa_blk(ct):
            return XN[:, ct * TCH:(ct + 1) * TCH]

        def xnb_data(ct):
            return XNb[:, ct * XBLK + 8: ct * XBLK + 8 + TCH]

        def xnb_yp(ct):
            return XNb[:, ct * XBLK + 7: ct * XBLK + 7 + TCH]

        for n in VEC_NAMES:
            pid = PIDX[n]
            P.opi("sync", "ldp", lambda E, v=vecs[n], pid=pid: E.dma_start(
                out=sb_ap(PV, pid * CT, [[NPV * CT, 128], [1, CT]]),
                in_=dr_ap(v, 0, [[1, 128], [128, CT]]),
                allow_slow_non_contiguous=True), 16)
        for i, n in enumerate(("s0", "s1", "s2")):
            P.opi("sync", "ldp", lambda E, v=s_in[n], i=i: E.dma_start(
                out=SC[:, i:i + 1], in_=v[:, :]), 16)
        v_init_ld = P.val("ldp")
        P.opi("gpsimd", "gmi", lambda E: E.dma_start(out=ONESB[:], in_=ones_in[:]), 16)
        P.opi("gpsimd", "gms", lambda E: E.memset(ZCOL[:], 0.0), 1)
        P.opi("gpsimd", "gms", lambda E: E.memset(EPST[:, 0:1], LN_EPS), 1)
        P.opi("gpsimd", "gms", lambda E: E.memset(EPST[:, 1:2], WKV_EPS), 1)
        for wname, wap in (("k", wk), ("v", wv), ("r", wr), ("o", wo)):
            P.opi("gpsimd", "gmi", lambda E, wap=wap, wt=WA[wname]: E.dma_start(
                out=sb_ap(wt, 0, [[CT * D, 128], [D, CT], [1, D]]),
                in_=dr_ap(wap, 0, [[D, 128], [128 * D, CT], [1, D]])), 16)
        v_init_g = P.val("gmi")

        P.wait("scalar", "ldp", v_init_ld)
        P.wait("scalar", "gmi", v_init_g)
        P.wait("scalar", "gms", 3)
        P.op("scalar", lambda E: E.activation(pv_blk("ew"), pv_blk("tdr"), AF.Exp))
        P.op("scalar", lambda E: E.activation(pv_blk("ew"), pv_blk("ew"), AF.Exp,
                                              scale=-1.0))
        P.op("scalar", lambda E: E.activation(pv_blk("etf"), pv_blk("tfr"), AF.Exp))
        P.op("scalar", lambda E: E.activation(SC[:, 0:1], SC[:, 0:1], AF.Identity,
                                              scale=4.0))
        P.op("scalar", lambda E: E.activation(SC[:, 1:2], SC[:, 1:2], AF.Identity,
                                              scale=4.0))
        P.op("scalar", lambda E: E.activation(SC[:, 2:3], SC[:, 2:3], AF.Identity,
                                              scale=2.0))
        P.op("scalar", lambda E: E.activation(pv_blk("omk"), pv_blk("mixk"),
                                              AF.Identity, scale=-1.0, bias=1.0))
        P.op("scalar", lambda E: E.activation(pv_blk("omv"), pv_blk("mixv"),
                                              AF.Identity, scale=-1.0, bias=1.0))
        s_params = P.opi("scalar", "s", lambda E: E.activation(
            pv_blk("omr"), pv_blk("mixr"), AF.Identity, scale=-1.0, bias=1.0))

        stat_free = [0, 0]
        stat_par = [0]

        def emit_stats(rhs_bf, t_rhs_waits, s_sq_src, s_sq_waits,
                       out_bf=False, xsq=None):
            sp = stat_par[0] % 2
            stat_par[0] += 1
            M, I, T = STATM[sp], STATI[sp], STATT[sp]
            if out_bf:
                mb = STATM[sp][:, :].bitcast(BF16)
                Mout, Iout = mb[:, 0:TCH], mb[:, TCH:2 * TCH]
            else:
                Mout, Iout = M[:], I[:]
            XSQ = xsq if xsq is not None else VIN
            for cn, cv in s_sq_waits:
                P.wait("scalar", cn, cv)
            src = rhs_bf if s_sq_src is None else s_sq_src
            sqv = P.opi("scalar", "s", lambda E, src=src, XSQ=XSQ: E.activation(
                XSQ[:], src[:], AF.Square))
            S1 = next_bank("tensor")
            for cn, cv in t_rhs_waits:
                P.wait("tensor", cn, cv)
            t_s1 = None
            for ct in range(CT):
                last = ct == CT - 1
                fn = lambda E, ct=ct, last=last, S1=S1: E.matmul(
                    S1[:], ONESB[:], rhs_bf[:, ct * TCH:(ct + 1) * TCH],
                    start=(ct == 0), stop=last)
                if last:
                    t_s1 = P.opi("tensor", "t", fn)
                else:
                    P.op("tensor", fn)
            S2 = next_bank("tensor")
            P.wait("tensor", "s", sqv)
            t_s2 = None
            for ct in range(CT):
                last = ct == CT - 1
                fn = lambda E, ct=ct, last=last, S2=S2, XSQ=XSQ: E.matmul(
                    S2[:], ONESB[:], XSQ[:, ct * TCH:(ct + 1) * TCH],
                    start=(ct == 0), stop=last)
                if last:
                    t_s2 = P.opi("tensor", "t", fn)
                else:
                    P.op("tensor", fn)
            P.wait("scalar", "t", t_s1)
            P.wait("scalar", "v", stat_free[sp])
            P.opi("scalar", "s", lambda E, M=Mout, S1=S1: E.activation(
                M, S1[:], AF.Identity, scale=INV_D))
            s_t2 = P.opi("scalar", "s", lambda E, T=T, S1=S1: E.activation(
                T[:], S1[:], AF.Square, scale=INV_D))
            bank_free[S1.name] = ("s", s_t2)
            P.wait("vector", "t", t_s2)
            P.wait("vector", "s", s_t2)
            v_veps = P.opi("vector", "v", lambda E, I=I, S2=S2, T=T:
                           E.scalar_tensor_tensor(
                               I[:], S2[:], INV_D, T[:], ALU.mult, ALU.subtract))
            bank_free[S2.name] = ("v", v_veps)
            P.wait("scalar", "v", v_veps)
            P.opi("scalar", "s", lambda E, I=I: E.activation(
                I[:], I[:], AF.Ln, bias=EPST[:, 0:1]))
            s_is = P.opi("scalar", "s", lambda E, I=I, Io=Iout: E.activation(
                Io, I[:], AF.Exp, scale=-0.5))
            return s_is, sp, Mout, Iout

        def xn_blk(ct, lo, hi):
            base = ct * (TCH + 1)
            return XN[:, base + lo: base + hi]

        def mm_group(rhs_fn, lhsT_fn, nk):
            bank = next_bank("tensor")
            tg = None
            for ki in range(nk):
                last = ki == nk - 1
                fn = lambda E, ki=ki, bank=bank, last=last: E.matmul(
                    bank[:], lhsT_fn(ki), rhs_fn(ki), start=(ki == 0), stop=last)
                if last:
                    tg = P.opi("tensor", "t", fn)
                else:
                    P.op("tensor", fn)
            return bank, tg

        spill_done = {}
        prev_kvr_t = [0]
        vgrp_t = [0]
        mix_tpar = [0]
        mix_tmp_rd = {0: 0, 1: 0}
        bscan_done = {-1: 0}
        v_rw_d = {-1: 0}
        wo_t = {-1: 0, -2: 0}
        s_ekd = {}
        front_mix = {}
        front_free = {}
        vxc = {}
        P.wait("tensor", "gmi", v_init_g)

        front_stats = {}

        def emit_front_stats(i):
            front_stats[i] = emit_stats(
                XB, [("gxb0", xbload[i])], None,
                [("t", vgrp_t[0]), ("gxb0", xbload[i])],
                out_bf=True, xsq=VIN)

        def emit_front_rest(i):
            b, c = chunks[i]
            s_is, sp, SMb, SIb = front_stats[i]
            pad_dst = bass.AP(XNb.ap().tensor, 7,
                              [[CT * XBLK, 128], [XBLK, CT], [1, 1]])
            if c == 0:
                P.wait("vector", "gmi", v_init_g)
                P.wait("vector", "gms", 3)
                P.op("vector", lambda E, d=pad_dst: E.tensor_copy(
                    d, sb_ap(ZCOL, 0, [[CT, 128], [1, CT], [1, 1]])))
            else:
                P.op("vector", lambda E, d=pad_dst: E.tensor_copy(
                    d, XNBb[:, 0:CT]))
            P.wait("vector", "gxb0", xbload[i])
            P.wait("vector", "s", s_is)
            for ct in range(CT):
                P.op("vector", lambda E, ct=ct, SMb=SMb: E.tensor_tensor(
                    xnb_data(ct), XB[:, ct * TCH:(ct + 1) * TCH],
                    SMb, ALU.subtract))
            v_lnmul = None
            for ct in range(CT):
                fn = lambda E, ct=ct, SIb=SIb: E.tensor_tensor(
                    xnb_data(ct), xnb_data(ct), SIb, ALU.mult)
                v_lnmul = P.opi("vector", "v", fn) if ct == CT - 1 else (
                    P.op("vector", fn) or v_lnmul)
            stat_free[sp] = v_lnmul
            P.wait("scalar", "v", v_lnmul)
            P.wait("scalar", "s", s_params)
            s_gb = None
            for ct in range(CT):
                fn = lambda E, ct=ct: E.activation(
                    xnb_data(ct), xnb_data(ct), AF.Identity,
                    scale=pv_col("ln1g", ct), bias=pv_col("ln1b", ct))
                s_gb = P.opi("scalar", "s", fn) if ct == CT - 1 else (
                    P.op("scalar", fn) or s_gb)
            P.wait("vector", "s", s_gb)
            P.wait("vector", "t", prev_kvr_t[0])
            P.wait("scalar", "t", prev_kvr_t[0])
            v_mix = None
            tpar = mix_tpar
            tmp_rd = mix_tmp_rd
            for mixn, omn, dst in (("mixk", "omk", KIN), ("mixv", "omv", VIN),
                                   ("mixr", "omr", RIN)):
                for ct in range(CT):
                    ti = tpar[0] % 2
                    tpar[0] += 1
                    P.wait("scalar", "v", tmp_rd[ti])
                    sv = P.opi("scalar", "s", lambda E, ct=ct, o=omn, ti=ti:
                               E.activation(TMPB[ti][:], xnb_yp(ct), AF.Identity,
                                            scale=pv_col(o, ct)))
                    P.op("vector", lambda E, ct=ct, m=mixn, d=dst:
                         E.tensor_scalar(d[:, ct * TCH:(ct + 1) * TCH],
                                         xnb_data(ct), pv_col(m, ct), None,
                                         ALU.mult))
                    P.wait("vector", "s", sv)
                    fn = lambda E, ct=ct, d=dst, ti=ti: E.tensor_tensor(
                        d[:, ct * TCH:(ct + 1) * TCH],
                        d[:, ct * TCH:(ct + 1) * TCH], TMPB[ti][:], ALU.add)
                    if mixn == "mixr" and ct == CT - 1:
                        v_mix = P.opi("vector", "v", fn)
                    else:
                        v_mix = P.opi("vector", "v", fn)
                    tmp_rd[ti] = v_mix
            P.op("vector", lambda E: E.tensor_copy(
                XNBb[:, 0:CT],
                bass.AP(XNb.ap().tensor, 8 + TCH - 1,
                        [[CT * XBLK, 128], [XBLK, CT], [1, 1]])))
            front_mix[i] = v_mix
            front_free[i] = v_lnmul

        def emit_kvr(i, mid_cb=None):
            P.wait("tensor", "v", front_mix[i])
            for wname, rhs_t in (("v", VIN), ("r", RIN), ("k", KIN)):
                for mo in range(CT):
                    bank, tg = mm_group(
                        lambda ki, rhs_t=rhs_t: rhs_t[:, ki * TCH:(ki + 1) * TCH],
                        lambda ki, w=WA[wname], mo=mo:
                            w[:, ki * D + mo * 128: ki * D + (mo + 1) * 128],
                        CT)
                    if wname == "k":
                        P.wait("scalar", "t", tg)
                        P.wait("scalar", "v", bscan_done[i - 1])
                        sv = P.opi("scalar", "s", lambda E, mo=mo, bank=bank:
                                   E.activation(EK[:, mo * TCH:(mo + 1) * TCH],
                                                bank[:], AF.Exp))
                        bank_free[bank.name] = ("s", sv)
                    elif wname == "v":
                        P.wait("vector", "t", tg)
                        vv2 = P.opi("vector", "v", lambda E, mo=mo, bank=bank:
                                    E.tensor_copy(VV[:, mo * TCH:(mo + 1) * TCH],
                                                  bank[:]))
                        bank_free[bank.name] = ("v", vv2)
                    else:
                        P.wait("scalar", "t", tg)
                        P.wait("scalar", "v", v_rw_d[i - 1])
                        sv = P.opi("scalar", "s", lambda E, mo=mo, bank=bank:
                                   E.activation(RG[:, mo * TCH:(mo + 1) * TCH],
                                                bank[:], AF.Sigmoid))
                        bank_free[bank.name] = ("s", sv)
                if wname == "v":
                    vgrp_t[0] = P.val("t")
                    if mid_cb is not None:
                        mid_cb()
            prev_kvr_t[0] = P.val("t")
            s_ekd[i] = P.val("s")

        def emit_back(i):
            b, c = chunks[i]
            P.wait("vector", "s", s_ekd[i])
            if c == 0:
                for ct in range(CT):
                    P.op("vector", lambda E, ct=ct: E.tensor_tensor(
                        EK[:, ct * TCH:ct * TCH + 1],
                        EK[:, ct * TCH:ct * TCH + 1],
                        pv_col("etf", ct), ALU.mult))
                P.op("vector", lambda E: E.tensor_copy(CARA[:], ZCOL[:]))
                P.op("vector", lambda E: E.tensor_copy(CARB[:], ZCOL[:]))
            P.opi("vector", "v", lambda E: E.tensor_tensor(
                VV[:], EK[:], VV[:], ALU.mult))
            P.wait("vector", "v")
            P.wait("vector", "t", wo_t[i - 1])
            for ct in range(CT):
                last = ct == CT - 1
                fn = lambda E, ct=ct: E.tensor_tensor_scan(
                    BB[:, ct * TCH:(ct + 1) * TCH],
                    pv_col("ew", ct).broadcast_to([128, TCH]),
                    EK[:, ct * TCH:(ct + 1) * TCH],
                    CARB[:, ct:ct + 1], ALU.mult, ALU.add)
                if last:
                    P.opi("vector", "v", fn)
                else:
                    P.op("vector", fn)
            P.wait("vector", "v")
            v_bs = P.opi("vector", "v", lambda E: E.tensor_copy(
                CARB[:], sb_ap(BB, TCH - 1, [[CT * TCH, 128], [TCH, CT], [1, 1]])))
            for ct in range(CT):
                last = ct == CT - 1
                fn = lambda E, ct=ct: E.tensor_tensor_scan(
                    aa_blk(ct), pv_col("ew", ct).broadcast_to([128, TCH]),
                    VV[:, ct * TCH:(ct + 1) * TCH],
                    CARA[:, ct:ct + 1], ALU.mult, ALU.add)
                if last:
                    P.opi("vector", "v", fn)
                else:
                    P.op("vector", fn)
            P.wait("vector", "v")
            v_car = P.opi("vector", "v", lambda E: E.tensor_copy(
                CARA[:], sb_ap(XN, TCH - 1,
                               [[CT * (TCH + 1), 128], [TCH, CT], [1, 1]])))
            bscan_done[i] = v_car
            P.wait("scalar", "v", v_bs)
            P.opi("scalar", "s", lambda E: E.activation(BB[:], BB[:], AF.Ln,
                                                        bias=EPST[:, 1:2]))
            s_rec = P.opi("scalar", "s", lambda E: E.activation(
                BB[:], BB[:], AF.Exp, scale=-1.0))
            P.wait("vector", "s", s_rec)
            P.opi("vector", "v", lambda E: E.tensor_tensor(AAv, AAv, BB[:],
                                                           ALU.mult))
            v_rw_d[i] = P.opi("vector", "v", lambda E: E.tensor_tensor(
                BBrw, AAv, RG[:], ALU.mult))

        def emit_wo(i):
            b, c = chunks[i]
            P.wait("tensor", "v", v_rw_d[i])
            for mo in range(CT):
                bank, tg = mm_group(
                    lambda ki: BBrw[:, ki * TCH:(ki + 1) * TCH],
                    lambda ki, mo=mo:
                        WA["o"][:, ki * D + mo * 128: ki * D + (mo + 1) * 128],
                    CT)
                P.wait("vector", "t", tg)
                P.wait("vector", "ldx", vxc[i])
                vv2 = P.opi("vector", "v", lambda E, mo=mo, bank=bank:
                            E.tensor_tensor(XC[:, mo * TCH:(mo + 1) * TCH],
                                            XC[:, mo * TCH:(mo + 1) * TCH],
                                            bank[:], ALU.add))
                bank_free[bank.name] = ("v", vv2)
            wo_t[i] = P.val("t")
            v_x1 = P.val("v")
            P.wait("sync", "v", v_x1)
            dst = out if debug_upto in ("A", "TAPS2") else x1sp
            spill_done[i] = P.dma("sync", "stx",
                                  lambda E, b=b, c=c, dst=dst: E.dma_start(
                                      out=dram_chunk(dst, b, c),
                                      in_=sb_ap(XC, 0, SBW)))

        NCH = len(chunks)

        def emit_front_taps():
            P.wait("gpsimd", "v", front_mix[0])
            xnd = bass.AP(XNb.ap().tensor, 8,
                          [[CT * XBLK, 128], [XBLK, CT], [1, TCH]])
            P.opi("gpsimd", "gmi", lambda E: E.dma_start(
                out=dram_chunk(out, 0, 0), in_=xnd), 16)
            P.opi("gpsimd", "gmi", lambda E: E.dma_start(
                out=dram_chunk(out, 0, 1), in_=sb_ap(KIN, 0, SBW)), 16)
            mb = STATM[0][:, :].bitcast(BF16)
            P.opi("gpsimd", "gmi", lambda E: E.dma_start(
                out=dr_ap(out, 2 * TCH, [[L, 128], [1, TCH]]),
                in_=mb[:, 0:TCH]), 16)
            P.opi("gpsimd", "gmi", lambda E: E.dma_start(
                out=dr_ap(out, 3 * TCH, [[L, 128], [1, TCH]]),
                in_=mb[:, TCH:2 * TCH]), 16)
            P.wait("gpsimd", "gmi", P.val("gmi"))

        xbload = {}
        xbload[0] = P.dma("gpsimd", "gxb0", lambda E, b=chunks[0][0],
                          c=chunks[0][1]: E.dma_start(
                              out=sb_ap(XB, 0, SBW), in_=dram_chunk(xt, b, c)))
        emit_front_stats(0)
        emit_front_rest(0)
        if debug_upto == "TAPS":
            emit_front_taps()
        nloop = NCH
        if debug_upto == "TAPS":
            nloop = 0
        elif debug_upto == "TAPS1":
            nloop = 1
        elif debug_upto == "TAPS2":
            nloop = 2
        for i in range(nloop):
            allow_pre = debug_upto not in ("TAPS1", "TAPS2") or \
                (debug_upto == "TAPS2" and i == 0)

            def mid_cb(i=i, allow_pre=allow_pre):
                if i + 1 < NCH and allow_pre:
                    P.wait("gpsimd", "v", front_free[i])
                    xbload[i + 1] = P.dma("gpsimd", "gxb0",
                                          lambda E, b=chunks[i + 1][0],
                                          c=chunks[i + 1][1]: E.dma_start(
                                              out=sb_ap(XB, 0, SBW),
                                              in_=dram_chunk(xt, b, c)))
                    emit_front_stats(i + 1)

            emit_kvr(i, mid_cb)
            if i + 1 < NCH and allow_pre:
                emit_front_rest(i + 1)
            if i >= 1:
                emit_wo(i - 1)
            if i >= 1:
                P.wait("sync", "stx", spill_done[i - 1])
            vxc[i] = P.dma("sync", "ldx", lambda E, b=chunks[i][0],
                           c=chunks[i][1]: E.dma_start(
                               out=sb_ap(XC, 0, SBW),
                               in_=dram_chunk(xt, b, c)))
            emit_back(i)
        if debug_upto == "TAPS1":
            P.wait("sync", "v", v_rw_d[0])
            P.wait("sync", "s", P.val("s"))
            P.opi("sync", "stx", lambda E: E.dma_start(
                out=dram_chunk(out, 0, 0), in_=sb_ap(EK, 0, SBW)), 16)
            P.opi("sync", "stx", lambda E: E.dma_start(
                out=dram_chunk(out, 0, 1), in_=sb_ap(VV, 0, SBW)), 16)
            P.opi("sync", "stx", lambda E: E.dma_start(
                out=dram_chunk(out, 0, 2),
                in_=bass.AP(XN.ap().tensor, 0,
                            [[CT * (TCH + 1), 128], [TCH, CT], [1, TCH]])), 16)
            P.wait("gpsimd", "v", v_rw_d[0])
            P.opi("gpsimd", "gmi", lambda E: E.dma_start(
                out=dram_chunk(out, 0, 3),
                in_=bass.AP(BB.ap().tensor, 0,
                            [[CT * TCH, 128], [TCH, CT], [1, TCH]])), 16)
            P.wait("sync", "stx", P.val("stx"))
            P.wait("gpsimd", "gmi", P.val("gmi"))
        if debug_upto == "TAPS2":
            emit_wo(1)
        if debug_upto not in ("TAPS", "TAPS1", "TAPS2"):
            emit_wo(NCH - 1)

        if debug_upto in ("A", "TAPS", "TAPS1", "TAPS2"):
            P.wait("sync", "stx", P.val("stx"))
        else:
            finals = [("t", P.val("t")), ("s", P.val("s")), ("v", P.val("v")),
                      ("stx", P.val("stx")), ("ldx", P.val("ldx")),
                      ("gxb0", P.val("gxb0")), ("gxb1", P.val("gxb1")),
                      ("gmi", P.val("gmi")), ("gms", P.val("gms"))]
            for eng in ENGS:
                for cn, cv in finals:
                    P.wait(eng, cn, cv)

            X1C = XC
            XOC = BB
            X1B = XB
            ZBB = VIN
            XSQB = VV[:, :].bitcast(BF16)[:, 0:CT * TCH]
            YN = XN
            MIXB = KIN
            YB = VIN
            RRG = RIN
            X3B = RG
            EKb = EK[:, :].bitcast(BF16)
            VVb = VV[:, :].bitcast(BF16)
            YNb = XN[:, 0:CT * TCH].bitcast(BF16)
            WOb = WA["o"][:, :]

            def kk_blk(f):
                base = EKb if f < 16 else VVb
                return base[:, (f % 16) * TCH:((f % 16) + 1) * TCH]

            def g0_blk(f):
                base = YNb if f < 16 else WOb
                return base[:, (f % 16) * TCH:((f % 16) + 1) * TCH]

            g1_blk = kk_blk

            WS = [WA["k"], WA["v"], WA["r"]]

            def yn_blk(ct, lo, hi):
                base = ct * (TCH + 1)
                return YN[:, base + lo: base + hi]

            ws_state = [None, None, None]
            ws_i = [0]
            ws_sems = ["gw0", "gw1", "gw2"]

            def ws_next():
                i = ws_i[0] % 3
                ws_i[0] += 1
                return i

            def ws_load(i, fn):
                if ws_state[i] is not None:
                    P.wait("gpsimd", ws_state[i][0], ws_state[i][1])
                return P.dma("gpsimd", ws_sems[i], fn)

            def ws_lhsT(wi, ki, mo, blk):
                return WS[wi][:, ki * blk + mo * 128: ki * blk + (mo + 1) * 128]

            x1_reuse = {}
            xo_reuse = {}
            out_done = {}
            x1b_reuse = {}
            prev_wfr_t = [0]
            prev_w1_t = [0]
            prev_b9_t = [0]
            prev_wfk_t = [0]
            tmpb_free = [0, 0]
            tmp_i = [0]

            ld1_pre = {}
            gxb_pre = {}
            v_mix_d = {}

            def emit_b1b2(k):
                b, c = chunks[k]
                if k == 0:
                    P.wait("sync", "stx", spill_done[k])
                    v_x1c_d[k] = P.dma("sync", "ld1", lambda E, b=b, c=c:
                                       E.dma_start(out=sb_ap(X1C, 0, SBW),
                                                   in_=dram_chunk(x1sp, b, c)))
                    P.wait("gpsimd", "stx", spill_done[k])
                    v_x1b = P.dma("gpsimd", "gxb0", lambda E, b=b, c=c:
                                  E.dma_start(out=sb_ap(X1B, 0, SBW),
                                              in_=dram_chunk(x1sp, b, c)))
                else:
                    v_x1c_d[k] = ld1_pre[k]
                    v_x1b = gxb_pre[k]
                s_is, sp, SM, SI = emit_stats(X1B, [("gxb0", v_x1b)], None,
                                              [("t", prev_wfk_t[0]),
                                               ("gxb0", v_x1b)],
                                              out_bf=True, xsq=MIXB)
                x1b_reuse[k] = P.val("t")
                pad_dst = bass.AP(XNb.ap().tensor, 7,
                                  [[CT * XBLK, 128], [XBLK, CT], [1, 1]])
                P.wait("vector", "t", prev_wfr_t[0])
                if c == 0:
                    P.op("vector", lambda E, d=pad_dst: E.tensor_copy(
                        d, sb_ap(ZCOL, 0, [[CT, 128], [1, CT], [1, 1]])))
                else:
                    P.op("vector", lambda E, d=pad_dst: E.tensor_copy(
                        d, XNBb[:, 0:CT]))
                P.wait("vector", "gxb0", v_x1b)
                P.wait("vector", "s", s_is)
                for ct in range(CT):
                    P.op("vector", lambda E, ct=ct, SM=SM: E.tensor_tensor(
                        xnb_data(ct), X1B[:, ct * TCH:(ct + 1) * TCH],
                        SM, ALU.subtract))
                v_ln = None
                for ct in range(CT):
                    fn = lambda E, ct=ct, SI=SI: E.tensor_tensor(
                        xnb_data(ct), xnb_data(ct), SI, ALU.mult)
                    v_ln = P.opi("vector", "v", fn) if ct == CT - 1 else (P.op("vector", fn) or v_ln)
                stat_free[sp] = v_ln
                if k + 1 < len(chunks):
                    nb, ncc = chunks[k + 1]
                    P.wait("gpsimd", "t", x1b_reuse[k])
                    P.wait("gpsimd", "v", v_ln)
                    gxb_pre[k + 1] = P.dma(
                        "gpsimd", "gxb0", lambda E, b=nb, c=ncc: E.dma_start(
                            out=sb_ap(X1B, 0, SBW), in_=dram_chunk(x1sp, b, c)))
                P.wait("scalar", "v", v_ln)
                s_yn = None
                for ct in range(CT):
                    fn = lambda E, ct=ct: E.activation(
                        xnb_data(ct), xnb_data(ct), AF.Identity,
                        scale=pv_col("ln2g", ct), bias=pv_col("ln2b", ct))
                    s_yn = P.opi("scalar", "s", fn) if ct == CT - 1 else (P.op("scalar", fn) or s_yn)
                P.wait("vector", "s", s_yn)
                P.wait("vector", "t", prev_wfk_t[0])
                v_mix = None
                for ct in range(CT):
                    fn = lambda E, ct=ct: E.tensor_tensor(
                        MIXB[:, ct * TCH:(ct + 1) * TCH], xnb_data(ct),
                        xnb_yp(ct), ALU.add)
                    v_mix = P.opi("vector", "v", fn) if ct == CT - 1 else (P.op("vector", fn) or v_mix)
                P.op("vector", lambda E: E.tensor_copy(
                    XNBb[:, 0:CT],
                    bass.AP(XNb.ap().tensor, 8 + TCH - 1,
                            [[CT * XBLK, 128], [XBLK, CT], [1, 1]])))
                v_mix_d[k] = v_mix

            v_x1c_d = {}
            emit_b1b2(0)
            for k, (b, c) in enumerate(chunks):
                v_x1c = v_x1c_d[k]
                v_mix = v_mix_d[k]
                if k >= 1:
                    P.wait("sync", "sto", out_done[k - 1])
                    P.wait("sync", "v", xo_reuse[k - 1])
                v_xo = P.dma("sync", "ldo", lambda E, b=b, c=c: E.dma_start(
                    out=sb_ap(XOC, 0, SBW), in_=dram_chunk(xt, b, c)))

                P.wait("tensor", "v", v_mix)
                for g in range(4):
                    wi = ws_next()
                    lv = ws_load(wi, lambda E, g=g, wi=wi: E.dma_start(
                        out=sb_ap(WS[wi], 0, [[CT * D, 128], [1024, CT], [1, 1024]]),
                        in_=dr_ap(wfk, g * 1024,
                                  [[4 * D, 128], [128 * 4 * D, CT], [1, 1024]])))
                    P.wait("tensor", ws_sems[wi], lv)
                    for mo in range(8):
                        moa = g * 8 + mo
                        bank, tg = mm_group(
                            lambda ki: MIXB[:, ki * TCH:(ki + 1) * TCH],
                            lambda ki, wi=wi, mo=mo: ws_lhsT(wi, ki, mo, 1024),
                            CT)
                        ti = tmp_i[0] % 2
                        tmp_i[0] += 1
                        P.wait("scalar", "t", tg)
                        P.wait("scalar", "v", tmpb_free[ti])
                        sv = P.opi("scalar", "s", lambda E, bank=bank, ti=ti:
                                   E.activation(TMPB[ti][:], bank[:], AF.Relu,
                                                scale=0.5))
                        bank_free[bank.name] = ("s", sv)
                        P.wait("vector", "s", sv)
                        vv2 = P.opi("vector", "v", lambda E, moa=moa, ti=ti:
                                    E.tensor_tensor(kk_blk(moa), TMPB[ti][:],
                                                    TMPB[ti][:], ALU.mult))
                        tmpb_free[ti] = vv2
                    ws_state[wi] = ("t", P.val("t"))
                prev_wfk_t[0] = P.val("t")
                v_kk_done = P.val("v")

                wi = ws_next()
                lv = ws_load(wi, lambda E, wi=wi: E.dma_start(
                    out=sb_ap(WS[wi], 0, [[CT * D, 128], [D, CT], [1, D]]),
                    in_=dr_ap(wfr, 0, [[D, 128], [128 * D, CT], [1, D]])))
                P.wait("tensor", ws_sems[wi], lv)
                for mo in range(CT):
                    bank, tg = mm_group(
                        lambda ki: xnb_data(ki),
                        lambda ki, wi=wi, mo=mo: ws_lhsT(wi, ki, mo, 1024),
                        CT)
                    P.wait("scalar", "t", tg)
                    sv = P.opi("scalar", "s", lambda E, mo=mo, bank=bank:
                               E.activation(RRG[:, mo * TCH:(mo + 1) * TCH],
                                            bank[:], AF.Sigmoid))
                    bank_free[bank.name] = ("s", sv)
                ws_state[wi] = ("t", P.val("t"))
                prev_wfr_t[0] = P.val("t")
                s_rr_done = P.val("s")

                P.wait("tensor", "v", v_kk_done)
                P.wait("vector", "ld1", v_x1c)
                for g in range(4):
                    wi = ws_next()
                    lv = ws_load(wi, lambda E, g=g, wi=wi: E.dma_start(
                        out=sb_ap(WS[wi], 0, [[CT * D, 128], [256, 32], [1, 256]]),
                        in_=dr_ap(wfv, g * 256, [[D, 128], [128 * D, 32], [1, 256]])))
                    P.wait("tensor", ws_sems[wi], lv)
                    for mo in range(2):
                        moa = g * 2 + mo
                        bank, tg = mm_group(
                            lambda ki: kk_blk(ki),
                            lambda ki, wi=wi, mo=mo: ws_lhsT(wi, ki, mo, 256),
                            32)
                        P.wait("vector", "t", tg)
                        P.wait("vector", "s", s_rr_done)
                        P.op("vector", lambda E, moa=moa, bank=bank:
                             E.tensor_tensor(TMPF[0][:],
                                             RRG[:, moa * TCH:(moa + 1) * TCH],
                                             bank[:], ALU.mult))
                        vv2 = P.opi("vector", "v", lambda E, moa=moa:
                                    E.tensor_tensor(X1C[:, moa * TCH:(moa + 1) * TCH],
                                                    X1C[:, moa * TCH:(moa + 1) * TCH],
                                                    TMPF[0][:], ALU.add))
                        bank_free[bank.name] = ("v", vv2)
                    ws_state[wi] = ("t", P.val("t"))

                P.wait("vector", "ldo", v_xo)
                v_z = P.opi("vector", "v", lambda E: E.tensor_tensor(
                    X1C[:], X1C[:], XOC[:], ALU.add))
                P.wait("vector", "t", x1b_reuse[k])
                v_zb = P.opi("vector", "v", lambda E: E.tensor_copy(ZBB[:], X1C[:]))
                s_is, sp, SM, SI = emit_stats(ZBB, [("v", v_zb)], X1C,
                                              [("v", v_z)], xsq=XSQB)
                P.wait("vector", "s", s_is)
                for ct in range(CT):
                    P.op("vector", lambda E, ct=ct, SM=SM: E.tensor_tensor(
                        XOC[:, ct * TCH:(ct + 1) * TCH],
                        X1C[:, ct * TCH:(ct + 1) * TCH], SM, ALU.subtract))
                for ct in range(CT):
                    P.op("vector", lambda E, ct=ct, SI=SI: E.tensor_tensor(
                        XOC[:, ct * TCH:(ct + 1) * TCH],
                        XOC[:, ct * TCH:(ct + 1) * TCH], SI, ALU.mult))
                v_x3 = P.opi("vector", "v", lambda E: E.tensor_copy(X3B[:], XOC[:]))
                stat_free[sp] = v_x3
                x1_reuse[k] = v_x3
                if k + 1 < len(chunks):
                    nb, ncc = chunks[k + 1]
                    P.wait("sync", "v", v_x3)
                    ld1_pre[k + 1] = P.dma(
                        "sync", "ld1", lambda E, b=nb, c=ncc: E.dma_start(
                            out=sb_ap(X1C, 0, SBW), in_=dram_chunk(x1sp, b, c)))

                for g in range(4):
                    wi_a = ws_next()
                    lva = ws_load(wi_a, lambda E, g=g, wi=wi_a: E.dma_start(
                        out=sb_ap(WS[wi], 0, [[CT * D, 128], [1024, CT], [1, 1024]]),
                        in_=dr_ap(k0, g * 1024,
                                  [[4 * D, 128], [128 * 4 * D, CT], [1, 1024]])))
                    wi_b = ws_next()
                    lvb = ws_load(wi_b, lambda E, g=g, wi=wi_b: E.dma_start(
                        out=sb_ap(WS[wi], 0, [[CT * D, 128], [1024, CT], [1, 1024]]),
                        in_=dr_ap(k0, D * 4 * D + g * 1024,
                                  [[4 * D, 128], [128 * 4 * D, CT], [1, 1024]])))
                    P.wait("vector", ws_sems[wi_a], lva)
                    P.wait("vector", ws_sems[wi_b], lvb)
                    v_sub = P.opi("vector", "v", lambda E, a=wi_a, bb2=wi_b:
                                  E.tensor_tensor(WS[a][:], WS[a][:],
                                                  WS[bb2][:], ALU.subtract))
                    ws_state[wi_b] = ("v", v_sub)
                    P.wait("tensor", "v", v_sub)
                    P.wait("tensor", "v", v_x3)
                    for mo in range(8):
                        moa = g * 8 + mo
                        bank, tg = mm_group(
                            lambda ki: X3B[:, ki * TCH:(ki + 1) * TCH],
                            lambda ki, wi=wi_a, mo=mo: ws_lhsT(wi, ki, mo, 1024),
                            CT)
                        P.wait("scalar", "t", tg)
                        sv = P.opi("scalar", "s", lambda E, moa=moa, bank=bank:
                                   E.activation(g0_blk(moa), bank[:], AF.Relu,
                                                scale=SC[:, 0:1]))
                        bank_free[bank.name] = ("s", sv)
                    ws_state[wi_a] = ("t", P.val("t"))
                s_g0_done = P.val("s")

                for g in range(16):
                    wi = ws_next()
                    lv = ws_load(wi, lambda E, g=g, wi=wi: E.dma_start(
                        out=sb_ap(WS[wi], 0, [[CT * D, 128], [256, 32], [1, 256]]),
                        in_=dr_ap(k1, g * 256,
                                  [[4 * D, 128], [128 * 4 * D, 32], [1, 256]])))
                    P.wait("vector", ws_sems[wi], lv)
                    v_sub = P.opi("vector", "v", lambda E, wi=wi: E.tensor_scalar(
                        WS[wi][:], WS[wi][:], -0.5, None, ALU.add))
                    P.wait("tensor", "v", v_sub)
                    P.wait("tensor", "s", s_g0_done)
                    for mo in range(2):
                        moa = g * 2 + mo
                        bank, tg = mm_group(
                            lambda ki: g0_blk(ki),
                            lambda ki, wi=wi, mo=mo: ws_lhsT(wi, ki, mo, 256),
                            32)
                        P.wait("scalar", "t", tg)
                        sv = P.opi("scalar", "s", lambda E, moa=moa, bank=bank:
                                   E.activation(g1_blk(moa), bank[:], AF.Relu,
                                                scale=SC[:, 1:2]))
                        bank_free[bank.name] = ("s", sv)
                    ws_state[wi] = ("t", P.val("t"))
                prev_w1_t[0] = P.val("t")
                s_g1_done = P.val("s")

                P.wait("tensor", "s", s_g1_done)
                for g in range(4):
                    wi = ws_next()
                    lv = ws_load(wi, lambda E, g=g, wi=wi: E.dma_start(
                        out=sb_ap(WS[wi], 0, [[CT * D, 128], [256, 32], [1, 256]]),
                        in_=dr_ap(k2, g * 256,
                                  [[D, 128], [128 * D, 32], [1, 256]])))
                    P.wait("vector", ws_sems[wi], lv)
                    v_sub = P.opi("vector", "v", lambda E, wi=wi: E.tensor_scalar(
                        WS[wi][:], WS[wi][:], -0.5, None, ALU.add))
                    P.wait("tensor", "v", v_sub)
                    for mo in range(2):
                        moa = g * 2 + mo
                        bank, tg = mm_group(
                            lambda ki: g1_blk(ki),
                            lambda ki, wi=wi, mo=mo: ws_lhsT(wi, ki, mo, 256),
                            32)
                        P.wait("vector", "t", tg)
                        vv2 = P.opi("vector", "v", lambda E, moa=moa, bank=bank:
                                    E.scalar_tensor_tensor(
                                        XOC[:, moa * TCH:(moa + 1) * TCH],
                                        bank[:], SC[:, 2:3],
                                        XOC[:, moa * TCH:(moa + 1) * TCH],
                                        ALU.mult, ALU.add))
                        bank_free[bank.name] = ("v", vv2)
                    ws_state[wi] = ("t", P.val("t"))
                prev_b9_t[0] = P.val("t")
                v_z2 = P.val("v")

                v_zb2 = P.opi("vector", "v", lambda E: E.tensor_copy(ZBB[:], XOC[:]))
                s_is, sp, SM, SI = emit_stats(ZBB, [("v", v_zb2)], XOC,
                                              [("v", v_z2)], xsq=XSQB)
                P.wait("vector", "s", s_is)
                for ct in range(CT):
                    P.op("vector", lambda E, ct=ct, SM=SM: E.tensor_tensor(
                        XOC[:, ct * TCH:(ct + 1) * TCH],
                        XOC[:, ct * TCH:(ct + 1) * TCH], SM, ALU.subtract))
                v_out = None
                for ct in range(CT):
                    fn = lambda E, ct=ct, SI=SI: E.tensor_tensor(
                        XOC[:, ct * TCH:(ct + 1) * TCH],
                        XOC[:, ct * TCH:(ct + 1) * TCH], SI, ALU.mult)
                    v_out = P.opi("vector", "v", fn) if ct == CT - 1 else (P.op("vector", fn) or v_out)
                stat_free[sp] = v_out
                xo_reuse[k] = v_out
                P.wait("sync", "v", v_out)
                out_done[k] = P.dma("sync", "sto", lambda E, b=b, c=c: E.dma_start(
                    out=dram_chunk(out, b, c), in_=sb_ap(XOC, 0, SBW)))
                if k + 1 < len(chunks):
                    emit_b1b2(k + 1)

            P.wait("sync", "sto", P.val("sto"))

        with nc.Block() as blk:
            @blk.tensor
            def _(E):
                for f in P.ops["tensor"]:
                    f(E)

            @blk.scalar
            def _(E):
                for f in P.ops["scalar"]:
                    f(E)

            @blk.vector
            def _(E):
                for f in P.ops["vector"]:
                    f(E)

            @blk.sync
            def _(E):
                for f in P.ops["sync"]:
                    f(E)

            @blk.gpsimd
            def _(E):
                for f in P.ops["gpsimd"]:
                    f(E)

    return nc


def kernel(L0_attn_x, L0_attn_ln1_g, L0_attn_ln1_b, L0_attn_Wk, L0_attn_Wv,
           L0_attn_Wr, L0_attn_Wo, L0_attn_time_decay, L0_attn_time_first,
           L0_attn_mix_k, L0_attn_mix_v, L0_attn_mix_r, L0_attn_ln2_g,
           L0_attn_ln2_b, L0_attn_Wfk, L0_attn_Wfv, L0_attn_Wfr,
           L0_ffn_k0, L0_ffn_k1, L0_ffn_k2, L0_ffn_s0, L0_ffn_s1, L0_ffn_s2,
           _debug_upto=None):
    x = np.asarray(L0_attn_x, dtype=np.float32)
    xT = np.ascontiguousarray(np.transpose(x, (0, 2, 1)))

    def c32(a):
        return np.ascontiguousarray(np.asarray(a, dtype=np.float32))

    def cu8(a):
        a = np.asarray(a)
        return np.ascontiguousarray(a.view(np.uint8).reshape(a.shape))

    def srep(a):
        return np.ascontiguousarray(
            np.broadcast_to(np.asarray(a, dtype=np.float32).reshape(1, 1), (128, 1)))

    shared = {
        "wk": c32(L0_attn_Wk), "wv": c32(L0_attn_Wv), "wr": c32(L0_attn_Wr),
        "wo": c32(L0_attn_Wo), "wfk": c32(L0_attn_Wfk), "wfv": c32(L0_attn_Wfv),
        "wfr": c32(L0_attn_Wfr),
        "k0": cu8(L0_ffn_k0), "k1": cu8(L0_ffn_k1), "k2": cu8(L0_ffn_k2),
        "ln1g": c32(L0_attn_ln1_g), "ln1b": c32(L0_attn_ln1_b),
        "mixk": c32(L0_attn_mix_k), "mixv": c32(L0_attn_mix_v),
        "mixr": c32(L0_attn_mix_r), "ln2g": c32(L0_attn_ln2_g),
        "ln2b": c32(L0_attn_ln2_b), "tdr": c32(L0_attn_time_decay),
        "tfr": c32(L0_attn_time_first),
        "s0": srep(L0_ffn_s0), "s1": srep(L0_ffn_s1), "s2": srep(L0_ffn_s2),
        "ones": np.ones((128, 128), dtype=np.float32),
    }
    in_maps = []
    for core in range(NCORES):
        m = dict(shared)
        m["xt"] = np.ascontiguousarray(xT[core * BPC:(core + 1) * BPC])
        in_maps.append(m)

    nc = build(debug_upto=_debug_upto)
    res = run_bass_kernel_spmd(nc, in_maps, core_ids=list(range(NCORES)))
    parts = [np.transpose(r["out"], (0, 2, 1)) for r in res.results]
    return np.ascontiguousarray(np.concatenate(parts, axis=0)).astype(np.float32)


# version 24
# speedup vs baseline: 1.0671x; 1.0143x over previous
import numpy as np
from contextlib import ExitStack

import concourse.bass as bass
import concourse.mybir as mybir
from concourse.bass_utils import run_bass_kernel_spmd

F32 = mybir.dt.float32
BF16 = mybir.dt.bfloat16
U8 = mybir.dt.uint8
AF = mybir.ActivationFunctionType
ALU = mybir.AluOpType

B_ALL, L, D = 16, 2048, 1024
NCORES = 8
BPC = B_ALL // NCORES
CT = D // 128
TCH = 512
NCHB = L // TCH
LN_EPS = 1e-5
WKV_EPS = 1e-8
INV_D = 1.0 / D

ENGS = ("tensor", "scalar", "vector", "sync", "gpsimd")


class Prog:

    def __init__(self, nc, es):
        self.nc = nc
        self.ops = {k: [] for k in ENGS}
        self.sems = {}
        self.es = es
        self.last_wait = {}

    def ctr(self, name):
        sem = self.es.enter_context(self.nc.semaphore(name))
        self.sems[name] = [sem, 0]

    def val(self, name):
        return self.sems[name][1]

    def wait(self, eng, name, v=None):
        s, cur = self.sems[name]
        if v is None:
            v = cur
        if v <= 0:
            return
        key = (eng, name)
        if self.last_wait.get(key, -1) >= v:
            return
        self.last_wait[key] = v
        self.ops[eng].append(lambda E, s=s, v=v: E.wait_ge(s, v))

    def op(self, eng, fn):
        self.ops[eng].append(fn)

    def opi(self, eng, name, fn, n=1):
        rec = self.sems[name]
        rec[1] += n
        self.ops[eng].append(lambda E, f=fn, s=rec[0], n=n: f(E).then_inc(s, n))
        return rec[1]

    def dma(self, eng, name, fn):
        self.wait(eng, name)
        return self.opi(eng, name, fn, 16)


def sb_ap(t, offset, dims):
    return bass.AP(t.ap().tensor, offset, dims)


def dr_ap(t, offset, dims):
    return bass.AP(t.tensor, offset, dims)


DBG_SERIAL = False


def build(debug_upto=None):
    nc = bass.Bass("TRN2", target_bir_lowering=False, debug=False,
                   detect_race_conditions=False)

    xt = nc.dram_tensor("xt", [BPC, D, L], F32, kind="ExternalInput").ap()
    wk = nc.dram_tensor("wk", [D, D], F32, kind="ExternalInput").ap()
    wv = nc.dram_tensor("wv", [D, D], F32, kind="ExternalInput").ap()
    wr = nc.dram_tensor("wr", [D, D], F32, kind="ExternalInput").ap()
    wo = nc.dram_tensor("wo", [D, D], F32, kind="ExternalInput").ap()
    wfk = nc.dram_tensor("wfk", [D, 4 * D], F32, kind="ExternalInput").ap()
    wfv = nc.dram_tensor("wfv", [4 * D, D], F32, kind="ExternalInput").ap()
    wfr = nc.dram_tensor("wfr", [D, D], F32, kind="ExternalInput").ap()
    k0 = nc.dram_tensor("k0", [2 * D, 4 * D], U8, kind="ExternalInput").ap()
    k1 = nc.dram_tensor("k1", [4 * D, 4 * D], U8, kind="ExternalInput").ap()
    k2 = nc.dram_tensor("k2", [4 * D, D], U8, kind="ExternalInput").ap()
    VEC_NAMES = ["ln1g", "ln1b", "mixk", "mixv", "mixr", "ln2g", "ln2b", "tdr", "tfr"]
    vecs = {n: nc.dram_tensor(n, [D], F32, kind="ExternalInput").ap() for n in VEC_NAMES}
    s_in = {n: nc.dram_tensor(n, [128, 1], F32, kind="ExternalInput").ap()
            for n in ("s0", "s1", "s2")}
    ones_in = nc.dram_tensor("ones", [128, 128], F32, kind="ExternalInput").ap()
    out = nc.dram_tensor("out", [BPC, D, L], F32, kind="ExternalOutput").ap()
    x1sp = nc.dram_tensor("x1sp", [BPC, D, L], F32, kind="Internal").ap()

    def dram_chunk(ap_t, b, c):
        return dr_ap(ap_t, b * D * L + c * TCH,
                     [[L, 128], [128 * L, CT], [1, TCH]])

    SBW = [[CT * TCH, 128], [TCH, CT], [1, TCH]]

    chunks = [(b, c) for b in range(BPC) for c in range(NCHB)]

    with ExitStack() as es:
        e = es.enter_context
        P = Prog(nc, es)

        NPV = 14
        PV = e(nc.sbuf_tensor("PV", [128, NPV * CT], F32))
        ONESB = e(nc.sbuf_tensor("ONESB", [128, 128], BF16))
        SC = e(nc.sbuf_tensor("SC", [128, 4], F32))
        ZCOL = e(nc.sbuf_tensor("ZCOL", [128, CT], F32))
        STATM = [e(nc.sbuf_tensor(f"STATM{i}", [128, TCH], F32)) for i in range(2)]
        STATI = [e(nc.sbuf_tensor(f"STATI{i}", [128, TCH], F32)) for i in range(2)]
        STATT = [e(nc.sbuf_tensor("STATT", [128, TCH], F32))] * 2
        EPST = e(nc.sbuf_tensor("EPST", [128, 2], F32))

        POOL = [e(nc.psum_tensor(f"PP{i}", [128, TCH], F32)) for i in range(8)]

        for name in ("t", "s", "v", "gmi", "gms", "gxb0", "gxb1", "ldx", "ldp",
                     "stx", "ld1", "ldo", "sto", "gw0", "gw1", "gw2"):
            P.ctr(name)

        PIDX = {n: i for i, n in enumerate(
            ["ln1g", "ln1b", "mixk", "mixv", "mixr", "ln2g", "ln2b",
             "tdr", "tfr", "ew", "etf", "omk", "omv", "omr"])}

        def pv_col(pname, ct):
            i = PIDX[pname] * CT + ct
            return PV[:, i:i + 1]

        def pv_blk(pname):
            i = PIDX[pname] * CT
            return PV[:, i:i + CT]

        bank_free = {}
        pool_i = [0]

        def next_bank(eng):
            b = POOL[pool_i[0] % 8]
            pool_i[0] += 1
            rec = bank_free.get(b.name)
            if rec is not None:
                P.wait(eng, rec[0], rec[1])
            return b

        def bank_guard(eng, b):
            rec = bank_free.get(b.name)
            if rec is not None:
                P.wait(eng, rec[0], rec[1])

        WA = {}
        for wname in ("k", "v", "r", "o"):
            WA[wname] = e(nc.sbuf_tensor(f"WA{wname}", [128, CT * D], BF16))
        XC = e(nc.sbuf_tensor("XC", [128, CT * TCH], F32))
        XB = e(nc.sbuf_tensor("XB", [128, CT * TCH], BF16))
        XN = e(nc.sbuf_tensor("XN", [128, CT * (TCH + 1)], F32))
        XBLK = TCH + 8
        XNb = e(nc.sbuf_tensor("XNb", [128, CT * XBLK], BF16))
        KIN = e(nc.sbuf_tensor("KIN", [128, CT * TCH], BF16))
        VIN = e(nc.sbuf_tensor("VIN", [128, CT * TCH], BF16))
        RIN = e(nc.sbuf_tensor("RIN", [128, CT * TCH], BF16))
        EK = e(nc.sbuf_tensor("EK", [128, CT * TCH], F32))
        VV = e(nc.sbuf_tensor("VV", [128, CT * TCH], F32))
        BB = e(nc.sbuf_tensor("BB", [128, CT * TCH], F32))
        RG = e(nc.sbuf_tensor("RG", [128, CT * TCH], BF16))
        CARA = e(nc.sbuf_tensor("CARA", [128, CT], F32))
        CARB = e(nc.sbuf_tensor("CARB", [128, CT], F32))
        XNB = e(nc.sbuf_tensor("XNB", [128, CT], F32))
        TMPB = [e(nc.sbuf_tensor(f"TMPB{i}", [128, TCH], BF16)) for i in range(2)]
        TMPF = [e(nc.sbuf_tensor(f"TMPF{i}", [128, TCH], F32)) for i in range(1)]
        AAv = XN[:, 0:CT * TCH]
        BBrw = BB[:, :].bitcast(BF16)[:, 0:CT * TCH]
        XNBb = XNB[:, :].bitcast(BF16)

        def aa_blk(ct):
            return XN[:, ct * TCH:(ct + 1) * TCH]

        def xnb_data(ct):
            return XNb[:, ct * XBLK + 8: ct * XBLK + 8 + TCH]

        def xnb_yp(ct):
            return XNb[:, ct * XBLK + 7: ct * XBLK + 7 + TCH]

        for n in VEC_NAMES:
            pid = PIDX[n]
            P.opi("sync", "ldp", lambda E, v=vecs[n], pid=pid: E.dma_start(
                out=sb_ap(PV, pid * CT, [[NPV * CT, 128], [1, CT]]),
                in_=dr_ap(v, 0, [[1, 128], [128, CT]]),
                allow_slow_non_contiguous=True), 16)
        for i, n in enumerate(("s0", "s1", "s2")):
            P.opi("sync", "ldp", lambda E, v=s_in[n], i=i: E.dma_start(
                out=SC[:, i:i + 1], in_=v[:, :]), 16)
        v_init_ld = P.val("ldp")
        P.opi("gpsimd", "gmi", lambda E: E.dma_start(out=ONESB[:], in_=ones_in[:]), 16)
        P.opi("gpsimd", "gms", lambda E: E.memset(ZCOL[:], 0.0), 1)
        P.opi("gpsimd", "gms", lambda E: E.memset(EPST[:, 0:1], LN_EPS), 1)
        P.opi("gpsimd", "gms", lambda E: E.memset(EPST[:, 1:2], WKV_EPS), 1)
        for wname, wap in (("k", wk), ("v", wv), ("r", wr), ("o", wo)):
            P.opi("gpsimd", "gmi", lambda E, wap=wap, wt=WA[wname]: E.dma_start(
                out=sb_ap(wt, 0, [[CT * D, 128], [D, CT], [1, D]]),
                in_=dr_ap(wap, 0, [[D, 128], [128 * D, CT], [1, D]])), 16)
        v_init_g = P.val("gmi")

        P.wait("scalar", "ldp", v_init_ld)
        P.wait("scalar", "gmi", v_init_g)
        P.wait("scalar", "gms", 3)
        P.op("scalar", lambda E: E.activation(pv_blk("ew"), pv_blk("tdr"), AF.Exp))
        P.op("scalar", lambda E: E.activation(pv_blk("ew"), pv_blk("ew"), AF.Exp,
                                              scale=-1.0))
        P.op("scalar", lambda E: E.activation(pv_blk("etf"), pv_blk("tfr"), AF.Exp))
        P.op("scalar", lambda E: E.activation(SC[:, 0:1], SC[:, 0:1], AF.Identity,
                                              scale=4.0))
        P.op("scalar", lambda E: E.activation(SC[:, 1:2], SC[:, 1:2], AF.Identity,
                                              scale=4.0))
        P.op("scalar", lambda E: E.activation(SC[:, 2:3], SC[:, 2:3], AF.Identity,
                                              scale=2.0))
        P.op("scalar", lambda E: E.activation(pv_blk("omk"), pv_blk("mixk"),
                                              AF.Identity, scale=-1.0, bias=1.0))
        P.op("scalar", lambda E: E.activation(pv_blk("omv"), pv_blk("mixv"),
                                              AF.Identity, scale=-1.0, bias=1.0))
        s_params = P.opi("scalar", "s", lambda E: E.activation(
            pv_blk("omr"), pv_blk("mixr"), AF.Identity, scale=-1.0, bias=1.0))

        stat_free = [0, 0]
        stat_par = [0]

        def emit_stats(rhs_bf, t_rhs_waits, s_sq_src, s_sq_waits,
                       out_bf=False, xsq=None):
            sp = stat_par[0] % 2
            stat_par[0] += 1
            M, I, T = STATM[sp], STATI[sp], STATT[sp]
            if out_bf:
                mb = STATM[sp][:, :].bitcast(BF16)
                Mout, Iout = mb[:, 0:TCH], mb[:, TCH:2 * TCH]
            else:
                Mout, Iout = M[:], I[:]
            XSQ = xsq if xsq is not None else VIN
            for cn, cv in s_sq_waits:
                P.wait("scalar", cn, cv)
            src = rhs_bf if s_sq_src is None else s_sq_src
            sqv = P.opi("scalar", "s", lambda E, src=src, XSQ=XSQ: E.activation(
                XSQ[:], src[:], AF.Square))
            S1 = next_bank("tensor")
            for cn, cv in t_rhs_waits:
                P.wait("tensor", cn, cv)
            t_s1 = None
            for ct in range(CT):
                last = ct == CT - 1
                fn = lambda E, ct=ct, last=last, S1=S1: E.matmul(
                    S1[:], ONESB[:], rhs_bf[:, ct * TCH:(ct + 1) * TCH],
                    start=(ct == 0), stop=last)
                if last:
                    t_s1 = P.opi("tensor", "t", fn)
                else:
                    P.op("tensor", fn)
            S2 = next_bank("tensor")
            P.wait("tensor", "s", sqv)
            t_s2 = None
            for ct in range(CT):
                last = ct == CT - 1
                fn = lambda E, ct=ct, last=last, S2=S2, XSQ=XSQ: E.matmul(
                    S2[:], ONESB[:], XSQ[:, ct * TCH:(ct + 1) * TCH],
                    start=(ct == 0), stop=last)
                if last:
                    t_s2 = P.opi("tensor", "t", fn)
                else:
                    P.op("tensor", fn)
            P.wait("scalar", "t", t_s1)
            P.wait("scalar", "v", stat_free[sp])
            P.opi("scalar", "s", lambda E, M=Mout, S1=S1: E.activation(
                M, S1[:], AF.Identity, scale=INV_D))
            s_t2 = P.opi("scalar", "s", lambda E, T=T, S1=S1: E.activation(
                T[:], S1[:], AF.Square, scale=INV_D))
            bank_free[S1.name] = ("s", s_t2)
            P.wait("vector", "t", t_s2)
            P.wait("vector", "s", s_t2)
            v_veps = P.opi("vector", "v", lambda E, I=I, S2=S2, T=T:
                           E.scalar_tensor_tensor(
                               I[:], S2[:], INV_D, T[:], ALU.mult, ALU.subtract))
            bank_free[S2.name] = ("v", v_veps)
            P.wait("scalar", "v", v_veps)
            P.opi("scalar", "s", lambda E, I=I: E.activation(
                I[:], I[:], AF.Ln, bias=EPST[:, 0:1]))
            s_is = P.opi("scalar", "s", lambda E, I=I, Io=Iout: E.activation(
                Io, I[:], AF.Exp, scale=-0.5))
            return s_is, sp, Mout, Iout

        def xn_blk(ct, lo, hi):
            base = ct * (TCH + 1)
            return XN[:, base + lo: base + hi]

        def mm_group(rhs_fn, lhsT_fn, nk):
            bank = next_bank("tensor")
            tg = None
            for ki in range(nk):
                last = ki == nk - 1
                fn = lambda E, ki=ki, bank=bank, last=last: E.matmul(
                    bank[:], lhsT_fn(ki), rhs_fn(ki), start=(ki == 0), stop=last)
                if last:
                    tg = P.opi("tensor", "t", fn)
                else:
                    P.op("tensor", fn)
            return bank, tg

        spill_done = {}
        prev_kvr_t = [0]
        vgrp_t = [0]
        mix_tpar = [0]
        mix_tmp_rd = {0: 0, 1: 0}
        bscan_done = {-1: 0}
        v_rw_d = {-1: 0}
        wo_t = {-1: 0, -2: 0}
        s_ekd = {}
        front_mix = {}
        front_free = {}
        vxc = {}
        P.wait("tensor", "gmi", v_init_g)

        front_stats = {}

        def emit_front_stats(i):
            front_stats[i] = emit_stats(
                XB, [("gxb0", xbload[i])], None,
                [("t", vgrp_t[0]), ("gxb0", xbload[i])],
                out_bf=True, xsq=VIN)

        def emit_front_rest(i):
            b, c = chunks[i]
            s_is, sp, SMb, SIb = front_stats[i]
            pad_dst = bass.AP(XNb.ap().tensor, 7,
                              [[CT * XBLK, 128], [XBLK, CT], [1, 1]])
            if c == 0:
                P.wait("vector", "gmi", v_init_g)
                P.wait("vector", "gms", 3)
                P.op("vector", lambda E, d=pad_dst: E.tensor_copy(
                    d, sb_ap(ZCOL, 0, [[CT, 128], [1, CT], [1, 1]])))
            else:
                P.op("vector", lambda E, d=pad_dst: E.tensor_copy(
                    d, XNBb[:, 0:CT]))
            P.wait("vector", "gxb0", xbload[i])
            P.wait("vector", "s", s_is)
            for ct in range(CT):
                P.op("vector", lambda E, ct=ct, SMb=SMb: E.tensor_tensor(
                    xnb_data(ct), XB[:, ct * TCH:(ct + 1) * TCH],
                    SMb, ALU.subtract))
            v_lnmul = None
            for ct in range(CT):
                fn = lambda E, ct=ct, SIb=SIb: E.tensor_tensor(
                    xnb_data(ct), xnb_data(ct), SIb, ALU.mult)
                v_lnmul = P.opi("vector", "v", fn) if ct == CT - 1 else (
                    P.op("vector", fn) or v_lnmul)
            stat_free[sp] = v_lnmul
            P.wait("scalar", "v", v_lnmul)
            P.wait("scalar", "s", s_params)
            s_gb = None
            for ct in range(CT):
                fn = lambda E, ct=ct: E.activation(
                    xnb_data(ct), xnb_data(ct), AF.Identity,
                    scale=pv_col("ln1g", ct), bias=pv_col("ln1b", ct))
                s_gb = P.opi("scalar", "s", fn) if ct == CT - 1 else (
                    P.op("scalar", fn) or s_gb)
            P.wait("vector", "s", s_gb)
            P.wait("vector", "t", prev_kvr_t[0])
            P.wait("scalar", "t", prev_kvr_t[0])
            v_mix = None
            tpar = mix_tpar
            tmp_rd = mix_tmp_rd
            for mixn, omn, dst in (("mixk", "omk", KIN), ("mixv", "omv", VIN),
                                   ("mixr", "omr", RIN)):
                for ct in range(CT):
                    ti = tpar[0] % 2
                    tpar[0] += 1
                    P.wait("scalar", "v", tmp_rd[ti])
                    sv = P.opi("scalar", "s", lambda E, ct=ct, o=omn, ti=ti:
                               E.activation(TMPB[ti][:], xnb_yp(ct), AF.Identity,
                                            scale=pv_col(o, ct)))
                    P.op("vector", lambda E, ct=ct, m=mixn, d=dst:
                         E.tensor_scalar(d[:, ct * TCH:(ct + 1) * TCH],
                                         xnb_data(ct), pv_col(m, ct), None,
                                         ALU.mult))
                    P.wait("vector", "s", sv)
                    fn = lambda E, ct=ct, d=dst, ti=ti: E.tensor_tensor(
                        d[:, ct * TCH:(ct + 1) * TCH],
                        d[:, ct * TCH:(ct + 1) * TCH], TMPB[ti][:], ALU.add)
                    if mixn == "mixr" and ct == CT - 1:
                        v_mix = P.opi("vector", "v", fn)
                    else:
                        v_mix = P.opi("vector", "v", fn)
                    tmp_rd[ti] = v_mix
            P.op("vector", lambda E: E.tensor_copy(
                XNBb[:, 0:CT],
                bass.AP(XNb.ap().tensor, 8 + TCH - 1,
                        [[CT * XBLK, 128], [XBLK, CT], [1, 1]])))
            front_mix[i] = v_mix
            front_free[i] = v_lnmul

        def emit_kvr(i, mid_cb=None):
            P.wait("tensor", "v", front_mix[i])
            for wname, rhs_t in (("v", VIN), ("r", RIN), ("k", KIN)):
                for mo in range(CT):
                    bank, tg = mm_group(
                        lambda ki, rhs_t=rhs_t: rhs_t[:, ki * TCH:(ki + 1) * TCH],
                        lambda ki, w=WA[wname], mo=mo:
                            w[:, ki * D + mo * 128: ki * D + (mo + 1) * 128],
                        CT)
                    if wname == "k":
                        P.wait("scalar", "t", tg)
                        P.wait("scalar", "v", bscan_done[i - 1])
                        sv = P.opi("scalar", "s", lambda E, mo=mo, bank=bank:
                                   E.activation(EK[:, mo * TCH:(mo + 1) * TCH],
                                                bank[:], AF.Exp))
                        bank_free[bank.name] = ("s", sv)
                    elif wname == "v":
                        P.wait("vector", "t", tg)
                        vv2 = P.opi("vector", "v", lambda E, mo=mo, bank=bank:
                                    E.tensor_copy(VV[:, mo * TCH:(mo + 1) * TCH],
                                                  bank[:]))
                        bank_free[bank.name] = ("v", vv2)
                    else:
                        P.wait("scalar", "t", tg)
                        P.wait("scalar", "v", v_rw_d[i - 1])
                        sv = P.opi("scalar", "s", lambda E, mo=mo, bank=bank:
                                   E.activation(RG[:, mo * TCH:(mo + 1) * TCH],
                                                bank[:], AF.Sigmoid))
                        bank_free[bank.name] = ("s", sv)
                if wname == "v":
                    vgrp_t[0] = P.val("t")
                    if mid_cb is not None:
                        mid_cb()
            prev_kvr_t[0] = P.val("t")
            s_ekd[i] = P.val("s")

        def emit_back(i):
            b, c = chunks[i]
            P.wait("vector", "s", s_ekd[i])
            if c == 0:
                for ct in range(CT):
                    P.op("vector", lambda E, ct=ct: E.tensor_tensor(
                        EK[:, ct * TCH:ct * TCH + 1],
                        EK[:, ct * TCH:ct * TCH + 1],
                        pv_col("etf", ct), ALU.mult))
                P.op("vector", lambda E: E.tensor_copy(CARA[:], ZCOL[:]))
                P.op("vector", lambda E: E.tensor_copy(CARB[:], ZCOL[:]))
            P.opi("vector", "v", lambda E: E.tensor_tensor(
                VV[:], EK[:], VV[:], ALU.mult))
            P.wait("vector", "v")
            P.wait("vector", "t", wo_t[i - 1])
            for ct in range(CT):
                last = ct == CT - 1
                fn = lambda E, ct=ct: E.tensor_tensor_scan(
                    BB[:, ct * TCH:(ct + 1) * TCH],
                    pv_col("ew", ct).broadcast_to([128, TCH]),
                    EK[:, ct * TCH:(ct + 1) * TCH],
                    CARB[:, ct:ct + 1], ALU.mult, ALU.add)
                if last:
                    P.opi("vector", "v", fn)
                else:
                    P.op("vector", fn)
            P.wait("vector", "v")
            v_bs = P.opi("vector", "v", lambda E: E.tensor_copy(
                CARB[:], sb_ap(BB, TCH - 1, [[CT * TCH, 128], [TCH, CT], [1, 1]])))
            for ct in range(CT):
                last = ct == CT - 1
                fn = lambda E, ct=ct: E.tensor_tensor_scan(
                    aa_blk(ct), pv_col("ew", ct).broadcast_to([128, TCH]),
                    VV[:, ct * TCH:(ct + 1) * TCH],
                    CARA[:, ct:ct + 1], ALU.mult, ALU.add)
                if last:
                    P.opi("vector", "v", fn)
                else:
                    P.op("vector", fn)
            P.wait("vector", "v")
            v_car = P.opi("vector", "v", lambda E: E.tensor_copy(
                CARA[:], sb_ap(XN, TCH - 1,
                               [[CT * (TCH + 1), 128], [TCH, CT], [1, 1]])))
            bscan_done[i] = v_car
            P.wait("scalar", "v", v_bs)
            P.opi("scalar", "s", lambda E: E.activation(BB[:], BB[:], AF.Ln,
                                                        bias=EPST[:, 1:2]))
            s_rec = P.opi("scalar", "s", lambda E: E.activation(
                BB[:], BB[:], AF.Exp, scale=-1.0))
            P.wait("vector", "s", s_rec)
            P.opi("vector", "v", lambda E: E.tensor_tensor(AAv, AAv, BB[:],
                                                           ALU.mult))
            v_rw_d[i] = P.opi("vector", "v", lambda E: E.tensor_tensor(
                BBrw, AAv, RG[:], ALU.mult))

        def emit_wo(i):
            b, c = chunks[i]
            P.wait("tensor", "v", v_rw_d[i])
            for mo in range(CT):
                bank, tg = mm_group(
                    lambda ki: BBrw[:, ki * TCH:(ki + 1) * TCH],
                    lambda ki, mo=mo:
                        WA["o"][:, ki * D + mo * 128: ki * D + (mo + 1) * 128],
                    CT)
                P.wait("vector", "t", tg)
                P.wait("vector", "ldx", vxc[i])
                vv2 = P.opi("vector", "v", lambda E, mo=mo, bank=bank:
                            E.tensor_tensor(XC[:, mo * TCH:(mo + 1) * TCH],
                                            XC[:, mo * TCH:(mo + 1) * TCH],
                                            bank[:], ALU.add))
                bank_free[bank.name] = ("v", vv2)
            wo_t[i] = P.val("t")
            v_x1 = P.val("v")
            P.wait("sync", "v", v_x1)
            dst = out if debug_upto in ("A", "TAPS2") else x1sp
            spill_done[i] = P.dma("sync", "stx",
                                  lambda E, b=b, c=c, dst=dst: E.dma_start(
                                      out=dram_chunk(dst, b, c),
                                      in_=sb_ap(XC, 0, SBW)))

        NCH = len(chunks)

        def emit_front_taps():
            P.wait("gpsimd", "v", front_mix[0])
            xnd = bass.AP(XNb.ap().tensor, 8,
                          [[CT * XBLK, 128], [XBLK, CT], [1, TCH]])
            P.opi("gpsimd", "gmi", lambda E: E.dma_start(
                out=dram_chunk(out, 0, 0), in_=xnd), 16)
            P.opi("gpsimd", "gmi", lambda E: E.dma_start(
                out=dram_chunk(out, 0, 1), in_=sb_ap(KIN, 0, SBW)), 16)
            mb = STATM[0][:, :].bitcast(BF16)
            P.opi("gpsimd", "gmi", lambda E: E.dma_start(
                out=dr_ap(out, 2 * TCH, [[L, 128], [1, TCH]]),
                in_=mb[:, 0:TCH]), 16)
            P.opi("gpsimd", "gmi", lambda E: E.dma_start(
                out=dr_ap(out, 3 * TCH, [[L, 128], [1, TCH]]),
                in_=mb[:, TCH:2 * TCH]), 16)
            P.wait("gpsimd", "gmi", P.val("gmi"))

        xbload = {}
        xbload[0] = P.dma("gpsimd", "gxb0", lambda E, b=chunks[0][0],
                          c=chunks[0][1]: E.dma_start(
                              out=sb_ap(XB, 0, SBW), in_=dram_chunk(xt, b, c)))
        emit_front_stats(0)
        emit_front_rest(0)
        if debug_upto == "TAPS":
            emit_front_taps()
        nloop = NCH
        if debug_upto == "TAPS":
            nloop = 0
        elif debug_upto == "TAPS1":
            nloop = 1
        elif debug_upto == "TAPS2":
            nloop = 2
        for i in range(nloop):
            allow_pre = debug_upto not in ("TAPS1", "TAPS2") or \
                (debug_upto == "TAPS2" and i == 0)

            def mid_cb(i=i, allow_pre=allow_pre):
                if i + 1 < NCH and allow_pre:
                    P.wait("gpsimd", "v", front_free[i])
                    xbload[i + 1] = P.dma("gpsimd", "gxb0",
                                          lambda E, b=chunks[i + 1][0],
                                          c=chunks[i + 1][1]: E.dma_start(
                                              out=sb_ap(XB, 0, SBW),
                                              in_=dram_chunk(xt, b, c)))
                    emit_front_stats(i + 1)

            emit_kvr(i, mid_cb)
            if i + 1 < NCH and allow_pre:
                emit_front_rest(i + 1)
            if i >= 1:
                emit_wo(i - 1)
            if i >= 1:
                P.wait("sync", "stx", spill_done[i - 1])
            vxc[i] = P.dma("sync", "ldx", lambda E, b=chunks[i][0],
                           c=chunks[i][1]: E.dma_start(
                               out=sb_ap(XC, 0, SBW),
                               in_=dram_chunk(xt, b, c)))
            emit_back(i)
        if debug_upto == "TAPS1":
            P.wait("sync", "v", v_rw_d[0])
            P.wait("sync", "s", P.val("s"))
            P.opi("sync", "stx", lambda E: E.dma_start(
                out=dram_chunk(out, 0, 0), in_=sb_ap(EK, 0, SBW)), 16)
            P.opi("sync", "stx", lambda E: E.dma_start(
                out=dram_chunk(out, 0, 1), in_=sb_ap(VV, 0, SBW)), 16)
            P.opi("sync", "stx", lambda E: E.dma_start(
                out=dram_chunk(out, 0, 2),
                in_=bass.AP(XN.ap().tensor, 0,
                            [[CT * (TCH + 1), 128], [TCH, CT], [1, TCH]])), 16)
            P.wait("gpsimd", "v", v_rw_d[0])
            P.opi("gpsimd", "gmi", lambda E: E.dma_start(
                out=dram_chunk(out, 0, 3),
                in_=bass.AP(BB.ap().tensor, 0,
                            [[CT * TCH, 128], [TCH, CT], [1, TCH]])), 16)
            P.wait("sync", "stx", P.val("stx"))
            P.wait("gpsimd", "gmi", P.val("gmi"))
        if debug_upto == "TAPS2":
            emit_wo(1)
        if debug_upto not in ("TAPS", "TAPS1", "TAPS2"):
            emit_wo(NCH - 1)

        if debug_upto in ("A", "TAPS", "TAPS1", "TAPS2"):
            P.wait("sync", "stx", P.val("stx"))
        else:
            finals = [("t", P.val("t")), ("s", P.val("s")), ("v", P.val("v")),
                      ("stx", P.val("stx")), ("ldx", P.val("ldx")),
                      ("gxb0", P.val("gxb0")), ("gxb1", P.val("gxb1")),
                      ("gmi", P.val("gmi")), ("gms", P.val("gms"))]
            for eng in ENGS:
                for cn, cv in finals:
                    P.wait(eng, cn, cv)

            X1C = XC
            XOC = BB
            X1B = XB
            ZBB = VIN
            XSQB = VV[:, :].bitcast(BF16)[:, 0:CT * TCH]
            YN = XN
            MIXB = KIN
            YB = VIN
            RRG = RIN
            X3B = RG
            EKb = EK[:, :].bitcast(BF16)
            VVb = VV[:, :].bitcast(BF16)
            YNb = XN[:, 0:CT * TCH].bitcast(BF16)
            WOb = WA["o"][:, :]

            def kk_blk(f):
                base = EKb if f < 16 else VVb
                return base[:, (f % 16) * TCH:((f % 16) + 1) * TCH]

            def g0_blk(f):
                base = YNb if f < 16 else WOb
                return base[:, (f % 16) * TCH:((f % 16) + 1) * TCH]

            g1_blk = kk_blk

            WS = [WA["k"], WA["v"], WA["r"]]

            def yn_blk(ct, lo, hi):
                base = ct * (TCH + 1)
                return YN[:, base + lo: base + hi]

            ws_state = [None, None, None]
            ws_i = [0]
            ws_sems = ["gw0", "gw1", "gw2"]

            def ws_next():
                i = ws_i[0] % 3
                ws_i[0] += 1
                return i

            def ws_load(i, fn):
                if ws_state[i] is not None:
                    P.wait("gpsimd", ws_state[i][0], ws_state[i][1])
                return P.dma("gpsimd", ws_sems[i], fn)

            def ws_lhsT(wi, ki, mo, blk):
                return WS[wi][:, ki * blk + mo * 128: ki * blk + (mo + 1) * 128]

            x1_reuse = {}
            xo_reuse = {}
            out_done = {}
            x1b_reuse = {}
            prev_wfr_t = [0]
            prev_w1_t = [0]
            prev_b9_t = [0]
            prev_wfk_t = [0]
            tmpb_free = [0, 0]
            tmp_i = [0]

            ld1_pre = {}
            gxb_pre = {}
            v_mix_d = {}

            def emit_b1b2(k):
                b, c = chunks[k]
                if k == 0:
                    P.wait("sync", "stx", spill_done[k])
                    v_x1c_d[k] = P.dma("sync", "ld1", lambda E, b=b, c=c:
                                       E.dma_start(out=sb_ap(X1C, 0, SBW),
                                                   in_=dram_chunk(x1sp, b, c)))
                    P.wait("gpsimd", "stx", spill_done[k])
                    v_x1b = P.dma("gpsimd", "gxb0", lambda E, b=b, c=c:
                                  E.dma_start(out=sb_ap(X1B, 0, SBW),
                                              in_=dram_chunk(x1sp, b, c)))
                else:
                    v_x1c_d[k] = ld1_pre[k]
                    v_x1b = gxb_pre[k]
                s_is, sp, SM, SI = emit_stats(X1B, [("gxb0", v_x1b)], None,
                                              [("t", prev_wfk_t[0]),
                                               ("gxb0", v_x1b)],
                                              out_bf=True, xsq=MIXB)
                x1b_reuse[k] = P.val("t")
                pad_dst = bass.AP(XNb.ap().tensor, 7,
                                  [[CT * XBLK, 128], [XBLK, CT], [1, 1]])
                P.wait("vector", "t", prev_wfr_t[0])
                if c == 0:
                    P.op("vector", lambda E, d=pad_dst: E.tensor_copy(
                        d, sb_ap(ZCOL, 0, [[CT, 128], [1, CT], [1, 1]])))
                else:
                    P.op("vector", lambda E, d=pad_dst: E.tensor_copy(
                        d, XNBb[:, 0:CT]))
                P.wait("vector", "gxb0", v_x1b)
                P.wait("vector", "s", s_is)
                for ct in range(CT):
                    P.op("vector", lambda E, ct=ct, SM=SM: E.tensor_tensor(
                        xnb_data(ct), X1B[:, ct * TCH:(ct + 1) * TCH],
                        SM, ALU.subtract))
                v_ln = None
                for ct in range(CT):
                    fn = lambda E, ct=ct, SI=SI: E.tensor_tensor(
                        xnb_data(ct), xnb_data(ct), SI, ALU.mult)
                    v_ln = P.opi("vector", "v", fn) if ct == CT - 1 else (P.op("vector", fn) or v_ln)
                stat_free[sp] = v_ln
                if k + 1 < len(chunks):
                    nb, ncc = chunks[k + 1]
                    P.wait("gpsimd", "t", x1b_reuse[k])
                    P.wait("gpsimd", "v", v_ln)
                    gxb_pre[k + 1] = P.dma(
                        "gpsimd", "gxb0", lambda E, b=nb, c=ncc: E.dma_start(
                            out=sb_ap(X1B, 0, SBW), in_=dram_chunk(x1sp, b, c)))
                P.wait("scalar", "v", v_ln)
                s_yn = None
                for ct in range(CT):
                    fn = lambda E, ct=ct: E.activation(
                        xnb_data(ct), xnb_data(ct), AF.Identity,
                        scale=pv_col("ln2g", ct), bias=pv_col("ln2b", ct))
                    s_yn = P.opi("scalar", "s", fn) if ct == CT - 1 else (P.op("scalar", fn) or s_yn)
                P.wait("vector", "s", s_yn)
                P.wait("vector", "t", prev_wfk_t[0])
                v_mix = None
                for ct in range(CT):
                    fn = lambda E, ct=ct: E.tensor_tensor(
                        MIXB[:, ct * TCH:(ct + 1) * TCH], xnb_data(ct),
                        xnb_yp(ct), ALU.add)
                    v_mix = P.opi("vector", "v", fn) if ct == CT - 1 else (P.op("vector", fn) or v_mix)
                P.op("vector", lambda E: E.tensor_copy(
                    XNBb[:, 0:CT],
                    bass.AP(XNb.ap().tensor, 8 + TCH - 1,
                            [[CT * XBLK, 128], [XBLK, CT], [1, 1]])))
                v_mix_d[k] = v_mix

            v_x1c_d = {}
            emit_b1b2(0)
            for k, (b, c) in enumerate(chunks):
                v_x1c = v_x1c_d[k]
                v_mix = v_mix_d[k]
                if k >= 1:
                    P.wait("sync", "sto", out_done[k - 1])
                    P.wait("sync", "v", xo_reuse[k - 1])
                v_xo = P.dma("sync", "ldo", lambda E, b=b, c=c: E.dma_start(
                    out=sb_ap(XOC, 0, SBW), in_=dram_chunk(xt, b, c)))

                P.wait("tensor", "v", v_mix)
                for g in range(4):
                    wi = ws_next()
                    lv = ws_load(wi, lambda E, g=g, wi=wi: E.dma_start(
                        out=sb_ap(WS[wi], 0, [[CT * D, 128], [1024, CT], [1, 1024]]),
                        in_=dr_ap(wfk, g * 1024,
                                  [[4 * D, 128], [128 * 4 * D, CT], [1, 1024]])))
                    P.wait("tensor", ws_sems[wi], lv)
                    for mo in range(8):
                        moa = g * 8 + mo
                        bank, tg = mm_group(
                            lambda ki: MIXB[:, ki * TCH:(ki + 1) * TCH],
                            lambda ki, wi=wi, mo=mo: ws_lhsT(wi, ki, mo, 1024),
                            CT)
                        ti = tmp_i[0] % 2
                        tmp_i[0] += 1
                        P.wait("scalar", "t", tg)
                        P.wait("scalar", "v", tmpb_free[ti])
                        sv = P.opi("scalar", "s", lambda E, bank=bank, ti=ti:
                                   E.activation(TMPB[ti][:], bank[:], AF.Relu,
                                                scale=0.5))
                        bank_free[bank.name] = ("s", sv)
                        P.wait("vector", "s", sv)
                        vv2 = P.opi("vector", "v", lambda E, moa=moa, ti=ti:
                                    E.tensor_tensor(kk_blk(moa), TMPB[ti][:],
                                                    TMPB[ti][:], ALU.mult))
                        tmpb_free[ti] = vv2
                    ws_state[wi] = ("t", P.val("t"))
                prev_wfk_t[0] = P.val("t")
                v_kk_done = P.val("v")

                wi = ws_next()
                lv = ws_load(wi, lambda E, wi=wi: E.dma_start(
                    out=sb_ap(WS[wi], 0, [[CT * D, 128], [D, CT], [1, D]]),
                    in_=dr_ap(wfr, 0, [[D, 128], [128 * D, CT], [1, D]])))
                P.wait("tensor", ws_sems[wi], lv)
                for mo in range(CT):
                    bank, tg = mm_group(
                        lambda ki: xnb_data(ki),
                        lambda ki, wi=wi, mo=mo: ws_lhsT(wi, ki, mo, 1024),
                        CT)
                    P.wait("scalar", "t", tg)
                    sv = P.opi("scalar", "s", lambda E, mo=mo, bank=bank:
                               E.activation(RRG[:, mo * TCH:(mo + 1) * TCH],
                                            bank[:], AF.Sigmoid))
                    bank_free[bank.name] = ("s", sv)
                ws_state[wi] = ("t", P.val("t"))
                prev_wfr_t[0] = P.val("t")
                s_rr_done = P.val("s")

                P.wait("tensor", "v", v_kk_done)
                P.wait("vector", "ld1", v_x1c)
                for g in range(4):
                    wi = ws_next()
                    lv = ws_load(wi, lambda E, g=g, wi=wi: E.dma_start(
                        out=sb_ap(WS[wi], 0, [[CT * D, 128], [256, 32], [1, 256]]),
                        in_=dr_ap(wfv, g * 256, [[D, 128], [128 * D, 32], [1, 256]])))
                    P.wait("tensor", ws_sems[wi], lv)
                    for mo in range(2):
                        moa = g * 2 + mo
                        bank, tg = mm_group(
                            lambda ki: kk_blk(ki),
                            lambda ki, wi=wi, mo=mo: ws_lhsT(wi, ki, mo, 256),
                            32)
                        P.wait("vector", "t", tg)
                        P.wait("vector", "s", s_rr_done)
                        P.op("vector", lambda E, moa=moa, bank=bank:
                             E.tensor_tensor(TMPF[0][:],
                                             RRG[:, moa * TCH:(moa + 1) * TCH],
                                             bank[:], ALU.mult))
                        vv2 = P.opi("vector", "v", lambda E, moa=moa:
                                    E.tensor_tensor(X1C[:, moa * TCH:(moa + 1) * TCH],
                                                    X1C[:, moa * TCH:(moa + 1) * TCH],
                                                    TMPF[0][:], ALU.add))
                        bank_free[bank.name] = ("v", vv2)
                    ws_state[wi] = ("t", P.val("t"))

                P.wait("vector", "ldo", v_xo)
                v_z = P.opi("vector", "v", lambda E: E.tensor_tensor(
                    X1C[:], X1C[:], XOC[:], ALU.add))
                P.wait("vector", "t", x1b_reuse[k])
                v_zb = P.opi("vector", "v", lambda E: E.tensor_copy(ZBB[:], X1C[:]))
                s_is, sp, SM, SI = emit_stats(ZBB, [("v", v_zb)], X1C,
                                              [("v", v_z)], xsq=XSQB)
                P.wait("vector", "s", s_is)
                for ct in range(CT):
                    P.op("vector", lambda E, ct=ct, SM=SM: E.tensor_tensor(
                        XOC[:, ct * TCH:(ct + 1) * TCH],
                        X1C[:, ct * TCH:(ct + 1) * TCH], SM, ALU.subtract))
                for ct in range(CT):
                    P.op("vector", lambda E, ct=ct, SI=SI: E.tensor_tensor(
                        XOC[:, ct * TCH:(ct + 1) * TCH],
                        XOC[:, ct * TCH:(ct + 1) * TCH], SI, ALU.mult))
                v_x3 = P.opi("vector", "v", lambda E: E.tensor_copy(X3B[:], XOC[:]))
                stat_free[sp] = v_x3
                x1_reuse[k] = v_x3
                if k + 1 < len(chunks):
                    nb, ncc = chunks[k + 1]
                    P.wait("sync", "v", v_x3)
                    ld1_pre[k + 1] = P.dma(
                        "sync", "ld1", lambda E, b=nb, c=ncc: E.dma_start(
                            out=sb_ap(X1C, 0, SBW), in_=dram_chunk(x1sp, b, c)))

                for g in range(4):
                    wi_a = ws_next()
                    lva = ws_load(wi_a, lambda E, g=g, wi=wi_a: E.dma_start(
                        out=sb_ap(WS[wi], 0, [[CT * D, 128], [1024, CT], [1, 1024]]),
                        in_=dr_ap(k0, g * 1024,
                                  [[4 * D, 128], [128 * 4 * D, CT], [1, 1024]])))
                    wi_b = ws_next()
                    lvb = ws_load(wi_b, lambda E, g=g, wi=wi_b: E.dma_start(
                        out=sb_ap(WS[wi], 0, [[CT * D, 128], [1024, CT], [1, 1024]]),
                        in_=dr_ap(k0, D * 4 * D + g * 1024,
                                  [[4 * D, 128], [128 * 4 * D, CT], [1, 1024]])))
                    P.wait("vector", ws_sems[wi_a], lva)
                    P.wait("vector", ws_sems[wi_b], lvb)
                    v_sub = P.opi("vector", "v", lambda E, a=wi_a, bb2=wi_b:
                                  E.tensor_tensor(WS[a][:], WS[a][:],
                                                  WS[bb2][:], ALU.subtract))
                    ws_state[wi_b] = ("v", v_sub)
                    P.wait("tensor", "v", v_sub)
                    P.wait("tensor", "v", v_x3)
                    for mo in range(8):
                        moa = g * 8 + mo
                        bank, tg = mm_group(
                            lambda ki: X3B[:, ki * TCH:(ki + 1) * TCH],
                            lambda ki, wi=wi_a, mo=mo: ws_lhsT(wi, ki, mo, 1024),
                            CT)
                        P.wait("scalar", "t", tg)
                        sv = P.opi("scalar", "s", lambda E, moa=moa, bank=bank:
                                   E.activation(g0_blk(moa), bank[:], AF.Relu,
                                                scale=SC[:, 0:1]))
                        bank_free[bank.name] = ("s", sv)
                    ws_state[wi_a] = ("t", P.val("t"))
                s_g0_done = P.val("s")
                if k + 1 < len(chunks):
                    emit_b1b2(k + 1)

                for g in range(16):
                    wi = ws_next()
                    lv = ws_load(wi, lambda E, g=g, wi=wi: E.dma_start(
                        out=sb_ap(WS[wi], 0, [[CT * D, 128], [256, 32], [1, 256]]),
                        in_=dr_ap(k1, g * 256,
                                  [[4 * D, 128], [128 * 4 * D, 32], [1, 256]])))
                    P.wait("vector", ws_sems[wi], lv)
                    v_sub = P.opi("vector", "v", lambda E, wi=wi: E.tensor_scalar(
                        WS[wi][:], WS[wi][:], -0.5, None, ALU.add))
                    P.wait("tensor", "v", v_sub)
                    P.wait("tensor", "s", s_g0_done)
                    for mo in range(2):
                        moa = g * 2 + mo
                        bank, tg = mm_group(
                            lambda ki: g0_blk(ki),
                            lambda ki, wi=wi, mo=mo: ws_lhsT(wi, ki, mo, 256),
                            32)
                        P.wait("scalar", "t", tg)
                        sv = P.opi("scalar", "s", lambda E, moa=moa, bank=bank:
                                   E.activation(g1_blk(moa), bank[:], AF.Relu,
                                                scale=SC[:, 1:2]))
                        bank_free[bank.name] = ("s", sv)
                    ws_state[wi] = ("t", P.val("t"))
                prev_w1_t[0] = P.val("t")
                s_g1_done = P.val("s")

                P.wait("tensor", "s", s_g1_done)
                for g in range(4):
                    wi = ws_next()
                    lv = ws_load(wi, lambda E, g=g, wi=wi: E.dma_start(
                        out=sb_ap(WS[wi], 0, [[CT * D, 128], [256, 32], [1, 256]]),
                        in_=dr_ap(k2, g * 256,
                                  [[D, 128], [128 * D, 32], [1, 256]])))
                    P.wait("vector", ws_sems[wi], lv)
                    v_sub = P.opi("vector", "v", lambda E, wi=wi: E.tensor_scalar(
                        WS[wi][:], WS[wi][:], -0.5, None, ALU.add))
                    P.wait("tensor", "v", v_sub)
                    for mo in range(2):
                        moa = g * 2 + mo
                        bank, tg = mm_group(
                            lambda ki: g1_blk(ki),
                            lambda ki, wi=wi, mo=mo: ws_lhsT(wi, ki, mo, 256),
                            32)
                        P.wait("vector", "t", tg)
                        vv2 = P.opi("vector", "v", lambda E, moa=moa, bank=bank:
                                    E.scalar_tensor_tensor(
                                        XOC[:, moa * TCH:(moa + 1) * TCH],
                                        bank[:], SC[:, 2:3],
                                        XOC[:, moa * TCH:(moa + 1) * TCH],
                                        ALU.mult, ALU.add))
                        bank_free[bank.name] = ("v", vv2)
                    ws_state[wi] = ("t", P.val("t"))
                prev_b9_t[0] = P.val("t")
                v_z2 = P.val("v")

                v_zb2 = P.opi("vector", "v", lambda E: E.tensor_copy(ZBB[:], XOC[:]))
                s_is, sp, SM, SI = emit_stats(ZBB, [("v", v_zb2)], XOC,
                                              [("v", v_z2)], xsq=XSQB)
                P.wait("vector", "s", s_is)
                for ct in range(CT):
                    P.op("vector", lambda E, ct=ct, SM=SM: E.tensor_tensor(
                        XOC[:, ct * TCH:(ct + 1) * TCH],
                        XOC[:, ct * TCH:(ct + 1) * TCH], SM, ALU.subtract))
                v_out = None
                for ct in range(CT):
                    fn = lambda E, ct=ct, SI=SI: E.tensor_tensor(
                        XOC[:, ct * TCH:(ct + 1) * TCH],
                        XOC[:, ct * TCH:(ct + 1) * TCH], SI, ALU.mult)
                    v_out = P.opi("vector", "v", fn) if ct == CT - 1 else (P.op("vector", fn) or v_out)
                stat_free[sp] = v_out
                xo_reuse[k] = v_out
                P.wait("sync", "v", v_out)
                out_done[k] = P.dma("sync", "sto", lambda E, b=b, c=c: E.dma_start(
                    out=dram_chunk(out, b, c), in_=sb_ap(XOC, 0, SBW)))

            P.wait("sync", "sto", P.val("sto"))

        with nc.Block() as blk:
            @blk.tensor
            def _(E):
                for f in P.ops["tensor"]:
                    f(E)

            @blk.scalar
            def _(E):
                for f in P.ops["scalar"]:
                    f(E)

            @blk.vector
            def _(E):
                for f in P.ops["vector"]:
                    f(E)

            @blk.sync
            def _(E):
                for f in P.ops["sync"]:
                    f(E)

            @blk.gpsimd
            def _(E):
                for f in P.ops["gpsimd"]:
                    f(E)

    return nc


def kernel(L0_attn_x, L0_attn_ln1_g, L0_attn_ln1_b, L0_attn_Wk, L0_attn_Wv,
           L0_attn_Wr, L0_attn_Wo, L0_attn_time_decay, L0_attn_time_first,
           L0_attn_mix_k, L0_attn_mix_v, L0_attn_mix_r, L0_attn_ln2_g,
           L0_attn_ln2_b, L0_attn_Wfk, L0_attn_Wfv, L0_attn_Wfr,
           L0_ffn_k0, L0_ffn_k1, L0_ffn_k2, L0_ffn_s0, L0_ffn_s1, L0_ffn_s2,
           _debug_upto=None):
    x = np.asarray(L0_attn_x, dtype=np.float32)
    xT = np.ascontiguousarray(np.transpose(x, (0, 2, 1)))

    def c32(a):
        return np.ascontiguousarray(np.asarray(a, dtype=np.float32))

    def cu8(a):
        a = np.asarray(a)
        return np.ascontiguousarray(a.view(np.uint8).reshape(a.shape))

    def srep(a):
        return np.ascontiguousarray(
            np.broadcast_to(np.asarray(a, dtype=np.float32).reshape(1, 1), (128, 1)))

    shared = {
        "wk": c32(L0_attn_Wk), "wv": c32(L0_attn_Wv), "wr": c32(L0_attn_Wr),
        "wo": c32(L0_attn_Wo), "wfk": c32(L0_attn_Wfk), "wfv": c32(L0_attn_Wfv),
        "wfr": c32(L0_attn_Wfr),
        "k0": cu8(L0_ffn_k0), "k1": cu8(L0_ffn_k1), "k2": cu8(L0_ffn_k2),
        "ln1g": c32(L0_attn_ln1_g), "ln1b": c32(L0_attn_ln1_b),
        "mixk": c32(L0_attn_mix_k), "mixv": c32(L0_attn_mix_v),
        "mixr": c32(L0_attn_mix_r), "ln2g": c32(L0_attn_ln2_g),
        "ln2b": c32(L0_attn_ln2_b), "tdr": c32(L0_attn_time_decay),
        "tfr": c32(L0_attn_time_first),
        "s0": srep(L0_ffn_s0), "s1": srep(L0_ffn_s1), "s2": srep(L0_ffn_s2),
        "ones": np.ones((128, 128), dtype=np.float32),
    }
    in_maps = []
    for core in range(NCORES):
        m = dict(shared)
        m["xt"] = np.ascontiguousarray(xT[core * BPC:(core + 1) * BPC])
        in_maps.append(m)

    nc = build(debug_upto=_debug_upto)
    res = run_bass_kernel_spmd(nc, in_maps, core_ids=list(range(NCORES)))
    parts = [np.transpose(r["out"], (0, 2, 1)) for r in res.results]
    return np.ascontiguousarray(np.concatenate(parts, axis=0)).astype(np.float32)


# version 25 (final)
# speedup vs baseline: 1.0901x; 1.0215x over previous
import numpy as np
from contextlib import ExitStack

import concourse.bass as bass
import concourse.mybir as mybir
from concourse.bass_utils import run_bass_kernel_spmd

F32 = mybir.dt.float32
BF16 = mybir.dt.bfloat16
U8 = mybir.dt.uint8
AF = mybir.ActivationFunctionType
ALU = mybir.AluOpType

B_ALL, L, D = 16, 2048, 1024
NCORES = 8
BPC = B_ALL // NCORES
CT = D // 128
TCH = 512
NCHB = L // TCH
LN_EPS = 1e-5
WKV_EPS = 1e-8
INV_D = 1.0 / D

ENGS = ("tensor", "scalar", "vector", "sync", "gpsimd")


class Prog:

    def __init__(self, nc, es):
        self.nc = nc
        self.ops = {k: [] for k in ENGS}
        self.sems = {}
        self.es = es
        self.last_wait = {}

    def ctr(self, name):
        sem = self.es.enter_context(self.nc.semaphore(name))
        self.sems[name] = [sem, 0]

    def val(self, name):
        return self.sems[name][1]

    def wait(self, eng, name, v=None):
        s, cur = self.sems[name]
        if v is None:
            v = cur
        if v <= 0:
            return
        key = (eng, name)
        if self.last_wait.get(key, -1) >= v:
            return
        self.last_wait[key] = v
        self.ops[eng].append(lambda E, s=s, v=v: E.wait_ge(s, v))

    def op(self, eng, fn):
        self.ops[eng].append(fn)

    def opi(self, eng, name, fn, n=1):
        rec = self.sems[name]
        rec[1] += n
        self.ops[eng].append(lambda E, f=fn, s=rec[0], n=n: f(E).then_inc(s, n))
        return rec[1]

    def dma(self, eng, name, fn):
        self.wait(eng, name)
        return self.opi(eng, name, fn, 16)


def sb_ap(t, offset, dims):
    return bass.AP(t.ap().tensor, offset, dims)


def dr_ap(t, offset, dims):
    return bass.AP(t.tensor, offset, dims)


DBG_SERIAL = False


def build(debug_upto=None):
    nc = bass.Bass("TRN2", target_bir_lowering=False, debug=False,
                   detect_race_conditions=False)

    xt = nc.dram_tensor("xt", [BPC, D, L], F32, kind="ExternalInput").ap()
    wk = nc.dram_tensor("wk", [D, D], F32, kind="ExternalInput").ap()
    wv = nc.dram_tensor("wv", [D, D], F32, kind="ExternalInput").ap()
    wr = nc.dram_tensor("wr", [D, D], F32, kind="ExternalInput").ap()
    wo = nc.dram_tensor("wo", [D, D], F32, kind="ExternalInput").ap()
    wfk = nc.dram_tensor("wfk", [D, 4 * D], F32, kind="ExternalInput").ap()
    wfv = nc.dram_tensor("wfv", [4 * D, D], F32, kind="ExternalInput").ap()
    wfr = nc.dram_tensor("wfr", [D, D], F32, kind="ExternalInput").ap()
    k0 = nc.dram_tensor("k0", [2 * D, 4 * D], U8, kind="ExternalInput").ap()
    k1 = nc.dram_tensor("k1", [4 * D, 4 * D], U8, kind="ExternalInput").ap()
    k2 = nc.dram_tensor("k2", [4 * D, D], U8, kind="ExternalInput").ap()
    VEC_NAMES = ["ln1g", "ln1b", "mixk", "mixv", "mixr", "ln2g", "ln2b", "tdr", "tfr"]
    vecs = {n: nc.dram_tensor(n, [D], F32, kind="ExternalInput").ap() for n in VEC_NAMES}
    s_in = {n: nc.dram_tensor(n, [128, 1], F32, kind="ExternalInput").ap()
            for n in ("s0", "s1", "s2")}
    ones_in = nc.dram_tensor("ones", [128, 128], F32, kind="ExternalInput").ap()
    out = nc.dram_tensor("out", [BPC, D, L], F32, kind="ExternalOutput").ap()
    x1sp = nc.dram_tensor("x1sp", [BPC, D, L], F32, kind="Internal").ap()

    def dram_chunk(ap_t, b, c):
        return dr_ap(ap_t, b * D * L + c * TCH,
                     [[L, 128], [128 * L, CT], [1, TCH]])

    SBW = [[CT * TCH, 128], [TCH, CT], [1, TCH]]

    chunks = [(b, c) for b in range(BPC) for c in range(NCHB)]

    with ExitStack() as es:
        e = es.enter_context
        P = Prog(nc, es)

        NPV = 14
        PV = e(nc.sbuf_tensor("PV", [128, NPV * CT], F32))
        ONESB = e(nc.sbuf_tensor("ONESB", [128, 128], BF16))
        SC = e(nc.sbuf_tensor("SC", [128, 4], F32))
        ZCOL = e(nc.sbuf_tensor("ZCOL", [128, CT], F32))
        STATM = [e(nc.sbuf_tensor(f"STATM{i}", [128, TCH], F32)) for i in range(2)]
        STATI = [e(nc.sbuf_tensor(f"STATI{i}", [128, TCH], F32)) for i in range(2)]
        STATT = [e(nc.sbuf_tensor("STATT", [128, TCH], F32))] * 2
        EPST = e(nc.sbuf_tensor("EPST", [128, 2], F32))

        POOL = [e(nc.psum_tensor(f"PP{i}", [128, TCH], F32)) for i in range(8)]

        for name in ("t", "s", "v", "gmi", "gms", "gxb0", "gxb1", "ldx", "ldp",
                     "stx", "ld1", "ldo", "sto", "gw0", "gw1", "gw2"):
            P.ctr(name)

        PIDX = {n: i for i, n in enumerate(
            ["ln1g", "ln1b", "mixk", "mixv", "mixr", "ln2g", "ln2b",
             "tdr", "tfr", "ew", "etf", "omk", "omv", "omr"])}

        def pv_col(pname, ct):
            i = PIDX[pname] * CT + ct
            return PV[:, i:i + 1]

        def pv_blk(pname):
            i = PIDX[pname] * CT
            return PV[:, i:i + CT]

        bank_free = {}
        pool_i = [0]

        def next_bank(eng):
            b = POOL[pool_i[0] % 8]
            pool_i[0] += 1
            rec = bank_free.get(b.name)
            if rec is not None:
                P.wait(eng, rec[0], rec[1])
            return b

        def bank_guard(eng, b):
            rec = bank_free.get(b.name)
            if rec is not None:
                P.wait(eng, rec[0], rec[1])

        WA = {}
        for wname in ("k", "v", "r", "o"):
            WA[wname] = e(nc.sbuf_tensor(f"WA{wname}", [128, CT * D], BF16))
        XC = e(nc.sbuf_tensor("XC", [128, CT * TCH], F32))
        XB = e(nc.sbuf_tensor("XB", [128, CT * TCH], BF16))
        XN = e(nc.sbuf_tensor("XN", [128, CT * (TCH + 1)], F32))
        XBLK = TCH + 8
        XNb = e(nc.sbuf_tensor("XNb", [128, CT * XBLK], BF16))
        KIN = e(nc.sbuf_tensor("KIN", [128, CT * TCH], BF16))
        VIN = e(nc.sbuf_tensor("VIN", [128, CT * TCH], BF16))
        RIN = e(nc.sbuf_tensor("RIN", [128, CT * TCH], BF16))
        EK = e(nc.sbuf_tensor("EK", [128, CT * TCH], F32))
        VV = e(nc.sbuf_tensor("VV", [128, CT * TCH], F32))
        BB = e(nc.sbuf_tensor("BB", [128, CT * TCH], F32))
        RG = e(nc.sbuf_tensor("RG", [128, CT * TCH], BF16))
        CARA = e(nc.sbuf_tensor("CARA", [128, CT], F32))
        CARB = e(nc.sbuf_tensor("CARB", [128, CT], F32))
        XNB = e(nc.sbuf_tensor("XNB", [128, CT], F32))
        TMPB = [e(nc.sbuf_tensor(f"TMPB{i}", [128, TCH], BF16)) for i in range(2)]
        TMPF = [e(nc.sbuf_tensor(f"TMPF{i}", [128, TCH], F32)) for i in range(1)]
        AAv = XN[:, 0:CT * TCH]
        BBrw = BB[:, :].bitcast(BF16)[:, 0:CT * TCH]
        XNBb = XNB[:, :].bitcast(BF16)

        def aa_blk(ct):
            return XN[:, ct * TCH:(ct + 1) * TCH]

        def xnb_data(ct):
            return XNb[:, ct * XBLK + 8: ct * XBLK + 8 + TCH]

        def xnb_yp(ct):
            return XNb[:, ct * XBLK + 7: ct * XBLK + 7 + TCH]

        for n in VEC_NAMES:
            pid = PIDX[n]
            P.opi("sync", "ldp", lambda E, v=vecs[n], pid=pid: E.dma_start(
                out=sb_ap(PV, pid * CT, [[NPV * CT, 128], [1, CT]]),
                in_=dr_ap(v, 0, [[1, 128], [128, CT]]),
                allow_slow_non_contiguous=True), 16)
        for i, n in enumerate(("s0", "s1", "s2")):
            P.opi("sync", "ldp", lambda E, v=s_in[n], i=i: E.dma_start(
                out=SC[:, i:i + 1], in_=v[:, :]), 16)
        v_init_ld = P.val("ldp")
        P.opi("gpsimd", "gmi", lambda E: E.dma_start(out=ONESB[:], in_=ones_in[:]), 16)
        P.opi("gpsimd", "gms", lambda E: E.memset(ZCOL[:], 0.0), 1)
        P.opi("gpsimd", "gms", lambda E: E.memset(EPST[:, 0:1], LN_EPS), 1)
        P.opi("gpsimd", "gms", lambda E: E.memset(EPST[:, 1:2], WKV_EPS), 1)
        for wname, wap in (("k", wk), ("v", wv), ("r", wr), ("o", wo)):
            P.opi("gpsimd", "gmi", lambda E, wap=wap, wt=WA[wname]: E.dma_start(
                out=sb_ap(wt, 0, [[CT * D, 128], [D, CT], [1, D]]),
                in_=dr_ap(wap, 0, [[D, 128], [128 * D, CT], [1, D]])), 16)
        v_init_g = P.val("gmi")

        P.wait("scalar", "ldp", v_init_ld)
        P.wait("scalar", "gmi", v_init_g)
        P.wait("scalar", "gms", 3)
        P.op("scalar", lambda E: E.activation(pv_blk("ew"), pv_blk("tdr"), AF.Exp))
        P.op("scalar", lambda E: E.activation(pv_blk("ew"), pv_blk("ew"), AF.Exp,
                                              scale=-1.0))
        P.op("scalar", lambda E: E.activation(pv_blk("etf"), pv_blk("tfr"), AF.Exp))
        P.op("scalar", lambda E: E.activation(SC[:, 0:1], SC[:, 0:1], AF.Identity,
                                              scale=4.0))
        P.op("scalar", lambda E: E.activation(SC[:, 1:2], SC[:, 1:2], AF.Identity,
                                              scale=4.0))
        P.op("scalar", lambda E: E.activation(SC[:, 2:3], SC[:, 2:3], AF.Identity,
                                              scale=2.0))
        P.op("scalar", lambda E: E.activation(pv_blk("omk"), pv_blk("mixk"),
                                              AF.Identity, scale=-1.0, bias=1.0))
        P.op("scalar", lambda E: E.activation(pv_blk("omv"), pv_blk("mixv"),
                                              AF.Identity, scale=-1.0, bias=1.0))
        s_params = P.opi("scalar", "s", lambda E: E.activation(
            pv_blk("omr"), pv_blk("mixr"), AF.Identity, scale=-1.0, bias=1.0))

        stat_free = [0, 0]
        stat_par = [0]

        def emit_stats(rhs_bf, t_rhs_waits, s_sq_src, s_sq_waits,
                       out_bf=False, xsq=None):
            sp = stat_par[0] % 2
            stat_par[0] += 1
            M, I, T = STATM[sp], STATI[sp], STATT[sp]
            if out_bf:
                mb = STATM[sp][:, :].bitcast(BF16)
                Mout, Iout = mb[:, 0:TCH], mb[:, TCH:2 * TCH]
            else:
                Mout, Iout = M[:], I[:]
            XSQ = xsq if xsq is not None else VIN
            for cn, cv in s_sq_waits:
                P.wait("scalar", cn, cv)
            src = rhs_bf if s_sq_src is None else s_sq_src
            sqv = P.opi("scalar", "s", lambda E, src=src, XSQ=XSQ: E.activation(
                XSQ[:], src[:], AF.Square))
            S1 = next_bank("tensor")
            for cn, cv in t_rhs_waits:
                P.wait("tensor", cn, cv)
            t_s1 = None
            for ct in range(CT):
                last = ct == CT - 1
                fn = lambda E, ct=ct, last=last, S1=S1: E.matmul(
                    S1[:], ONESB[:], rhs_bf[:, ct * TCH:(ct + 1) * TCH],
                    start=(ct == 0), stop=last)
                if last:
                    t_s1 = P.opi("tensor", "t", fn)
                else:
                    P.op("tensor", fn)
            S2 = next_bank("tensor")
            P.wait("tensor", "s", sqv)
            t_s2 = None
            for ct in range(CT):
                last = ct == CT - 1
                fn = lambda E, ct=ct, last=last, S2=S2, XSQ=XSQ: E.matmul(
                    S2[:], ONESB[:], XSQ[:, ct * TCH:(ct + 1) * TCH],
                    start=(ct == 0), stop=last)
                if last:
                    t_s2 = P.opi("tensor", "t", fn)
                else:
                    P.op("tensor", fn)
            P.wait("scalar", "t", t_s1)
            P.wait("scalar", "v", stat_free[sp])
            P.opi("scalar", "s", lambda E, M=Mout, S1=S1: E.activation(
                M, S1[:], AF.Identity, scale=INV_D))
            s_t2 = P.opi("scalar", "s", lambda E, T=T, S1=S1: E.activation(
                T[:], S1[:], AF.Square, scale=INV_D))
            bank_free[S1.name] = ("s", s_t2)
            P.wait("vector", "t", t_s2)
            P.wait("vector", "s", s_t2)
            v_veps = P.opi("vector", "v", lambda E, I=I, S2=S2, T=T:
                           E.scalar_tensor_tensor(
                               I[:], S2[:], INV_D, T[:], ALU.mult, ALU.subtract))
            bank_free[S2.name] = ("v", v_veps)
            P.wait("scalar", "v", v_veps)
            P.opi("scalar", "s", lambda E, I=I: E.activation(
                I[:], I[:], AF.Ln, bias=EPST[:, 0:1]))
            s_is = P.opi("scalar", "s", lambda E, I=I, Io=Iout: E.activation(
                Io, I[:], AF.Exp, scale=-0.5))
            return s_is, sp, Mout, Iout

        def xn_blk(ct, lo, hi):
            base = ct * (TCH + 1)
            return XN[:, base + lo: base + hi]

        def mm_group(rhs_fn, lhsT_fn, nk):
            bank = next_bank("tensor")
            tg = None
            for ki in range(nk):
                last = ki == nk - 1
                fn = lambda E, ki=ki, bank=bank, last=last: E.matmul(
                    bank[:], lhsT_fn(ki), rhs_fn(ki), start=(ki == 0), stop=last)
                if last:
                    tg = P.opi("tensor", "t", fn)
                else:
                    P.op("tensor", fn)
            return bank, tg

        spill_done = {}
        prev_kvr_t = [0]
        vgrp_t = [0]
        mix_tpar = [0]
        mix_tmp_rd = {0: 0, 1: 0}
        bscan_done = {-1: 0}
        v_rw_d = {-1: 0}
        wo_t = {-1: 0, -2: 0}
        s_ekd = {}
        front_mix = {}
        front_free = {}
        vxc = {}
        P.wait("tensor", "gmi", v_init_g)

        front_stats = {}

        def emit_front_stats(i):
            front_stats[i] = emit_stats(
                XB, [("gxb0", xbload[i])], None,
                [("t", vgrp_t[0]), ("gxb0", xbload[i])],
                out_bf=True, xsq=VIN)

        def emit_front_rest(i):
            b, c = chunks[i]
            s_is, sp, SMb, SIb = front_stats[i]
            pad_dst = bass.AP(XNb.ap().tensor, 7,
                              [[CT * XBLK, 128], [XBLK, CT], [1, 1]])
            if c == 0:
                P.wait("vector", "gmi", v_init_g)
                P.wait("vector", "gms", 3)
                P.op("vector", lambda E, d=pad_dst: E.tensor_copy(
                    d, sb_ap(ZCOL, 0, [[CT, 128], [1, CT], [1, 1]])))
            else:
                P.op("vector", lambda E, d=pad_dst: E.tensor_copy(
                    d, XNBb[:, 0:CT]))
            P.wait("vector", "gxb0", xbload[i])
            P.wait("vector", "s", s_is)
            for ct in range(CT):
                P.op("vector", lambda E, ct=ct, SMb=SMb: E.tensor_tensor(
                    xnb_data(ct), XB[:, ct * TCH:(ct + 1) * TCH],
                    SMb, ALU.subtract))
            v_lnmul = None
            for ct in range(CT):
                fn = lambda E, ct=ct, SIb=SIb: E.tensor_tensor(
                    xnb_data(ct), xnb_data(ct), SIb, ALU.mult)
                v_lnmul = P.opi("vector", "v", fn) if ct == CT - 1 else (
                    P.op("vector", fn) or v_lnmul)
            stat_free[sp] = v_lnmul
            P.wait("scalar", "v", v_lnmul)
            P.wait("scalar", "s", s_params)
            s_gb = None
            for ct in range(CT):
                fn = lambda E, ct=ct: E.activation(
                    xnb_data(ct), xnb_data(ct), AF.Identity,
                    scale=pv_col("ln1g", ct), bias=pv_col("ln1b", ct))
                s_gb = P.opi("scalar", "s", fn) if ct == CT - 1 else (
                    P.op("scalar", fn) or s_gb)
            P.wait("vector", "s", s_gb)
            P.wait("vector", "t", prev_kvr_t[0])
            P.wait("scalar", "t", prev_kvr_t[0])
            v_mix = None
            tpar = mix_tpar
            tmp_rd = mix_tmp_rd
            for mixn, omn, dst in (("mixk", "omk", KIN), ("mixv", "omv", VIN),
                                   ("mixr", "omr", RIN)):
                for ct in range(CT):
                    ti = tpar[0] % 2
                    tpar[0] += 1
                    P.wait("scalar", "v", tmp_rd[ti])
                    sv = P.opi("scalar", "s", lambda E, ct=ct, o=omn, ti=ti:
                               E.activation(TMPB[ti][:], xnb_yp(ct), AF.Identity,
                                            scale=pv_col(o, ct)))
                    P.op("vector", lambda E, ct=ct, m=mixn, d=dst:
                         E.tensor_scalar(d[:, ct * TCH:(ct + 1) * TCH],
                                         xnb_data(ct), pv_col(m, ct), None,
                                         ALU.mult))
                    P.wait("vector", "s", sv)
                    fn = lambda E, ct=ct, d=dst, ti=ti: E.tensor_tensor(
                        d[:, ct * TCH:(ct + 1) * TCH],
                        d[:, ct * TCH:(ct + 1) * TCH], TMPB[ti][:], ALU.add)
                    if mixn == "mixr" and ct == CT - 1:
                        v_mix = P.opi("vector", "v", fn)
                    else:
                        v_mix = P.opi("vector", "v", fn)
                    tmp_rd[ti] = v_mix
            P.op("vector", lambda E: E.tensor_copy(
                XNBb[:, 0:CT],
                bass.AP(XNb.ap().tensor, 8 + TCH - 1,
                        [[CT * XBLK, 128], [XBLK, CT], [1, 1]])))
            front_mix[i] = v_mix
            front_free[i] = v_lnmul

        def emit_kvr(i, mid_cb=None):
            P.wait("tensor", "v", front_mix[i])
            for wname, rhs_t in (("v", VIN), ("r", RIN), ("k", KIN)):
                for mo in range(CT):
                    bank, tg = mm_group(
                        lambda ki, rhs_t=rhs_t: rhs_t[:, ki * TCH:(ki + 1) * TCH],
                        lambda ki, w=WA[wname], mo=mo:
                            w[:, ki * D + mo * 128: ki * D + (mo + 1) * 128],
                        CT)
                    if wname == "k":
                        P.wait("scalar", "t", tg)
                        P.wait("scalar", "v", bscan_done[i - 1])
                        sv = P.opi("scalar", "s", lambda E, mo=mo, bank=bank:
                                   E.activation(EK[:, mo * TCH:(mo + 1) * TCH],
                                                bank[:], AF.Exp))
                        bank_free[bank.name] = ("s", sv)
                    elif wname == "v":
                        P.wait("vector", "t", tg)
                        vv2 = P.opi("vector", "v", lambda E, mo=mo, bank=bank:
                                    E.tensor_copy(VV[:, mo * TCH:(mo + 1) * TCH],
                                                  bank[:]))
                        bank_free[bank.name] = ("v", vv2)
                    else:
                        P.wait("scalar", "t", tg)
                        P.wait("scalar", "v", v_rw_d[i - 1])
                        sv = P.opi("scalar", "s", lambda E, mo=mo, bank=bank:
                                   E.activation(RG[:, mo * TCH:(mo + 1) * TCH],
                                                bank[:], AF.Sigmoid))
                        bank_free[bank.name] = ("s", sv)
                if wname == "v":
                    vgrp_t[0] = P.val("t")
                    if mid_cb is not None:
                        mid_cb()
            prev_kvr_t[0] = P.val("t")
            s_ekd[i] = P.val("s")

        def emit_back(i):
            b, c = chunks[i]
            P.wait("vector", "s", s_ekd[i])
            if c == 0:
                for ct in range(CT):
                    P.op("vector", lambda E, ct=ct: E.tensor_tensor(
                        EK[:, ct * TCH:ct * TCH + 1],
                        EK[:, ct * TCH:ct * TCH + 1],
                        pv_col("etf", ct), ALU.mult))
                P.op("vector", lambda E: E.tensor_copy(CARA[:], ZCOL[:]))
                P.op("vector", lambda E: E.tensor_copy(CARB[:], ZCOL[:]))
            P.opi("vector", "v", lambda E: E.tensor_tensor(
                VV[:], EK[:], VV[:], ALU.mult))
            P.wait("vector", "v")
            P.wait("vector", "t", wo_t[i - 1])
            for ct in range(CT):
                last = ct == CT - 1
                fn = lambda E, ct=ct: E.tensor_tensor_scan(
                    BB[:, ct * TCH:(ct + 1) * TCH],
                    pv_col("ew", ct).broadcast_to([128, TCH]),
                    EK[:, ct * TCH:(ct + 1) * TCH],
                    CARB[:, ct:ct + 1], ALU.mult, ALU.add)
                if last:
                    P.opi("vector", "v", fn)
                else:
                    P.op("vector", fn)
            P.wait("vector", "v")
            v_bs = P.opi("vector", "v", lambda E: E.tensor_copy(
                CARB[:], sb_ap(BB, TCH - 1, [[CT * TCH, 128], [TCH, CT], [1, 1]])))
            for ct in range(CT):
                last = ct == CT - 1
                fn = lambda E, ct=ct: E.tensor_tensor_scan(
                    aa_blk(ct), pv_col("ew", ct).broadcast_to([128, TCH]),
                    VV[:, ct * TCH:(ct + 1) * TCH],
                    CARA[:, ct:ct + 1], ALU.mult, ALU.add)
                if last:
                    P.opi("vector", "v", fn)
                else:
                    P.op("vector", fn)
            P.wait("vector", "v")
            v_car = P.opi("vector", "v", lambda E: E.tensor_copy(
                CARA[:], sb_ap(XN, TCH - 1,
                               [[CT * (TCH + 1), 128], [TCH, CT], [1, 1]])))
            bscan_done[i] = v_car
            P.wait("scalar", "v", v_bs)
            P.opi("scalar", "s", lambda E: E.activation(BB[:], BB[:], AF.Ln,
                                                        bias=EPST[:, 1:2]))
            s_rec = P.opi("scalar", "s", lambda E: E.activation(
                BB[:], BB[:], AF.Exp, scale=-1.0))
            P.wait("vector", "s", s_rec)
            P.opi("vector", "v", lambda E: E.tensor_tensor(AAv, AAv, BB[:],
                                                           ALU.mult))
            v_rw_d[i] = P.opi("vector", "v", lambda E: E.tensor_tensor(
                BBrw, AAv, RG[:], ALU.mult))

        def emit_wo(i):
            b, c = chunks[i]
            P.wait("tensor", "v", v_rw_d[i])
            for mo in range(CT):
                bank, tg = mm_group(
                    lambda ki: BBrw[:, ki * TCH:(ki + 1) * TCH],
                    lambda ki, mo=mo:
                        WA["o"][:, ki * D + mo * 128: ki * D + (mo + 1) * 128],
                    CT)
                P.wait("vector", "t", tg)
                P.wait("vector", "ldx", vxc[i])
                vv2 = P.opi("vector", "v", lambda E, mo=mo, bank=bank:
                            E.tensor_tensor(XC[:, mo * TCH:(mo + 1) * TCH],
                                            XC[:, mo * TCH:(mo + 1) * TCH],
                                            bank[:], ALU.add))
                bank_free[bank.name] = ("v", vv2)
            wo_t[i] = P.val("t")
            v_x1 = P.val("v")
            P.wait("sync", "v", v_x1)
            dst = out if debug_upto in ("A", "TAPS2") else x1sp
            spill_done[i] = P.dma("sync", "stx",
                                  lambda E, b=b, c=c, dst=dst: E.dma_start(
                                      out=dram_chunk(dst, b, c),
                                      in_=sb_ap(XC, 0, SBW)))

        NCH = len(chunks)

        def emit_front_taps():
            P.wait("gpsimd", "v", front_mix[0])
            xnd = bass.AP(XNb.ap().tensor, 8,
                          [[CT * XBLK, 128], [XBLK, CT], [1, TCH]])
            P.opi("gpsimd", "gmi", lambda E: E.dma_start(
                out=dram_chunk(out, 0, 0), in_=xnd), 16)
            P.opi("gpsimd", "gmi", lambda E: E.dma_start(
                out=dram_chunk(out, 0, 1), in_=sb_ap(KIN, 0, SBW)), 16)
            mb = STATM[0][:, :].bitcast(BF16)
            P.opi("gpsimd", "gmi", lambda E: E.dma_start(
                out=dr_ap(out, 2 * TCH, [[L, 128], [1, TCH]]),
                in_=mb[:, 0:TCH]), 16)
            P.opi("gpsimd", "gmi", lambda E: E.dma_start(
                out=dr_ap(out, 3 * TCH, [[L, 128], [1, TCH]]),
                in_=mb[:, TCH:2 * TCH]), 16)
            P.wait("gpsimd", "gmi", P.val("gmi"))

        xbload = {}
        xbload[0] = P.dma("gpsimd", "gxb0", lambda E, b=chunks[0][0],
                          c=chunks[0][1]: E.dma_start(
                              out=sb_ap(XB, 0, SBW), in_=dram_chunk(xt, b, c)))
        emit_front_stats(0)
        emit_front_rest(0)
        if debug_upto == "TAPS":
            emit_front_taps()
        nloop = NCH
        if debug_upto == "TAPS":
            nloop = 0
        elif debug_upto == "TAPS1":
            nloop = 1
        elif debug_upto == "TAPS2":
            nloop = 2
        for i in range(nloop):
            allow_pre = debug_upto not in ("TAPS1", "TAPS2") or \
                (debug_upto == "TAPS2" and i == 0)

            def mid_cb(i=i, allow_pre=allow_pre):
                if i + 1 < NCH and allow_pre:
                    P.wait("gpsimd", "v", front_free[i])
                    xbload[i + 1] = P.dma("gpsimd", "gxb0",
                                          lambda E, b=chunks[i + 1][0],
                                          c=chunks[i + 1][1]: E.dma_start(
                                              out=sb_ap(XB, 0, SBW),
                                              in_=dram_chunk(xt, b, c)))
                    emit_front_stats(i + 1)

            emit_kvr(i, mid_cb)
            if i + 1 < NCH and allow_pre:
                emit_front_rest(i + 1)
            if i >= 1:
                emit_wo(i - 1)
            if i >= 1:
                P.wait("sync", "stx", spill_done[i - 1])
            vxc[i] = P.dma("sync", "ldx", lambda E, b=chunks[i][0],
                           c=chunks[i][1]: E.dma_start(
                               out=sb_ap(XC, 0, SBW),
                               in_=dram_chunk(xt, b, c)))
            emit_back(i)
        if debug_upto == "TAPS1":
            P.wait("sync", "v", v_rw_d[0])
            P.wait("sync", "s", P.val("s"))
            P.opi("sync", "stx", lambda E: E.dma_start(
                out=dram_chunk(out, 0, 0), in_=sb_ap(EK, 0, SBW)), 16)
            P.opi("sync", "stx", lambda E: E.dma_start(
                out=dram_chunk(out, 0, 1), in_=sb_ap(VV, 0, SBW)), 16)
            P.opi("sync", "stx", lambda E: E.dma_start(
                out=dram_chunk(out, 0, 2),
                in_=bass.AP(XN.ap().tensor, 0,
                            [[CT * (TCH + 1), 128], [TCH, CT], [1, TCH]])), 16)
            P.wait("gpsimd", "v", v_rw_d[0])
            P.opi("gpsimd", "gmi", lambda E: E.dma_start(
                out=dram_chunk(out, 0, 3),
                in_=bass.AP(BB.ap().tensor, 0,
                            [[CT * TCH, 128], [TCH, CT], [1, TCH]])), 16)
            P.wait("sync", "stx", P.val("stx"))
            P.wait("gpsimd", "gmi", P.val("gmi"))
        if debug_upto == "TAPS2":
            emit_wo(1)
        if debug_upto not in ("TAPS", "TAPS1", "TAPS2"):
            emit_wo(NCH - 1)

        if debug_upto in ("A", "TAPS", "TAPS1", "TAPS2"):
            P.wait("sync", "stx", P.val("stx"))
        else:
            finals = [("t", P.val("t")), ("s", P.val("s")), ("v", P.val("v")),
                      ("stx", P.val("stx")), ("ldx", P.val("ldx")),
                      ("gxb0", P.val("gxb0")), ("gxb1", P.val("gxb1")),
                      ("gmi", P.val("gmi")), ("gms", P.val("gms"))]
            for eng in ENGS:
                for cn, cv in finals:
                    P.wait(eng, cn, cv)

            X1C = XC
            XOC = BB
            X1B = XB
            ZBB = VIN
            XSQB = VV[:, :].bitcast(BF16)[:, 0:CT * TCH]
            YN = XN
            MIXB = KIN
            YB = VIN
            RRG = RIN
            X3B = RG
            EKb = EK[:, :].bitcast(BF16)
            VVb = VV[:, :].bitcast(BF16)
            YNb = XN[:, 0:CT * TCH].bitcast(BF16)
            WOb = WA["o"][:, :]

            def kk_blk(f):
                base = EKb if f < 16 else VVb
                return base[:, (f % 16) * TCH:((f % 16) + 1) * TCH]

            def g0_blk(f):
                base = YNb if f < 16 else WOb
                return base[:, (f % 16) * TCH:((f % 16) + 1) * TCH]

            g1_blk = kk_blk

            WS = [WA["k"], WA["v"], WA["r"]]

            def yn_blk(ct, lo, hi):
                base = ct * (TCH + 1)
                return YN[:, base + lo: base + hi]

            ws_state = [None, None, None]
            ws_i = [0]
            ws_sems = ["gw0", "gw1", "gw2"]

            def ws_next():
                i = ws_i[0] % 3
                ws_i[0] += 1
                return i

            def ws_load(i, fn):
                if ws_state[i] is not None:
                    P.wait("gpsimd", ws_state[i][0], ws_state[i][1])
                return P.dma("gpsimd", ws_sems[i], fn)

            def ws_lhsT(wi, ki, mo, blk):
                return WS[wi][:, ki * blk + mo * 128: ki * blk + (mo + 1) * 128]

            x1_reuse = {}
            xo_reuse = {}
            out_done = {}
            x1b_reuse = {}
            prev_wfr_t = [0]
            prev_w1_t = [0]
            prev_b9_t = [0]
            prev_wfk_t = [0]
            tmpb_free = [0, 0]
            tmp_i = [0]

            ld1_pre = {}
            gxb_pre = {}
            v_mix_d = {}

            def emit_b1b2(k):
                b, c = chunks[k]
                if k == 0:
                    P.wait("sync", "stx", spill_done[k])
                    v_x1c_d[k] = P.dma("sync", "ld1", lambda E, b=b, c=c:
                                       E.dma_start(out=sb_ap(X1C, 0, SBW),
                                                   in_=dram_chunk(x1sp, b, c)))
                    P.wait("gpsimd", "stx", spill_done[k])
                    v_x1b = P.dma("gpsimd", "gxb0", lambda E, b=b, c=c:
                                  E.dma_start(out=sb_ap(X1B, 0, SBW),
                                              in_=dram_chunk(x1sp, b, c)))
                else:
                    v_x1b = gxb_pre[k]
                s_is, sp, SM, SI = emit_stats(X1B, [("gxb0", v_x1b)], None,
                                              [("t", prev_wfk_t[0]),
                                               ("gxb0", v_x1b)],
                                              out_bf=True, xsq=MIXB)
                x1b_reuse[k] = P.val("t")
                pad_dst = bass.AP(XNb.ap().tensor, 7,
                                  [[CT * XBLK, 128], [XBLK, CT], [1, 1]])
                P.wait("vector", "t", prev_wfr_t[0])
                if c == 0:
                    P.op("vector", lambda E, d=pad_dst: E.tensor_copy(
                        d, sb_ap(ZCOL, 0, [[CT, 128], [1, CT], [1, 1]])))
                else:
                    P.op("vector", lambda E, d=pad_dst: E.tensor_copy(
                        d, XNBb[:, 0:CT]))
                P.wait("vector", "gxb0", v_x1b)
                P.wait("vector", "s", s_is)
                for ct in range(CT):
                    P.op("vector", lambda E, ct=ct, SM=SM: E.tensor_tensor(
                        xnb_data(ct), X1B[:, ct * TCH:(ct + 1) * TCH],
                        SM, ALU.subtract))
                v_ln = None
                for ct in range(CT):
                    fn = lambda E, ct=ct, SI=SI: E.tensor_tensor(
                        xnb_data(ct), xnb_data(ct), SI, ALU.mult)
                    v_ln = P.opi("vector", "v", fn) if ct == CT - 1 else (P.op("vector", fn) or v_ln)
                stat_free[sp] = v_ln
                if k + 1 < len(chunks):
                    nb, ncc = chunks[k + 1]
                    P.wait("gpsimd", "t", x1b_reuse[k])
                    P.wait("gpsimd", "v", v_ln)
                    gxb_pre[k + 1] = P.dma(
                        "gpsimd", "gxb0", lambda E, b=nb, c=ncc: E.dma_start(
                            out=sb_ap(X1B, 0, SBW), in_=dram_chunk(x1sp, b, c)))
                P.wait("scalar", "v", v_ln)
                s_yn = None
                for ct in range(CT):
                    fn = lambda E, ct=ct: E.activation(
                        xnb_data(ct), xnb_data(ct), AF.Identity,
                        scale=pv_col("ln2g", ct), bias=pv_col("ln2b", ct))
                    s_yn = P.opi("scalar", "s", fn) if ct == CT - 1 else (P.op("scalar", fn) or s_yn)
                P.wait("vector", "s", s_yn)
                P.wait("vector", "t", prev_wfk_t[0])
                v_mix = None
                for ct in range(CT):
                    fn = lambda E, ct=ct: E.tensor_tensor(
                        MIXB[:, ct * TCH:(ct + 1) * TCH], xnb_data(ct),
                        xnb_yp(ct), ALU.add)
                    v_mix = P.opi("vector", "v", fn) if ct == CT - 1 else (P.op("vector", fn) or v_mix)
                P.op("vector", lambda E: E.tensor_copy(
                    XNBb[:, 0:CT],
                    bass.AP(XNb.ap().tensor, 8 + TCH - 1,
                            [[CT * XBLK, 128], [XBLK, CT], [1, 1]])))
                v_mix_d[k] = v_mix

            v_x1c_d = {}
            emit_b1b2(0)
            for k, (b, c) in enumerate(chunks):
                v_x1c = v_x1c_d[k]
                v_mix = v_mix_d[k]
                if k >= 1:
                    P.wait("sync", "sto", out_done[k - 1])
                    P.wait("sync", "v", xo_reuse[k - 1])
                v_xo = P.dma("sync", "ldo", lambda E, b=b, c=c: E.dma_start(
                    out=sb_ap(XOC, 0, SBW), in_=dram_chunk(xt, b, c)))

                P.wait("tensor", "v", v_mix)
                for g in range(4):
                    wi = ws_next()
                    lv = ws_load(wi, lambda E, g=g, wi=wi: E.dma_start(
                        out=sb_ap(WS[wi], 0, [[CT * D, 128], [1024, CT], [1, 1024]]),
                        in_=dr_ap(wfk, g * 1024,
                                  [[4 * D, 128], [128 * 4 * D, CT], [1, 1024]])))
                    P.wait("tensor", ws_sems[wi], lv)
                    for mo in range(8):
                        moa = g * 8 + mo
                        bank, tg = mm_group(
                            lambda ki: MIXB[:, ki * TCH:(ki + 1) * TCH],
                            lambda ki, wi=wi, mo=mo: ws_lhsT(wi, ki, mo, 1024),
                            CT)
                        ti = tmp_i[0] % 2
                        tmp_i[0] += 1
                        P.wait("scalar", "t", tg)
                        P.wait("scalar", "v", tmpb_free[ti])
                        sv = P.opi("scalar", "s", lambda E, bank=bank, ti=ti:
                                   E.activation(TMPB[ti][:], bank[:], AF.Relu,
                                                scale=0.5))
                        bank_free[bank.name] = ("s", sv)
                        P.wait("vector", "s", sv)
                        vv2 = P.opi("vector", "v", lambda E, moa=moa, ti=ti:
                                    E.tensor_tensor(kk_blk(moa), TMPB[ti][:],
                                                    TMPB[ti][:], ALU.mult))
                        tmpb_free[ti] = vv2
                    ws_state[wi] = ("t", P.val("t"))
                prev_wfk_t[0] = P.val("t")
                v_kk_done = P.val("v")

                wi = ws_next()
                lv = ws_load(wi, lambda E, wi=wi: E.dma_start(
                    out=sb_ap(WS[wi], 0, [[CT * D, 128], [D, CT], [1, D]]),
                    in_=dr_ap(wfr, 0, [[D, 128], [128 * D, CT], [1, D]])))
                P.wait("tensor", ws_sems[wi], lv)
                for mo in range(CT):
                    bank, tg = mm_group(
                        lambda ki: xnb_data(ki),
                        lambda ki, wi=wi, mo=mo: ws_lhsT(wi, ki, mo, 1024),
                        CT)
                    P.wait("scalar", "t", tg)
                    sv = P.opi("scalar", "s", lambda E, mo=mo, bank=bank:
                               E.activation(RRG[:, mo * TCH:(mo + 1) * TCH],
                                            bank[:], AF.Sigmoid))
                    bank_free[bank.name] = ("s", sv)
                ws_state[wi] = ("t", P.val("t"))
                prev_wfr_t[0] = P.val("t")
                s_rr_done = P.val("s")

                P.wait("tensor", "v", v_kk_done)
                P.wait("vector", "ld1", v_x1c)
                for g in range(4):
                    wi = ws_next()
                    lv = ws_load(wi, lambda E, g=g, wi=wi: E.dma_start(
                        out=sb_ap(WS[wi], 0, [[CT * D, 128], [256, 32], [1, 256]]),
                        in_=dr_ap(wfv, g * 256, [[D, 128], [128 * D, 32], [1, 256]])))
                    P.wait("tensor", ws_sems[wi], lv)
                    for mo in range(2):
                        moa = g * 2 + mo
                        bank, tg = mm_group(
                            lambda ki: kk_blk(ki),
                            lambda ki, wi=wi, mo=mo: ws_lhsT(wi, ki, mo, 256),
                            32)
                        P.wait("vector", "t", tg)
                        P.wait("vector", "s", s_rr_done)
                        P.op("vector", lambda E, moa=moa, bank=bank:
                             E.tensor_tensor(TMPF[0][:],
                                             RRG[:, moa * TCH:(moa + 1) * TCH],
                                             bank[:], ALU.mult))
                        vv2 = P.opi("vector", "v", lambda E, moa=moa:
                                    E.tensor_tensor(X1C[:, moa * TCH:(moa + 1) * TCH],
                                                    X1C[:, moa * TCH:(moa + 1) * TCH],
                                                    TMPF[0][:], ALU.add))
                        bank_free[bank.name] = ("v", vv2)
                    ws_state[wi] = ("t", P.val("t"))

                if k + 1 < len(chunks):
                    emit_b1b2(k + 1)

                P.wait("vector", "ldo", v_xo)
                v_z = P.opi("vector", "v", lambda E: E.tensor_tensor(
                    X1C[:], X1C[:], XOC[:], ALU.add))
                P.wait("vector", "t", x1b_reuse[k])
                v_zb = P.opi("vector", "v", lambda E: E.tensor_copy(ZBB[:], X1C[:]))
                s_is, sp, SM, SI = emit_stats(ZBB, [("v", v_zb)], X1C,
                                              [("v", v_z)], xsq=XSQB)
                P.wait("vector", "s", s_is)
                for ct in range(CT):
                    P.op("vector", lambda E, ct=ct, SM=SM: E.tensor_tensor(
                        XOC[:, ct * TCH:(ct + 1) * TCH],
                        X1C[:, ct * TCH:(ct + 1) * TCH], SM, ALU.subtract))
                for ct in range(CT):
                    P.op("vector", lambda E, ct=ct, SI=SI: E.tensor_tensor(
                        XOC[:, ct * TCH:(ct + 1) * TCH],
                        XOC[:, ct * TCH:(ct + 1) * TCH], SI, ALU.mult))
                v_x3 = P.opi("vector", "v", lambda E: E.tensor_copy(X3B[:], XOC[:]))
                stat_free[sp] = v_x3
                x1_reuse[k] = v_x3
                if k + 1 < len(chunks):
                    nb, ncc = chunks[k + 1]
                    P.wait("sync", "v", v_x3)
                    ld1_pre[k + 1] = P.dma(
                        "sync", "ld1", lambda E, b=nb, c=ncc: E.dma_start(
                            out=sb_ap(X1C, 0, SBW), in_=dram_chunk(x1sp, b, c)))
                    v_x1c_d[k + 1] = ld1_pre[k + 1]

                for g in range(4):
                    wi_a = ws_next()
                    lva = ws_load(wi_a, lambda E, g=g, wi=wi_a: E.dma_start(
                        out=sb_ap(WS[wi], 0, [[CT * D, 128], [1024, CT], [1, 1024]]),
                        in_=dr_ap(k0, g * 1024,
                                  [[4 * D, 128], [128 * 4 * D, CT], [1, 1024]])))
                    wi_b = ws_next()
                    lvb = ws_load(wi_b, lambda E, g=g, wi=wi_b: E.dma_start(
                        out=sb_ap(WS[wi], 0, [[CT * D, 128], [1024, CT], [1, 1024]]),
                        in_=dr_ap(k0, D * 4 * D + g * 1024,
                                  [[4 * D, 128], [128 * 4 * D, CT], [1, 1024]])))
                    P.wait("vector", ws_sems[wi_a], lva)
                    P.wait("vector", ws_sems[wi_b], lvb)
                    v_sub = P.opi("vector", "v", lambda E, a=wi_a, bb2=wi_b:
                                  E.tensor_tensor(WS[a][:], WS[a][:],
                                                  WS[bb2][:], ALU.subtract))
                    ws_state[wi_b] = ("v", v_sub)
                    P.wait("tensor", "v", v_sub)
                    P.wait("tensor", "v", v_x3)
                    for mo in range(8):
                        moa = g * 8 + mo
                        bank, tg = mm_group(
                            lambda ki: X3B[:, ki * TCH:(ki + 1) * TCH],
                            lambda ki, wi=wi_a, mo=mo: ws_lhsT(wi, ki, mo, 1024),
                            CT)
                        P.wait("scalar", "t", tg)
                        sv = P.opi("scalar", "s", lambda E, moa=moa, bank=bank:
                                   E.activation(g0_blk(moa), bank[:], AF.Relu,
                                                scale=SC[:, 0:1]))
                        bank_free[bank.name] = ("s", sv)
                    ws_state[wi_a] = ("t", P.val("t"))
                s_g0_done = P.val("s")

                for g in range(16):
                    wi = ws_next()
                    lv = ws_load(wi, lambda E, g=g, wi=wi: E.dma_start(
                        out=sb_ap(WS[wi], 0, [[CT * D, 128], [256, 32], [1, 256]]),
                        in_=dr_ap(k1, g * 256,
                                  [[4 * D, 128], [128 * 4 * D, 32], [1, 256]])))
                    P.wait("vector", ws_sems[wi], lv)
                    v_sub = P.opi("vector", "v", lambda E, wi=wi: E.tensor_scalar(
                        WS[wi][:], WS[wi][:], -0.5, None, ALU.add))
                    P.wait("tensor", "v", v_sub)
                    P.wait("tensor", "s", s_g0_done)
                    for mo in range(2):
                        moa = g * 2 + mo
                        bank, tg = mm_group(
                            lambda ki: g0_blk(ki),
                            lambda ki, wi=wi, mo=mo: ws_lhsT(wi, ki, mo, 256),
                            32)
                        P.wait("scalar", "t", tg)
                        sv = P.opi("scalar", "s", lambda E, moa=moa, bank=bank:
                                   E.activation(g1_blk(moa), bank[:], AF.Relu,
                                                scale=SC[:, 1:2]))
                        bank_free[bank.name] = ("s", sv)
                    ws_state[wi] = ("t", P.val("t"))
                prev_w1_t[0] = P.val("t")
                s_g1_done = P.val("s")

                P.wait("tensor", "s", s_g1_done)
                for g in range(4):
                    wi = ws_next()
                    lv = ws_load(wi, lambda E, g=g, wi=wi: E.dma_start(
                        out=sb_ap(WS[wi], 0, [[CT * D, 128], [256, 32], [1, 256]]),
                        in_=dr_ap(k2, g * 256,
                                  [[D, 128], [128 * D, 32], [1, 256]])))
                    P.wait("vector", ws_sems[wi], lv)
                    v_sub = P.opi("vector", "v", lambda E, wi=wi: E.tensor_scalar(
                        WS[wi][:], WS[wi][:], -0.5, None, ALU.add))
                    P.wait("tensor", "v", v_sub)
                    for mo in range(2):
                        moa = g * 2 + mo
                        bank, tg = mm_group(
                            lambda ki: g1_blk(ki),
                            lambda ki, wi=wi, mo=mo: ws_lhsT(wi, ki, mo, 256),
                            32)
                        P.wait("vector", "t", tg)
                        vv2 = P.opi("vector", "v", lambda E, moa=moa, bank=bank:
                                    E.scalar_tensor_tensor(
                                        XOC[:, moa * TCH:(moa + 1) * TCH],
                                        bank[:], SC[:, 2:3],
                                        XOC[:, moa * TCH:(moa + 1) * TCH],
                                        ALU.mult, ALU.add))
                        bank_free[bank.name] = ("v", vv2)
                    ws_state[wi] = ("t", P.val("t"))
                prev_b9_t[0] = P.val("t")
                v_z2 = P.val("v")

                v_zb2 = P.opi("vector", "v", lambda E: E.tensor_copy(ZBB[:], XOC[:]))
                s_is, sp, SM, SI = emit_stats(ZBB, [("v", v_zb2)], XOC,
                                              [("v", v_z2)], xsq=XSQB)
                P.wait("vector", "s", s_is)
                for ct in range(CT):
                    P.op("vector", lambda E, ct=ct, SM=SM: E.tensor_tensor(
                        XOC[:, ct * TCH:(ct + 1) * TCH],
                        XOC[:, ct * TCH:(ct + 1) * TCH], SM, ALU.subtract))
                v_out = None
                for ct in range(CT):
                    fn = lambda E, ct=ct, SI=SI: E.tensor_tensor(
                        XOC[:, ct * TCH:(ct + 1) * TCH],
                        XOC[:, ct * TCH:(ct + 1) * TCH], SI, ALU.mult)
                    v_out = P.opi("vector", "v", fn) if ct == CT - 1 else (P.op("vector", fn) or v_out)
                stat_free[sp] = v_out
                xo_reuse[k] = v_out
                P.wait("sync", "v", v_out)
                out_done[k] = P.dma("sync", "sto", lambda E, b=b, c=c: E.dma_start(
                    out=dram_chunk(out, b, c), in_=sb_ap(XOC, 0, SBW)))

            P.wait("sync", "sto", P.val("sto"))

        with nc.Block() as blk:
            @blk.tensor
            def _(E):
                for f in P.ops["tensor"]:
                    f(E)

            @blk.scalar
            def _(E):
                for f in P.ops["scalar"]:
                    f(E)

            @blk.vector
            def _(E):
                for f in P.ops["vector"]:
                    f(E)

            @blk.sync
            def _(E):
                for f in P.ops["sync"]:
                    f(E)

            @blk.gpsimd
            def _(E):
                for f in P.ops["gpsimd"]:
                    f(E)

    return nc


def kernel(L0_attn_x, L0_attn_ln1_g, L0_attn_ln1_b, L0_attn_Wk, L0_attn_Wv,
           L0_attn_Wr, L0_attn_Wo, L0_attn_time_decay, L0_attn_time_first,
           L0_attn_mix_k, L0_attn_mix_v, L0_attn_mix_r, L0_attn_ln2_g,
           L0_attn_ln2_b, L0_attn_Wfk, L0_attn_Wfv, L0_attn_Wfr,
           L0_ffn_k0, L0_ffn_k1, L0_ffn_k2, L0_ffn_s0, L0_ffn_s1, L0_ffn_s2,
           _debug_upto=None):
    x = np.asarray(L0_attn_x, dtype=np.float32)
    xT = np.ascontiguousarray(np.transpose(x, (0, 2, 1)))

    def c32(a):
        return np.ascontiguousarray(np.asarray(a, dtype=np.float32))

    def cu8(a):
        a = np.asarray(a)
        return np.ascontiguousarray(a.view(np.uint8).reshape(a.shape))

    def srep(a):
        return np.ascontiguousarray(
            np.broadcast_to(np.asarray(a, dtype=np.float32).reshape(1, 1), (128, 1)))

    shared = {
        "wk": c32(L0_attn_Wk), "wv": c32(L0_attn_Wv), "wr": c32(L0_attn_Wr),
        "wo": c32(L0_attn_Wo), "wfk": c32(L0_attn_Wfk), "wfv": c32(L0_attn_Wfv),
        "wfr": c32(L0_attn_Wfr),
        "k0": cu8(L0_ffn_k0), "k1": cu8(L0_ffn_k1), "k2": cu8(L0_ffn_k2),
        "ln1g": c32(L0_attn_ln1_g), "ln1b": c32(L0_attn_ln1_b),
        "mixk": c32(L0_attn_mix_k), "mixv": c32(L0_attn_mix_v),
        "mixr": c32(L0_attn_mix_r), "ln2g": c32(L0_attn_ln2_g),
        "ln2b": c32(L0_attn_ln2_b), "tdr": c32(L0_attn_time_decay),
        "tfr": c32(L0_attn_time_first),
        "s0": srep(L0_ffn_s0), "s1": srep(L0_ffn_s1), "s2": srep(L0_ffn_s2),
        "ones": np.ones((128, 128), dtype=np.float32),
    }
    in_maps = []
    for core in range(NCORES):
        m = dict(shared)
        m["xt"] = np.ascontiguousarray(xT[core * BPC:(core + 1) * BPC])
        in_maps.append(m)

    nc = build(debug_upto=_debug_upto)
    res = run_bass_kernel_spmd(nc, in_maps, core_ids=list(range(NCORES)))
    parts = [np.transpose(r["out"], (0, 2, 1)) for r in res.results]
    return np.ascontiguousarray(np.concatenate(parts, axis=0)).astype(np.float32)
